# version 02
# baseline (speedup 1.0000x reference)
import math
from contextlib import ExitStack

import numpy as np
import ml_dtypes

import concourse.bass as bass
import concourse.mybir as mybir
from concourse.alu_op_type import AluOpType as AL
from concourse.bass_utils import run_bass_kernel_spmd

F32 = mybir.dt.float32
BF16 = mybir.dt.bfloat16
AF = mybir.ActivationFunctionType
BF = ml_dtypes.bfloat16

D = 2048
NCH = D // 128
B = 2
S = 4096
NCORES = 8
TLOC = 1024
NTILE = 8
NEG = -1.0e30
EPS = 1e-5


def core_cols(core):
    cc = core % 4
    return np.concatenate([128 * (4 * g + cc) + np.arange(128) for g in range(NTILE)])

def core_batch(core):
    return core // 4

def fm(x):
    return [np.ascontiguousarray(np.asarray(x)[b].T) for b in range(B)]

def pmaj(a):
    d, t = a.shape
    nch = d // 128
    return np.ascontiguousarray(a.reshape(nch, 128, t).transpose(1, 0, 2))

def un_pmaj(a):
    p, nch, t = a.shape
    return np.ascontiguousarray(a.transpose(1, 0, 2).reshape(nch * 128, t))

def pack_lhsT(w):
    w = np.ascontiguousarray(np.asarray(w))
    O, I = w.shape
    return np.ascontiguousarray(
        w.reshape(O // 128, 128, I // 128, 128).transpose(0, 3, 2, 1).reshape(O // 128, 128, I))

def rope0_tables():
    d = 128
    theta = np.power(10000.0, np.repeat(-np.arange(0, d, 2, dtype=np.float64) / d, 2))
    ang = np.arange(S, dtype=np.float64)[:, None] * theta[None, :]
    cos = np.ascontiguousarray(np.cos(ang).T.astype(np.float32))
    sin = np.ascontiguousarray(np.sin(ang).T.astype(np.float32))
    P = np.zeros((d, d), dtype=np.float32)
    for i in range(0, d, 2):
        P[i, i + 1] = -1.0
        P[i + 1, i] = 1.0
    return cos, sin, np.ascontiguousarray(P.T)

def rope1_tables():
    d, half = 128, 64
    freqs = np.power(10000.0, -np.arange(half, dtype=np.float64) / half)
    fr = np.concatenate([freqs, freqs])
    ang = np.arange(S, dtype=np.float64)[:, None] * fr[None, :]
    cos = np.ascontiguousarray(np.cos(ang).T.astype(np.float32))
    sin = np.ascontiguousarray(np.sin(ang).T.astype(np.float32))
    P = np.zeros((d, d), dtype=np.float32)
    for j in range(half):
        P[j, j + half] = -1.0
        P[j + half, j] = 1.0
    return cos, sin, np.ascontiguousarray(P.T)

def causal_diag_mask(core):
    cc = core % 4
    m = np.zeros((NTILE, 128, 512), dtype=np.float32)
    for g in range(NTILE):
        q = 512 * g + 128 * cc + np.arange(128)[:, None]
        kt = 512 * g + np.arange(512)[None, :]
        m[g] = np.where(q < kt, NEG, 0.0).astype(np.float32)
    return m

IDENT = np.eye(128, dtype=np.float32)
ONES_COL = np.ones((128, 1), dtype=np.float32)
ONES_ROW = np.ones((1, 128), dtype=np.float32)


ENGINES = ("tensor", "vector", "scalar", "gpsimd", "sync")

class Prog:
    def __init__(self, name):
        self.name = name
        self.nc = bass.Bass("TRN2", target_bir_lowering=False)
        self.stack = ExitStack()
        self.q = {e: [] for e in ENGINES}
        self.barn = 0
        self.wait_at = {e: 0 for e in ENGINES}
        self.bar = self.stack.enter_context(self.nc.semaphore("bar"))
        self.stm = self.stack.enter_context(self.nc.semaphore("stm"))
        self.ldm = self.stack.enter_context(self.nc.semaphore("ldm"))
        self.st_cnt = 0
        self.ld_cnt = 0

    def dram(self, name, shape, dt, kind):
        return self.nc.dram_tensor(name, shape, dt, kind=kind)

    def _uniq(self, name):
        n = getattr(self, "_names", None)
        if n is None:
            n = self._names = {}
        k = n.get(name, 0)
        n[name] = k + 1
        return name if k == 0 else f"{name}_{k}"

    def sb(self, name, shape, dt, stack=None):
        return (stack or self.stack).enter_context(
            self.nc.sbuf_tensor(self._uniq(name), shape, dt))

    def ps(self, name, shape, dt=F32, stack=None):
        return (stack or self.stack).enter_context(
            self.nc.psum_tensor(self._uniq(name), shape, dt))

    def on(self, eng, f):
        if self.wait_at[eng] < self.barn:
            tgt, bar = self.barn, self.bar
            self.q[eng].append(lambda e, t=tgt, b=bar: e.wait_ge(b, t))
            self.wait_at[eng] = self.barn
        self.q[eng].append(f)

    def barrier(self, *producers):
        bar = self.bar
        for eng in producers:
            if self.wait_at[eng] < self.barn:
                tgt = self.barn
                self.q[eng].append(lambda e, t=tgt, b=bar: e.wait_ge(b, t))
                self.wait_at[eng] = self.barn
            def f(e, b=bar):
                e.drain()
                e.sem_inc(b, 1)
            self.q[eng].append(f)
        self.barn += len(producers)

    def load(self, sbuf_ap, dram_ap):
        self.ld_cnt += 16
        self.on("gpsimd", lambda e, o=sbuf_ap, i=dram_ap, s=self.ldm:
                e.dma_start(out=o, in_=i, max_dma_last_dim=8192).then_inc(s, 16))

    def store(self, dram_ap, sbuf_ap):
        self.st_cnt += 16
        self.on("gpsimd", lambda e, o=dram_ap, i=sbuf_ap, s=self.stm:
                e.dma_start(out=o, in_=i, max_dma_last_dim=8192).then_inc(s, 16))
        self.barrier("gpsimd")

    def finalize(self):
        n = self.st_cnt
        if n:
            self.on("gpsimd", lambda e, s=self.stm, t=n: e.wait_ge(s, t))
        block = self.stack.enter_context(self.nc.Block())
        for name in ENGINES:
            fs = self.q[name]
            if not fs:
                continue
            getattr(block, name)(lambda e, fs=fs: [f(e) for f in fs])
        self.stack.close()
        return self.nc


def alloc_consts(p):
    c = {}
    c["onesc_bf"] = p.sb("onesc_bf", [128, 1], BF16)
    c["onesc_f"] = p.sb("onesc_f", [128, 1], F32)
    c["onesr_f"] = p.sb("onesr_f", [1, 128], F32)
    c["onesr_bf"] = p.sb("onesr_bf", [1, 128], BF16)
    c["ident_bf"] = p.sb("ident_bf", [128, 128], BF16)
    onesc_d = p.dram("ones_col", [128, 1], F32, "ExternalInput")
    onesr_d = p.dram("ones_row", [1, 128], F32, "ExternalInput")
    ident_d = p.dram("ident", [128, 128], F32, "ExternalInput")
    p.load(c["onesc_bf"][:, :], onesc_d[:, :])
    p.load(c["onesc_f"][:, :], onesc_d[:, :])
    p.load(c["onesr_f"][0:1, :], onesr_d[0:1, :])
    p.load(c["onesr_bf"][0:1, :], onesr_d[0:1, :])
    p.load(c["ident_bf"][:, :], ident_d[:, :])
    return c

def const_inputs():
    return dict(ones_col=ONES_COL, ones_row=ONES_ROW, ident=IDENT)

def emit_ln(p, consts, x_dram, x_sb, xc_bf, r_b, invr, T=TLOC, nch=NCH, load_x=True):
    with ExitStack() as sk:
        sq_f = p.sb("ln_sq", [128, T], F32, sk)
        ps_sum = p.ps("ln_ps1", [1, T], F32, sk)
        ps_sum2 = p.ps("ln_ps2", [1, T], F32, sk)
        ps_bc = p.ps("ln_bc", [128, T], F32, sk)
        rows = p.sb("ln_rows", [1, 2 * T], F32, sk)
        eps_cell = p.sb("eps_cell", [1, 1], F32, sk)
        p.on("vector", lambda e: e.memset(eps_cell[0:1, 0:1], float(EPS)))
        if load_x:
            p.load(x_sb[:, :], x_dram[:, :, :])
            p.barrier("gpsimd")
        onesc = consts["onesc_f"]
        grps = [(s, min(512, T - s)) for s in range(0, T, 512)]
        def sums1(e):
            for s0, n in grps:
                for cch in range(nch):
                    e.matmul(ps_sum[0:1, s0:s0 + n], onesc[:, 0:1],
                             x_sb[:, cch * T + s0: cch * T + s0 + n],
                             start=(cch == 0), stop=(cch == nch - 1))
        p.on("tensor", sums1)
        p.barrier("tensor")
        for cch in range(nch):
            p.on("scalar", lambda e, cch=cch: e.activation(
                sq_f[:, :], x_sb[:, cch * T:(cch + 1) * T], AF.Square))
            p.barrier("scalar")
            def sums2(e, cch=cch):
                for s0, n in grps:
                    e.matmul(ps_sum2[0:1, s0:s0 + n], onesc[:, 0:1], sq_f[:, s0:s0 + n],
                             start=(cch == 0), stop=(cch == nch - 1))
            p.on("tensor", sums2)
            p.barrier("tensor")
        negmu = rows[0:1, 0:T]
        var = rows[0:1, T:2 * T]
        p.on("vector", lambda e: e.tensor_scalar(negmu, ps_sum[0:1, :], -1.0 / D, None, AL.mult))
        p.on("vector", lambda e: e.drain())
        p.on("vector", lambda e: e.tensor_tensor(var, negmu, negmu, AL.mult))
        p.on("vector", lambda e: e.drain())
        p.on("vector", lambda e: e.scalar_tensor_tensor(var, ps_sum2[0:1, :], 1.0 / D, var,
                                                        AL.mult, AL.subtract))
        p.barrier("vector")
        p.on("scalar", lambda e: e.activation(invr[0:1, :], var, AF.Sqrt, bias=eps_cell[0:1, 0:1]))
        p.barrier("scalar")
        rstd = rows[0:1, T:2 * T]
        p.on("vector", lambda e: e.reciprocal(rstd, invr[0:1, :]))
        p.barrier("vector")
        onesr = consts["onesr_f"]
        def bc1(e):
            for s0, n in grps:
                e.matmul(ps_bc[:, s0:s0 + n], onesr[0:1, :], rstd[0:1, s0:s0 + n],
                         start=True, stop=True)
        p.on("tensor", bc1)
        p.barrier("tensor")
        p.on("vector", lambda e: e.tensor_copy(r_b[:, :], ps_bc[:, :]))
        p.barrier("vector")
        def bc2(e):
            for s0, n in grps:
                e.matmul(ps_bc[:, s0:s0 + n], onesr[0:1, :], negmu[0:1, s0:s0 + n],
                         start=True, stop=True)
        p.on("tensor", bc2)
        p.barrier("tensor")
        def center(e):
            for cch in range(nch):
                sl = slice(cch * T, (cch + 1) * T)
                e.tensor_tensor(xc_bf[:, sl], x_sb[:, sl], ps_bc[:, :], AL.add)
        p.on("vector", center)
        p.barrier("vector")

def emit_proj_stream(p, w_d, n_oc, xc_bf, wbuf, psums, out_cb, rank1=None, T=TLOC,
                     nch=NCH, xc_T=None):
    xT = xc_T or T
    ngr = T // 512
    for oc in range(n_oc):
        p.load(wbuf[:, :], w_d[oc, :, :])
        p.barrier("gpsimd")
        def mms(e, oc=oc):
            for t in range(ngr):
                for cch in range(nch):
                    e.matmul(psums[t][:, :], wbuf[:, cch * 128:(cch + 1) * 128],
                             xc_bf[:, cch * xT + t * 512: cch * xT + t * 512 + 512],
                             start=(cch == 0), stop=(rank1 is None and cch == nch - 1))
                if rank1 is not None:
                    lhs_row, rhs_row = rank1
                    e.matmul(psums[t][:, :], lhs_row(oc), rhs_row(t), start=False, stop=True)
        p.on("tensor", mms)
        p.barrier("tensor")
        out_cb(oc)


def build_l0a_qkv(rope_first=True, has_bias=True, tag="l0a"):
    p = Prog(f"{tag}_qkv")
    x_d = p.dram("x", [128, NCH, TLOC], F32, "ExternalInput")
    wq_d = p.dram("wq", [NCH, 128, D], F32, "ExternalInput")
    wk_d = p.dram("wk", [NCH, 128, D], F32, "ExternalInput")
    wv_d = p.dram("wv", [NCH, 128, D], F32, "ExternalInput")
    bias_d = p.dram("biases", [1, 3 * D], F32, "ExternalInput")
    cos_d = p.dram("cosT", [128, TLOC], F32, "ExternalInput")
    sin_d = p.dram("sinT", [128, TLOC], F32, "ExternalInput")
    p0t_d = p.dram("p0t", [128, 128], F32, "ExternalInput")
    q_o = p.dram("q_out", [128, NCH, TLOC], BF16, "ExternalOutput")
    k_o = p.dram("k_out", [128, NCH, TLOC], BF16, "ExternalOutput")
    vt_o = p.dram("vt_out", [NCH, TLOC, 128], BF16, "ExternalOutput")

    consts = alloc_consts(p)
    x_sb = p.sb("x_sb", [128, NCH * TLOC], F32)
    xc_bf = p.sb("xc", [128, NCH * TLOC], BF16)
    r_b = p.sb("r_b", [128, TLOC], F32)
    invr = p.sb("invr", [1, TLOC], F32)
    emit_ln(p, consts, x_d, x_sb, xc_bf, r_b, invr)

    cos_bf = p.sb("cosb", [128, TLOC], BF16)
    sin_bf = p.sb("sinb", [128, TLOC], BF16)
    p0t_bf = p.sb("p0tb", [128, 128], BF16)
    b_sb = p.sb("b_sb", [1, 3 * D], BF16)
    invr_bf = p.sb("invr_bf", [1, TLOC], BF16)
    p.load(cos_bf[:, :], cos_d[:, :])
    p.load(sin_bf[:, :], sin_d[:, :])
    p.load(p0t_bf[:, :], p0t_d[:, :])
    p.load(b_sb[0:1, :], bias_d[0:1, :])
    p.barrier("gpsimd")
    p.on("vector", lambda e: e.tensor_copy(invr_bf[0:1, :], invr[0:1, :]))
    p.barrier("vector")

    wbuf = p.sb("wbuf", [128, D], BF16)
    ps_p = [p.ps(f"ps_p{i}", [128, 512]) for i in range(2)]
    ps_r = [p.ps(f"ps_r{i}", [128, 512]) for i in range(2)]
    vt_ps = p.ps("vt_ps", [128, 1024], BF16)
    qn_bf = p.sb("qn", [128, TLOC], BF16)
    t1_bf = p.sb("t1", [128, TLOC], BF16)
    t2_bf = p.sb("t2", [128, TLOC], BF16)
    ro_bf = p.sb("ro", [128, TLOC], BF16)

    def rank1(boff):
        if not has_bias:
            return None
        return (lambda oc, bo=boff: b_sb[0:1, bo + oc * 128: bo + (oc + 1) * 128],
                lambda t: invr_bf[0:1, t * 512:(t + 1) * 512])

    def evac(oc):
        def f(e):
            for t in range(2):
                sl = slice(t * 512, (t + 1) * 512)
                e.tensor_tensor(qn_bf[:, sl], ps_p[t][:, :], r_b[:, sl], AL.mult)
        p.on("vector", f)
        p.barrier("vector")

    def rope_store(oc, out_dram):
        def shift(e):
            for t in range(2):
                e.matmul(ps_r[t][:, :], p0t_bf[:, :], qn_bf[:, t * 512:(t + 1) * 512],
                         start=True, stop=True)
        p.on("tensor", shift)
        p.barrier("tensor")
        def rope(e):
            for t in range(2):
                sl = slice(t * 512, (t + 1) * 512)
                e.tensor_tensor(t1_bf[:, sl], qn_bf[:, sl], cos_bf[:, sl], AL.mult)
                e.tensor_tensor(t2_bf[:, sl], ps_r[t][:, :], sin_bf[:, sl], AL.mult)
                e.drain()
                e.tensor_tensor(ro_bf[:, sl], t1_bf[:, sl], t2_bf[:, sl], AL.add)
        p.on("vector", rope)
        p.barrier("vector")
        p.store(out_dram[:, oc, :], ro_bf[:, :])

    def q_cb(oc):
        evac(oc)
        rope_store(oc, q_o)

    def k_cb(oc):
        evac(oc)
        rope_store(oc, k_o)

    def v_cb(oc):
        evac(oc)
        def vtr(e):
            for tc in range(8):
                e.transpose(vt_ps[:, tc * 128:(tc + 1) * 128],
                            qn_bf[:, tc * 128:(tc + 1) * 128], consts["ident_bf"][:, :])
        p.on("tensor", vtr)
        p.barrier("tensor")
        p.on("vector", lambda e: e.tensor_copy(ro_bf[:, :], vt_ps[:, :]))
        p.barrier("vector")
        p.store(vt_o[oc, :, :].rearrange("(tc pt) d -> pt tc d", pt=128), ro_bf[:, :])

    emit_proj_stream(p, wq_d, NCH, xc_bf, wbuf, ps_p, q_cb, rank1=rank1(0))
    emit_proj_stream(p, wk_d, NCH, xc_bf, wbuf, ps_p, k_cb, rank1=rank1(D))
    emit_proj_stream(p, wv_d, NCH, xc_bf, wbuf, ps_p, v_cb, rank1=rank1(2 * D))
    return p.finalize()

def build_causal_attn(tag="causal"):
    p = Prog(f"{tag}_attn")
    q_d = p.dram("q", [128, NCH, TLOC], BF16, "ExternalInput")
    k_d = p.dram("kfull", [128, NCH, S], BF16, "ExternalInput")
    vt_d = p.dram("vtfull", [NCH, S, 128], BF16, "ExternalInput")
    x_d = p.dram("x", [128, NCH, TLOC], F32, "ExternalInput")
    wo_d = p.dram("wo", [NCH, 128, D], F32, "ExternalInput")
    bo_d = p.dram("bo", [1, D], F32, "ExternalInput")
    mask_d = p.dram("masks", [NTILE, 128, 512], F32, "ExternalInput")
    scales_d = p.dram("scales", [1, 2], F32, "ExternalInput")
    scalecol_d = p.dram("scale_col", [128, 1], F32, "ExternalInput")
    xo_d = p.dram("xo", [128, NCH, TLOC], F32, "ExternalOutput")

    consts = alloc_consts(p)
    ident_f = p.sb("ident_f", [128, 128], F32)
    ident_d = p.dram("ident_f32", [128, 128], F32, "ExternalInput")
    p.load(ident_f[:, :], ident_d[:, :])

    q_sb = p.sb("q_sb", [128, NCH * TLOC], BF16)
    mask_sb = p.sb("mask_sb", [128, NTILE * 512], F32)
    kh = p.sb("kh", [128, S], BF16)
    vth = p.sb("vth", [128, S], BF16)
    attn_bf = p.sb("attn_bf", [128, NCH * TLOC], BF16)
    p_bf = p.sb("p_bf", [128, 512], BF16)
    pt_bf = p.sb("pt_bf", [128, 512], BF16)
    sq_bf = p.sb("sq_bf", [128, S], BF16)
    rows = p.sb("a_rows", [1, S], F32)
    cells = p.sb("cells", [1, 8], F32)
    negB = p.sb("negB", [128, 8], F32)
    ls = p.sb("ls", [128, 64], F32)
    lred = p.sb("lred", [128, 16], F32)
    invl_row = p.sb("invl_row", [1, TLOC], F32)
    bcast_sb = p.sb("bcast_sb", [128, 128], F32)

    attn_psum_stack = ExitStack()
    ps_s = p.ps("ps_s", [128, 512], F32, attn_psum_stack)
    pt_ps = p.ps("pt_ps", [128, 512], BF16, attn_psum_stack)
    out_ps = p.ps("out_ps", [128, 1024], F32, attn_psum_stack)
    ps_row = p.ps("ps_row", [1, 512], F32, attn_psum_stack)
    ps_col = p.ps("ps_col", [128, 512], F32, attn_psum_stack)

    sc_cells = p.sb("sc_cells", [1, 2], F32)
    sc_col = p.sb("sc_col", [128, 1], F32)
    p.load(sc_cells[0:1, :], scales_d[0:1, :])
    p.load(sc_col[:, :], scalecol_d[:, :])
    p.load(q_sb[:, :], q_d[:, :, :])
    p.load(mask_sb[:, :], mask_d.rearrange("g p t -> p g t"))
    p.barrier("gpsimd")

    for h in range(NCH):
        p.load(kh[:, :], k_d[:, h, :])
        p.load(vth[:, :], vt_d[h, :, :].rearrange("(tc pt) d -> pt tc d", pt=128))
        p.barrier("gpsimd")
        p.on("scalar", lambda e: e.activation(sq_bf[:, :], kh[:, :], AF.Square))
        p.barrier("scalar")
        for blk in range(8):
            def knm(e, blk=blk):
                e.matmul(ps_row[0:1, :], consts["onesc_bf"][:, 0:1],
                         sq_bf[:, blk * 512:(blk + 1) * 512], start=True, stop=True)
            p.on("tensor", knm)
            p.barrier("tensor")
            p.on("vector", lambda e, blk=blk: e.tensor_copy(
                rows[0:1, blk * 512:(blk + 1) * 512], ps_row[0:1, :]))
            p.barrier("vector")
        p.on("vector", lambda e: e.tensor_reduce(cells[0:1, 0:1], rows[0:1, :],
                                                 mybir.AxisListType.X, AL.max))
        p.barrier("vector")
        p.on("scalar", lambda e: e.activation(cells[0:1, 1:2], cells[0:1, 0:1], AF.Sqrt))
        p.barrier("scalar")
        p.on("scalar", lambda e, h=h: e.activation(
            sq_bf[:, 0:TLOC], q_sb[:, h * TLOC:(h + 1) * TLOC], AF.Square))
        p.barrier("scalar")
        for t in range(2):
            def qnm(e, t=t):
                e.matmul(ps_row[0:1, :], consts["onesc_bf"][:, 0:1],
                         sq_bf[:, t * 512:(t + 1) * 512], start=True, stop=True)
            p.on("tensor", qnm)
            p.barrier("tensor")
            p.on("scalar", lambda e, t=t: e.activation(
                rows[0:1, t * 512:(t + 1) * 512], ps_row[0:1, :], AF.Sqrt))
            p.barrier("scalar")
        p.on("vector", lambda e: e.tensor_scalar(
            rows[0:1, 0:TLOC], rows[0:1, 0:TLOC], cells[0:1, 1:2], sc_cells[0:1, 1:2],
            AL.mult, AL.mult))
        p.barrier("vector")
        for g in range(NTILE):
            def trb(e, g=g):
                e.transpose(ps_col[:, g:g + 1], rows[0:1, g * 128:(g + 1) * 128],
                            ident_f[0:1, 0:1])
            p.on("tensor", trb)
        p.barrier("tensor")
        p.on("vector", lambda e: e.tensor_copy(negB[:, :], ps_col[:, 0:8]))
        p.on("vector", lambda e: e.memset(ls[:, :], 0.0))
        p.barrier("vector")
        for g in range(NTILE):
            for blk in range(g + 1):
                def sc_mm(e, h=h, g=g, blk=blk):
                    e.matmul(ps_s[:, :], q_sb[:, h * TLOC + g * 128: h * TLOC + g * 128 + 128],
                             kh[:, blk * 512:(blk + 1) * 512], start=True, stop=True)
                p.on("tensor", sc_mm)
                p.barrier("tensor")
                if blk == g:
                    p.on("vector", lambda e, g=g: e.tensor_tensor(
                        ps_s[:, :], ps_s[:, :], mask_sb[:, g * 512:(g + 1) * 512], AL.add))
                    p.barrier("vector")
                def expp(e, g=g, blk=blk):
                    e.activation(p_bf[:, :], ps_s[:, :], AF.Exp, scale=sc_col[:, 0:1],
                                 bias=negB[:, g:g + 1], accum_out=ls[:, g * 8 + blk: g * 8 + blk + 1])
                p.on("scalar", expp)
                p.barrier("scalar")
                def trp(e):
                    for j in range(4):
                        e.transpose(pt_ps[:, j * 128:(j + 1) * 128],
                                    p_bf[:, j * 128:(j + 1) * 128], consts["ident_bf"][:, :])
                p.on("tensor", trp)
                p.barrier("tensor")
                p.on("vector", lambda e: e.tensor_copy(pt_bf[:, :], pt_ps[:, :]))
                p.barrier("vector")
                def pv(e, g=g, blk=blk):
                    for j in range(4):
                        kt = blk * 4 + j
                        e.matmul(out_ps[:, g * 128:(g + 1) * 128],
                                 vth[:, kt * 128:(kt + 1) * 128],
                                 pt_bf[:, j * 128:(j + 1) * 128],
                                 start=(blk == 0 and j == 0), stop=(blk == g and j == 3))
                p.on("tensor", pv)
                p.barrier("tensor")
        p.on("vector", lambda e: e.tensor_reduce(
            lred[:, 0:8], ls[:, :].rearrange("p (g b) -> p g b", g=8),
            mybir.AxisListType.X, AL.add))
        p.barrier("vector")
        p.on("vector", lambda e: e.reciprocal(lred[:, 8:16], lred[:, 0:8]))
        p.barrier("vector")
        for g in range(NTILE):
            def tri(e, g=g):
                e.transpose(ps_row[0:1, 0:128], lred[:, 8 + g:9 + g], ident_f[:, :])
            p.on("tensor", tri)
            p.barrier("tensor")
            p.on("vector", lambda e, g=g: e.tensor_copy(invl_row[0:1, g * 128:(g + 1) * 128],
                                                        ps_row[0:1, 0:128]))
            p.barrier("vector")
        for g in range(NTILE):
            def bcm(e, g=g):
                e.matmul(ps_col[:, 0:128], consts["onesr_f"][0:1, :],
                         invl_row[0:1, g * 128:(g + 1) * 128], start=True, stop=True)
            p.on("tensor", bcm)
            p.barrier("tensor")
            p.on("vector", lambda e: e.tensor_copy(bcast_sb[:, :], ps_col[:, 0:128]))
            p.barrier("vector")
            p.on("vector", lambda e, h=h, g=g: e.tensor_tensor(
                attn_bf[:, h * TLOC + g * 128: h * TLOC + (g + 1) * 128],
                out_ps[:, g * 128:(g + 1) * 128], bcast_sb[:, :], AL.mult))
            p.barrier("vector")

    attn_psum_stack.close()
    x_sb = p.sb("x_sb", [128, NCH * TLOC], F32)
    bo_sb = p.sb("bo_sb", [1, D], BF16)
    ones512 = p.sb("ones512", [1, 512], BF16)
    p.load(x_sb[:, :], x_d[:, :, :])
    p.load(bo_sb[0:1, :], bo_d[0:1, :])
    p.on("vector", lambda e: e.memset(ones512[0:1, :], 1.0))
    p.barrier("gpsimd", "vector")
    wbuf = p.sb("wbuf", [128, D], BF16)
    ps_p = [p.ps(f"ps_p{i}", [128, 512]) for i in range(2)]
    rank1 = (lambda oc: bo_sb[0:1, oc * 128:(oc + 1) * 128],
             lambda t: ones512[0:1, :])
    def out_cb(oc):
        def f(e, oc=oc):
            for t in range(2):
                sl = slice(oc * TLOC + t * 512, oc * TLOC + t * 512 + 512)
                e.tensor_tensor(x_sb[:, sl], ps_p[t][:, :], x_sb[:, sl], AL.add)
        p.on("vector", f)
        p.barrier("vector")
        p.store(xo_d[:, oc, :], x_sb[:, oc * TLOC:(oc + 1) * TLOC])
    emit_proj_stream(p, wo_d, NCH, attn_bf, wbuf, ps_p, out_cb, rank1=rank1)
    return p.finalize()

def build_l0_ffn():
    NI = 5632 // 128
    p = Prog("l0_ffn")
    x_d = p.dram("x", [128, NCH, TLOC], F32, "ExternalInput")
    gw_d = p.dram("gate_w", [NI, 128, D], F32, "ExternalInput")
    uw_d = p.dram("up_w", [NI, 128, D], F32, "ExternalInput")
    dw_d = p.dram("down_w", [NCH, 128, 5632], F32, "ExternalInput")
    gs_d = p.dram("gate_s", [128, NI], F32, "ExternalInput")
    us_d = p.dram("up_s", [128, NI], F32, "ExternalInput")
    ds_d = p.dram("down_s", [128, NCH], F32, "ExternalInput")
    xo_d = p.dram("xo", [128, NCH, TLOC], F32, "ExternalOutput")

    consts = alloc_consts(p)
    x_sb = p.sb("x_sb", [128, NCH * TLOC], F32)
    r_b = p.sb("r_b", [128, TLOC], F32)
    invr = p.sb("invr", [1, TLOC], F32)
    gs_sb = p.sb("gs_sb", [128, NI], F32)
    us_sb = p.sb("us_sb", [128, NI], F32)
    ds_sb = p.sb("ds_sb", [128, NCH], F32)
    p.load(gs_sb[:, :], gs_d[:, :])
    p.load(us_sb[:, :], us_d[:, :])
    p.load(ds_sb[:, :], ds_d[:, :])

    xc_bf = p.sb("xc", [128, NCH * TLOC], BF16)
    h_sb = p.sb("h_sb", [128, NI * TLOC], BF16)
    emit_ln(p, consts, x_d, x_sb, xc_bf, r_b, invr)

    sk1 = ExitStack()
    gbuf = p.sb("gbuf", [128, D], BF16, sk1)
    ubuf = p.sb("ubuf", [128, D], BF16, sk1)
    ps_g = [p.ps(f"ps_g{i}", [128, 512], F32, sk1) for i in range(2)]
    ps_u = [p.ps(f"ps_u{i}", [128, 512], F32, sk1) for i in range(2)]
    g_bf = p.sb("g_bf", [128, TLOC], BF16, sk1)
    s_bf = p.sb("s_bf", [128, TLOC], BF16, sk1)
    u_bf = p.sb("u_bf", [128, TLOC], BF16, sk1)

    for oc in range(NI):
        p.load(gbuf[:, :], gw_d[oc, :, :])
        p.load(ubuf[:, :], uw_d[oc, :, :])
        p.barrier("gpsimd")
        def mms(e, oc=oc):
            for t in range(2):
                for cch in range(NCH):
                    rhs = xc_bf[:, cch * TLOC + t * 512: cch * TLOC + t * 512 + 512]
                    e.matmul(ps_g[t][:, :], gbuf[:, cch * 128:(cch + 1) * 128], rhs,
                             start=(cch == 0), stop=(cch == NCH - 1))
                    e.matmul(ps_u[t][:, :], ubuf[:, cch * 128:(cch + 1) * 128], rhs,
                             start=(cch == 0), stop=(cch == NCH - 1))
        p.on("tensor", mms)
        p.barrier("tensor")
        def scg(e, oc=oc):
            for t in range(2):
                sl = slice(t * 512, (t + 1) * 512)
                e.scalar_tensor_tensor(g_bf[:, sl], ps_g[t][:, :], gs_sb[:, oc:oc + 1],
                                       r_b[:, sl], AL.mult, AL.mult)
                e.scalar_tensor_tensor(u_bf[:, sl], ps_u[t][:, :], us_sb[:, oc:oc + 1],
                                       r_b[:, sl], AL.mult, AL.mult)
        p.on("vector", scg)
        p.barrier("vector")
        p.on("scalar", lambda e: e.activation(s_bf[:, :], g_bf[:, :], AF.Silu))
        p.barrier("scalar")
        p.on("vector", lambda e, oc=oc: e.tensor_tensor(
            h_sb[:, oc * TLOC:(oc + 1) * TLOC], s_bf[:, :], u_bf[:, :], AL.mult))
        p.barrier("vector")
    sk1.close()

    dbuf = p.sb("dbuf", [128, 5632], BF16)
    ps_p = [p.ps(f"ps_d{i}", [128, 512]) for i in range(2)]
    def out_cb(oc):
        def f(e, oc=oc):
            for t in range(2):
                sl = slice(oc * TLOC + t * 512, oc * TLOC + t * 512 + 512)
                e.scalar_tensor_tensor(x_sb[:, sl], ps_p[t][:, :], ds_sb[:, oc:oc + 1],
                                       x_sb[:, sl], AL.mult, AL.add)
        p.on("vector", f)
        p.barrier("vector")
        p.store(xo_d[:, oc, :], x_sb[:, oc * TLOC:(oc + 1) * TLOC])
    emit_proj_stream(p, dw_d, NCH, h_sb, dbuf, ps_p, out_cb, nch=44)
    return p.finalize()

def build_rwkv_mix():
    p = Prog("rwkv_mix")
    x_d = p.dram("x", [128, NCH, TLOC], F32, "ExternalInput")
    xh_d = p.dram("xhalo", [128, NCH, NTILE], F32, "ExternalInput")
    tm_d = p.dram("tmix", [128, 3 * NCH], F32, "ExternalInput")
    outs = [p.dram(n, [128, NCH, TLOC], BF16, "ExternalOutput") for n in ("xk", "xv", "xr")]

    consts = alloc_consts(p)
    x_sb = p.sb("x_sb", [128, NCH * TLOC], F32)
    xc_bf = p.sb("xc", [128, NCH * TLOC], BF16)
    r_b = p.sb("r_b", [128, TLOC], F32)
    invr = p.sb("invr", [1, TLOC], F32)
    xln = p.sb("xln", [128, NCH * TLOC], BF16)
    xs = p.sb("xs", [128, NCH * TLOC], BF16)
    tm_sb = p.sb("tm_sb", [128, 3 * NCH], F32)
    omt_sb = p.sb("omt_sb", [128, 3 * NCH], F32)
    scr = p.sb("scr", [128, TLOC], BF16)
    p.load(tm_sb[:, :], tm_d[:, :])
    emit_ln(p, consts, x_d, x_sb, xc_bf, r_b, invr)
    def mkxln(e):
        for cch in range(NCH):
            sl = slice(cch * TLOC, (cch + 1) * TLOC)
            e.tensor_tensor(xln[:, sl], xc_bf[:, sl], r_b[:, :], AL.mult)
    p.on("vector", mkxln)
    p.on("vector", lambda e: e.tensor_scalar(omt_sb[:, :], tm_sb[:, :], -1.0, 1.0,
                                             AL.mult, AL.add))
    p.barrier("vector")
    with ExitStack() as sk:
        xh_sb = p.sb("xh_sb", [128, NCH * NTILE], F32, sk)
        xch_bf = p.sb("xch", [128, NCH * NTILE], BF16, sk)
        rh_b = p.sb("rh_b", [128, NTILE], F32, sk)
        invrh = p.sb("invrh", [1, NTILE], F32, sk)
        emit_ln(p, consts, xh_d, xh_sb, xch_bf, rh_b, invrh, T=NTILE)
        def mkxlnh(e):
            for cch in range(NCH):
                sl = slice(cch * NTILE, (cch + 1) * NTILE)
                e.tensor_tensor(xch_bf[:, sl], xch_bf[:, sl], rh_b[:, :], AL.mult)
        p.on("vector", mkxlnh)
        p.barrier("vector")
        def shift(e):
            for cch in range(NCH):
                for g in range(NTILE):
                    o = cch * TLOC + g * 128
                    e.tensor_copy(xs[:, o + 1:o + 128], xln[:, o:o + 127])
                    e.tensor_copy(xs[:, o:o + 1], xch_bf[:, cch * NTILE + g:cch * NTILE + g + 1])
        p.on("vector", shift)
        p.barrier("vector")
    for w in range(3):
        for cch in range(NCH):
            sl = slice(cch * TLOC, (cch + 1) * TLOC)
            def mix(e, w=w, cch=cch, sl=sl):
                e.tensor_scalar(scr[:, :], xs[:, sl], omt_sb[:, w * NCH + cch: w * NCH + cch + 1],
                                None, AL.mult)
                e.drain()
                e.scalar_tensor_tensor(scr[:, :], xln[:, sl],
                                       tm_sb[:, w * NCH + cch: w * NCH + cch + 1],
                                       scr[:, :], AL.mult, AL.add)
            p.on("vector", mix)
            p.barrier("vector")
            p.store(outs[w][:, cch, :], scr[:, :])
    return p.finalize()

def build_rwkv_core():
    BT = B * S
    p = Prog("rwkv_core")
    xk_d = p.dram("xk", [128, NCH, BT], BF16, "ExternalInput")
    xv_d = p.dram("xv", [128, NCH, BT], BF16, "ExternalInput")
    xr_d = p.dram("xr", [128, NCH, BT], BF16, "ExternalInput")
    wk_d = p.dram("wk", [2, 128, D], F32, "ExternalInput")
    wv_d = p.dram("wv", [2, 128, D], F32, "ExternalInput")
    wr_d = p.dram("wr", [2, 128, D], F32, "ExternalInput")
    td_d = p.dram("td", [128, 2], F32, "ExternalInput")
    tf_d = p.dram("tf", [128, 2], F32, "ExternalInput")
    g_o = p.dram("g_out", [128, 2, BT], BF16, "ExternalOutput")

    wk_sb = p.sb("wk_sb", [128, 2 * D], BF16)
    wv_sb = p.sb("wv_sb", [128, 2 * D], BF16)
    wr_sb = p.sb("wr_sb", [128, 2 * D], BF16)
    td_sb = p.sb("td_sb", [128, 2], F32)
    tf_sb = p.sb("tf_sb", [128, 2], F32)
    cells = p.sb("cells", [128, 8], F32)
    xbuf = p.sb("xbuf", [128, 3 * 512], BF16)
    k_sb = [p.sb(f"k_sb{i}", [128, S], F32) for i in range(2)]
    v_sb = [p.sb(f"v_sb{i}", [128, S], F32) for i in range(2)]
    sr_bf = [p.sb(f"sr_bf{i}", [128, S], BF16) for i in range(2)]
    dwt = p.sb("dwt", [128, S], F32)
    scan = p.sb("scan", [128, S + 4], F32)
    g_bf = p.sb("g_bf", [128, S], BF16)
    ps = [p.ps(f"ps_kvr{i}", [128, 512]) for i in range(6)]

    for i, wd, ws in ((0, wk_d, wk_sb), (1, wv_d, wv_sb), (2, wr_d, wr_sb)):
        for cg in range(2):
            p.load(ws[:, cg * D:(cg + 1) * D], wd[cg, :, :])
    p.load(td_sb[:, :], td_d[:, :])
    p.load(tf_sb[:, :], tf_d[:, :])
    p.barrier("gpsimd")

    for b in range(B):
        for tg in range(8):
            t0 = b * S + tg * 512
            p.load(xbuf[:, 0:512], xk_d[:, 0, t0:t0 + 512])
            for cch in range(NCH):
                if cch > 0:
                    p.load(xbuf[:, 0:512], xk_d[:, cch, t0:t0 + 512])
                p.load(xbuf[:, 512:1024], xv_d[:, cch, t0:t0 + 512])
                p.load(xbuf[:, 1024:1536], xr_d[:, cch, t0:t0 + 512])
                p.barrier("gpsimd")
                def mms(e, cch=cch):
                    for cg in range(2):
                        for w, ws in ((0, wk_sb), (1, wv_sb), (2, wr_sb)):
                            e.matmul(ps[cg * 3 + w][:, :],
                                     ws[:, cg * D + cch * 128: cg * D + (cch + 1) * 128],
                                     xbuf[:, w * 512:(w + 1) * 512],
                                     start=(cch == 0), stop=(cch == NCH - 1))
                p.on("tensor", mms)
                p.barrier("tensor")
            sl = slice(tg * 512, (tg + 1) * 512)
            def evac(e, sl=sl):
                for cg in range(2):
                    e.tensor_copy(k_sb[cg][:, sl], ps[cg * 3 + 0][:, :])
                    e.tensor_copy(v_sb[cg][:, sl], ps[cg * 3 + 1][:, :])
            p.on("vector", evac)
            def sig(e, sl=sl):
                for cg in range(2):
                    e.activation(sr_bf[cg][:, sl], ps[cg * 3 + 2][:, :], AF.Sigmoid)
            p.on("scalar", sig)
            p.barrier("vector", "scalar")
        for cg in range(2):
            p.on("vector", lambda e, cg=cg: e.tensor_reduce(
                cells[:, cg:cg + 1], k_sb[cg][:, :], mybir.AxisListType.X, AL.max,
                negate=True))
            p.barrier("vector")
            p.on("scalar", lambda e, cg=cg: e.activation(cells[:, 2 + cg:3 + cg],
                                                         td_sb[:, cg:cg + 1], AF.Exp))
            p.barrier("scalar")
            p.on("scalar", lambda e, cg=cg: e.activation(cells[:, 4 + cg:5 + cg],
                                                         cells[:, 2 + cg:3 + cg], AF.Exp,
                                                         scale=-1.0))
            p.on("scalar", lambda e, cg=cg: e.activation(cells[:, 6 + cg:7 + cg],
                                                         tf_sb[:, cg:cg + 1], AF.Exp))
            p.barrier("scalar")
            p.on("scalar", lambda e, cg=cg: e.activation(k_sb[cg][:, :], k_sb[cg][:, :],
                                                         AF.Exp, bias=cells[:, cg:cg + 1]))
            p.barrier("scalar")
            p.on("vector", lambda e, cg=cg: e.tensor_tensor(v_sb[cg][:, :], k_sb[cg][:, :],
                                                            v_sb[cg][:, :], AL.mult))
            p.on("vector", lambda e, cg=cg: e.tensor_scalar(dwt[:, :], k_sb[cg][:, :], 0.0,
                                                            cells[:, 4 + cg:5 + cg],
                                                            AL.mult, AL.add))
            p.on("vector", lambda e: e.memset(scan[:, 0:1], 0.0))
            p.barrier("vector")
            p.on("vector", lambda e, cg=cg: e.tensor_tensor_scan(
                scan[:, 1:S + 1], dwt[:, :], v_sb[cg][:, :], 0.0, AL.mult, AL.add))
            p.barrier("vector")
            p.on("vector", lambda e, cg=cg: e.scalar_tensor_tensor(
                v_sb[cg][:, :], v_sb[cg][:, :], cells[:, 6 + cg:7 + cg], scan[:, 0:S],
                AL.mult, AL.add))
            p.barrier("vector")
            p.on("vector", lambda e, cg=cg: e.tensor_tensor_scan(
                scan[:, 1:S + 1], dwt[:, :], k_sb[cg][:, :], 0.0, AL.mult, AL.add))
            p.barrier("vector")
            p.on("vector", lambda e, cg=cg: e.scalar_tensor_tensor(
                k_sb[cg][:, :], k_sb[cg][:, :], cells[:, 6 + cg:7 + cg], scan[:, 0:S],
                AL.mult, AL.add))
            p.barrier("vector")
            p.on("vector", lambda e, cg=cg: e.reciprocal(k_sb[cg][:, :], k_sb[cg][:, :]))
            p.barrier("vector")
            p.on("vector", lambda e, cg=cg: e.tensor_tensor(v_sb[cg][:, :], v_sb[cg][:, :],
                                                            k_sb[cg][:, :], AL.mult))
            p.barrier("vector")
            p.on("vector", lambda e, cg=cg: e.tensor_tensor(g_bf[:, :], v_sb[cg][:, :],
                                                            sr_bf[cg][:, :], AL.mult))
            p.barrier("vector")
            p.store(g_o[:, cg, b * S:(b + 1) * S], g_bf[:, :])
    return p.finalize()

def build_rwkv_out():
    p = Prog("rwkv_out")
    g_d = p.dram("g", [128, NCH, TLOC], BF16, "ExternalInput")
    x_d = p.dram("x", [128, NCH, TLOC], F32, "ExternalInput")
    wo_d = p.dram("wo", [NCH, 128, D], F32, "ExternalInput")
    xo_d = p.dram("xo", [128, NCH, TLOC], F32, "ExternalOutput")
    consts = alloc_consts(p)
    g_sb = p.sb("g_sb", [128, NCH * TLOC], BF16)
    x_sb = p.sb("x_sb", [128, NCH * TLOC], F32)
    p.load(g_sb[:, :], g_d[:, :, :])
    p.load(x_sb[:, :], x_d[:, :, :])
    p.barrier("gpsimd")
    wbuf = p.sb("wbuf", [128, D], BF16)
    ps_p = [p.ps(f"ps_p{i}", [128, 512]) for i in range(2)]
    def out_cb(oc):
        def f(e, oc=oc):
            for t in range(2):
                sl = slice(oc * TLOC + t * 512, oc * TLOC + t * 512 + 512)
                e.tensor_tensor(x_sb[:, sl], ps_p[t][:, :], x_sb[:, sl], AL.add)
        p.on("vector", f)
        p.barrier("vector")
        p.store(xo_d[:, oc, :], x_sb[:, oc * TLOC:(oc + 1) * TLOC])
    emit_proj_stream(p, wo_d, NCH, g_sb, wbuf, ps_p, out_cb)
    return p.finalize()

def build_ternarize():
    NFC, NPJ = 40, 16
    SFC, SPJ = NFC // 8, NPJ // 8
    p = Prog("ternarize")
    wfc_d = p.dram("wfc", [NFC, 128, D], F32, "ExternalInput")
    wpj_d = p.dram("wpj", [NPJ, 128, 5120], F32, "ExternalInput")
    wfc_sh = p.dram("wfc_sh", [SFC, 128, D], F32, "ExternalInput")
    wpj_sh = p.dram("wpj_sh", [SPJ, 128, 5120], F32, "ExternalInput")
    qfc_o = p.dram("qfc", [SFC, 128, D], BF16, "ExternalOutput")
    qpj_o = p.dram("qpj", [SPJ, 128, 5120], BF16, "ExternalOutput")
    am_o = p.dram("am", [1, 2], F32, "ExternalOutput")

    consts = alloc_consts(p)
    wbuf = p.sb("wbuf", [128, 5120], F32)
    scr = p.sb("scr", [128, 5120], BF16)
    t2 = p.sb("t2", [128, 5120], BF16)
    qb = p.sb("qb", [128, 5120], BF16)
    parts = p.sb("parts", [128, NFC + NPJ], F32)
    cells = p.sb("cells", [1, 16], F32)
    cols = p.sb("cols", [128, 4], F32)
    ps_row = p.ps("ps_row", [1, 512])
    ps_col = p.ps("ps_col", [128, 8])

    for wi, (wd, nb, wid) in enumerate(((wfc_d, NFC, D), (wpj_d, NPJ, 5120))):
        for b in range(nb):
            p.load(wbuf[:, 0:wid], wd[b, :, :])
            p.barrier("gpsimd")
            off = b if wi == 0 else NFC + b
            p.on("scalar", lambda e, off=off, wid=wid: e.activation(
                scr[:, 0:wid], wbuf[:, 0:wid], AF.Abs,
                accum_out=parts[:, off:off + 1]))
            p.barrier("scalar")
    def tot(e):
        e.matmul(ps_row[0:1, 0:NFC + NPJ], consts["onesc_f"][:, 0:1],
                 parts[:, 0:NFC + NPJ], start=True, stop=True)
    p.on("tensor", tot)
    p.barrier("tensor")
    p.on("vector", lambda e: e.tensor_reduce(cells[0:1, 0:1], ps_row[0:1, 0:NFC],
                                             mybir.AxisListType.X, AL.add))
    p.on("vector", lambda e: e.tensor_reduce(cells[0:1, 1:2], ps_row[0:1, NFC:NFC + NPJ],
                                             mybir.AxisListType.X, AL.add))
    p.barrier("vector")
    def amop(e):
        e.tensor_scalar(cells[0:1, 0:1], cells[0:1, 0:1], 1.0 / (5120.0 * D), 1e-5,
                        AL.mult, AL.max)
        e.tensor_scalar(cells[0:1, 1:2], cells[0:1, 1:2], 1.0 / (5120.0 * D), 1e-5,
                        AL.mult, AL.max)
        e.drain()
        e.tensor_scalar(cells[0:1, 2:3], cells[0:1, 0:1], 0.7, None, AL.mult)
        e.tensor_scalar(cells[0:1, 3:4], cells[0:1, 1:2], 0.7, None, AL.mult)
        e.drain()
        e.tensor_scalar(cells[0:1, 4:5], cells[0:1, 2:3], -1.0, None, AL.mult)
        e.tensor_scalar(cells[0:1, 5:6], cells[0:1, 3:4], -1.0, None, AL.mult)
    p.on("vector", amop)
    p.barrier("vector")
    p.store(am_o[0:1, :], cells[0:1, 0:2])
    def bc(e):
        for j in range(4):
            e.matmul(ps_col[:, j:j + 1], consts["onesr_f"][0:1, :], cells[0:1, 2 + j:3 + j],
                     start=True, stop=True)
    p.on("tensor", bc)
    p.barrier("tensor")
    p.on("vector", lambda e: e.tensor_copy(cols[:, :], ps_col[:, 0:4]))
    p.barrier("vector")
    for wi, (wd, qo, nb, wid) in enumerate(((wfc_sh, qfc_o, SFC, D),
                                            (wpj_sh, qpj_o, SPJ, 5120))):
        thr = cols[:, wi:wi + 1]
        nthr = cols[:, 2 + wi:3 + wi]
        for b in range(nb):
            p.load(wbuf[:, 0:wid], wd[b, :, :])
            p.barrier("gpsimd")
            def tern(e, wid=wid, thr=thr, nthr=nthr):
                e.tensor_scalar(t2[:, 0:wid], wbuf[:, 0:wid], nthr, None, AL.is_lt)
                e.drain()
                e.scalar_tensor_tensor(qb[:, 0:wid], wbuf[:, 0:wid], thr, t2[:, 0:wid],
                                       AL.is_gt, AL.subtract)
            p.on("vector", tern)
            p.barrier("vector")
            p.store(qo[b, :, :], qb[:, 0:wid])
    return p.finalize()

def build_l1_ffn():
    NI = 5120 // 128
    p = Prog("l1_ffn")
    x_d = p.dram("x", [128, NCH, TLOC], F32, "ExternalInput")
    qfc_d = p.dram("qfc", [NI, 128, D], BF16, "ExternalInput")
    qpj_d = p.dram("qpj", [NCH, 128, 5120], BF16, "ExternalInput")
    am_d = p.dram("am", [1, 2], F32, "ExternalInput")
    xo_d = p.dram("xo", [128, NCH, TLOC], F32, "ExternalOutput")

    consts = alloc_consts(p)
    x_sb = p.sb("x_sb", [128, NCH * TLOC], F32)
    r_b = p.sb("r_b", [128, TLOC], F32)
    invr = p.sb("invr", [1, TLOC], F32)
    cells = p.sb("cells", [1, 4], F32)
    r2s = p.sb("r2s", [128, TLOC], F32)
    xc_bf = p.sb("xc", [128, NCH * TLOC], BF16)
    h_sb = p.sb("h_sb", [128, NI * TLOC], BF16)
    emit_ln(p, consts, x_d, x_sb, xc_bf, r_b, invr)

    am_sb = cells[0:1, 0:2]
    p.load(am_sb, am_d[0:1, :])
    p.barrier("gpsimd")
    ps_c = p.ps("ps_c", [128, 8])
    def sstar(e):
        e.tensor_tensor(cells[0:1, 2:3], cells[0:1, 0:1], cells[0:1, 0:1], AL.mult)
        e.drain()
        e.tensor_tensor(cells[0:1, 2:3], cells[0:1, 2:3], cells[0:1, 1:2], AL.mult)
    p.on("vector", sstar)
    p.barrier("vector")
    p.on("tensor", lambda e: e.matmul(ps_c[:, 0:1], consts["onesr_f"][0:1, :],
                                      cells[0:1, 2:3], start=True, stop=True))
    p.barrier("tensor")
    def r2(e):
        e.tensor_tensor(r2s[:, :], r_b[:, :], r_b[:, :], AL.mult)
        e.drain()
        e.tensor_scalar(r2s[:, :], r2s[:, :], ps_c[:, 0:1], None, AL.mult)
    p.on("vector", r2)
    p.barrier("vector")

    sk1 = ExitStack()
    fbuf = p.sb("fbuf", [128, D], BF16, sk1)
    ps_z = [p.ps(f"ps_z{i}", [128, 512], F32, sk1) for i in range(2)]
    rl_bf = p.sb("rl_bf", [128, TLOC], BF16, sk1)
    for oc in range(NI):
        p.load(fbuf[:, :], qfc_d[oc, :, :])
        p.barrier("gpsimd")
        def mms(e, oc=oc):
            for t in range(2):
                for cch in range(NCH):
                    e.matmul(ps_z[t][:, :], fbuf[:, cch * 128:(cch + 1) * 128],
                             xc_bf[:, cch * TLOC + t * 512: cch * TLOC + t * 512 + 512],
                             start=(cch == 0), stop=(cch == NCH - 1))
        p.on("tensor", mms)
        p.barrier("tensor")
        def relu(e):
            for t in range(2):
                e.activation(rl_bf[:, t * 512:(t + 1) * 512], ps_z[t][:, :], AF.Relu)
        p.on("scalar", relu)
        p.barrier("scalar")
        p.on("vector", lambda e, oc=oc: e.tensor_tensor(
            h_sb[:, oc * TLOC:(oc + 1) * TLOC], rl_bf[:, :], rl_bf[:, :], AL.mult))
        p.barrier("vector")
    sk1.close()

    pbuf = p.sb("pbuf", [128, 5120], BF16)
    scr = p.sb("oscr", [128, TLOC], F32)
    ps_p = [p.ps(f"ps_o{i}", [128, 512]) for i in range(2)]
    def out_cb(oc):
        def f(e, oc=oc):
            for t in range(2):
                sl = slice(oc * TLOC + t * 512, oc * TLOC + t * 512 + 512)
                tsl = slice(t * 512, (t + 1) * 512)
                e.tensor_tensor(scr[:, tsl], ps_p[t][:, :], r2s[:, tsl], AL.mult)
                e.drain()
                e.tensor_tensor(x_sb[:, sl], scr[:, tsl], x_sb[:, sl], AL.add)
        p.on("vector", f)
        p.barrier("vector")
        p.store(xo_d[:, oc, :], x_sb[:, oc * TLOC:(oc + 1) * TLOC])
    emit_proj_stream(p, qpj_d, NCH, h_sb, pbuf, ps_p, out_cb, nch=NI)
    return p.finalize()

def build_l1b_qkv():
    p = Prog("l1b_qkv")
    x_d = p.dram("x", [128, NCH, TLOC], F32, "ExternalInput")
    wq_d = p.dram("wq", [NCH, 128, D], F32, "ExternalInput")
    wk_d = p.dram("wk", [4, 128, D], F32, "ExternalInput")
    wv_d = p.dram("wv", [4, 128, D], F32, "ExternalInput")
    q_o = p.dram("q_out", [128, NCH, TLOC], BF16, "ExternalOutput")
    k_o = p.dram("k_out", [128, 4, TLOC], BF16, "ExternalOutput")
    vt_o = p.dram("vt_out", [4, TLOC, 128], BF16, "ExternalOutput")

    consts = alloc_consts(p)
    x_sb = p.sb("x_sb", [128, NCH * TLOC], F32)
    xc_bf = p.sb("xc", [128, NCH * TLOC], BF16)
    r_b = p.sb("r_b", [128, TLOC], F32)
    invr = p.sb("invr", [1, TLOC], F32)
    emit_ln(p, consts, x_d, x_sb, xc_bf, r_b, invr)

    wbuf = p.sb("wbuf", [128, D], BF16)
    ps_p = [p.ps(f"ps_p{i}", [128, 512]) for i in range(2)]
    vt_ps = p.ps("vt_ps", [128, 1024], BF16)
    qn_bf = p.sb("qn", [128, TLOC], BF16)
    ro_bf = p.sb("ro", [128, TLOC], BF16)

    def evac(oc):
        def f(e):
            for t in range(2):
                sl = slice(t * 512, (t + 1) * 512)
                e.tensor_tensor(qn_bf[:, sl], ps_p[t][:, :], r_b[:, sl], AL.mult)
        p.on("vector", f)
        p.barrier("vector")

    def q_cb(oc):
        evac(oc)
        p.store(q_o[:, oc, :], qn_bf[:, :])

    def k_cb(oc):
        evac(oc)
        p.store(k_o[:, oc, :], qn_bf[:, :])

    def v_cb(oc):
        evac(oc)
        def vtr(e):
            for tc in range(8):
                e.transpose(vt_ps[:, tc * 128:(tc + 1) * 128],
                            qn_bf[:, tc * 128:(tc + 1) * 128], consts["ident_bf"][:, :])
        p.on("tensor", vtr)
        p.barrier("tensor")
        p.on("vector", lambda e: e.tensor_copy(ro_bf[:, :], vt_ps[:, :]))
        p.barrier("vector")
        p.store(vt_o[oc, :, :].rearrange("(tc pt) d -> pt tc d", pt=128), ro_bf[:, :])

    emit_proj_stream(p, wq_d, NCH, xc_bf, wbuf, ps_p, q_cb)
    emit_proj_stream(p, wk_d, 4, xc_bf, wbuf, ps_p, k_cb)
    emit_proj_stream(p, wv_d, 4, xc_bf, wbuf, ps_p, v_cb)
    return p.finalize()

def build_l1b_attn():
    SC = 0.125
    p = Prog("l1b_attn")
    q_d = p.dram("q", [128, NCH, TLOC], BF16, "ExternalInput")
    k_d = p.dram("kfull", [128, 4, S], BF16, "ExternalInput")
    vt_d = p.dram("vtfull", [4, S, 128], BF16, "ExternalInput")
    x_d = p.dram("x", [128, NCH, TLOC], F32, "ExternalInput")
    mem_d = p.dram("mem", [4, 128, 512], F32, "ExternalInput")
    mn_d = p.dram("mnorm", [128, 4], F32, "ExternalInput")
    mw_d = p.dram("mweight", [1, 1], F32, "ExternalInput")
    ident_d = p.dram("ident_f32", [128, 128], F32, "ExternalInput")
    xo_d = p.dram("xo", [128, NCH, TLOC], F32, "ExternalOutput")

    consts = alloc_consts(p)
    ident_f = p.sb("ident_f", [128, 128], F32)
    p.load(ident_f[:, :], ident_d[:, :])
    q_sb = p.sb("q_sb", [128, NCH * TLOC], BF16)
    attn_bf = p.sb("attn_bf", [128, NCH * TLOC], BF16)
    amem_bf = p.sb("amem_bf", [128, NCH * TLOC], BF16)
    cells = p.sb("cells", [1, 16], F32)
    gcol = p.sb("gcol", [128, 2], F32)
    p.load(q_sb[:, :], q_d[:, :, :])
    p.load(cells[0:1, 0:1], mw_d[0:1, :])
    p.barrier("gpsimd")

    with ExitStack() as skg:
        ps_c = p.ps("ps_c", [128, 8], F32, skg)
        p.on("scalar", lambda e: e.activation(cells[0:1, 1:2], cells[0:1, 0:1], AF.Sigmoid))
        p.barrier("scalar")
        p.on("vector", lambda e: e.tensor_scalar(cells[0:1, 2:3], cells[0:1, 1:2], -1.0, 1.0,
                                                 AL.mult, AL.add))
        p.barrier("vector")
        def gbc(e):
            e.matmul(ps_c[:, 0:1], consts["onesr_f"][0:1, :], cells[0:1, 1:2], start=True, stop=True)
            e.matmul(ps_c[:, 1:2], consts["onesr_f"][0:1, :], cells[0:1, 2:3], start=True, stop=True)
        p.on("tensor", gbc)
        p.barrier("tensor")
        p.on("vector", lambda e: e.tensor_copy(gcol[:, :], ps_c[:, 0:2]))
        p.barrier("vector")

    with ExitStack() as sk:
        sig_bf = p.sb("sig_bf", [128, NCH * TLOC], BF16, sk)
        scr1 = p.sb("m_scr1", [128, TLOC], BF16, sk)
        scr2 = p.sb("m_scr2", [128, TLOC], BF16, sk)
        membuf = p.sb("membuf", [128, 4 * 512], BF16, sk)
        mn_bf = p.sb("mn_bf", [128, 4], BF16, sk)
        den_row = p.sb("den_row", [1, TLOC], F32, sk)
        bc_sb = p.sb("m_bc", [128, TLOC], F32, sk)
        ps_m = [p.ps(f"ps_m{i}", [128, 512], F32, sk) for i in range(2)]
        ps_d = p.ps("ps_d", [1, TLOC], F32, sk)
        ps_bc = p.ps("ps_bc", [128, TLOC], F32, sk)
        for ob in range(4):
            p.load(membuf[:, ob * 512:(ob + 1) * 512], mem_d[ob, :, :])
        p.load(mn_bf[:, :], mn_d[:, :])
        p.barrier("gpsimd")
        for oc in range(NCH):
            qsl = q_sb[:, oc * TLOC:(oc + 1) * TLOC]
            def elu(e, qsl=qsl):
                e.activation(scr1[:, :], qsl, AF.Relu)
                e.drain()
                e.activation(scr2[:, :], qsl, AF.Relu, scale=-1.0)
                e.drain()
                e.activation(scr2[:, :], scr2[:, :], AF.Exp, scale=-1.0)
            p.on("scalar", elu)
            p.barrier("scalar")
            p.on("vector", lambda e, oc=oc: e.scalar_tensor_tensor(
                sig_bf[:, oc * TLOC:(oc + 1) * TLOC], scr1[:, :], -1.0, scr2[:, :],
                AL.add, AL.add))
            p.barrier("vector")
        for qg in range(4):
            def denm(e, qg=qg):
                for t in range(2):
                    for ic in range(4):
                        e.matmul(ps_d[0:1, t * 512:(t + 1) * 512], mn_bf[:, ic:ic + 1],
                                 sig_bf[:, (qg * 4 + ic) * TLOC + t * 512:
                                        (qg * 4 + ic) * TLOC + t * 512 + 512],
                                 start=(ic == 0), stop=(ic == 3))
            p.on("tensor", denm)
            p.barrier("tensor")
            p.on("vector", lambda e: e.reciprocal(den_row[0:1, :], ps_d[0:1, :]))
            p.barrier("vector")
            def dbc(e):
                for t in range(2):
                    e.matmul(ps_bc[:, t * 512:(t + 1) * 512], consts["onesr_f"][0:1, :],
                             den_row[0:1, t * 512:(t + 1) * 512], start=True, stop=True)
            p.on("tensor", dbc)
            p.barrier("tensor")
            p.on("vector", lambda e: e.tensor_copy(bc_sb[:, :], ps_bc[:, :]))
            p.barrier("vector")
            for ob in range(4):
                def amm(e, qg=qg, ob=ob):
                    for t in range(2):
                        for ic in range(4):
                            e.matmul(ps_m[t][:, :], membuf[:, ob * 512 + ic * 128:
                                                           ob * 512 + (ic + 1) * 128],
                                     sig_bf[:, (qg * 4 + ic) * TLOC + t * 512:
                                            (qg * 4 + ic) * TLOC + t * 512 + 512],
                                     start=(ic == 0), stop=(ic == 3))
                p.on("tensor", amm)
                p.barrier("tensor")
                def amn(e, qg=qg, ob=ob):
                    for t in range(2):
                        sl = slice((qg * 4 + ob) * TLOC + t * 512,
                                   (qg * 4 + ob) * TLOC + t * 512 + 512)
                        e.tensor_tensor(amem_bf[:, sl], ps_m[t][:, :],
                                        bc_sb[:, t * 512:(t + 1) * 512], AL.mult)
                p.on("vector", amn)
                p.barrier("vector")

    with ExitStack() as sk:
        k_sb = p.sb("k_sb", [128, 4 * S], BF16, sk)
        vt_sb = p.sb("vt_sb", [128, 4 * S], BF16, sk)
        sq_bf = p.sb("a_sq", [128, S], BF16, sk)
        rows = p.sb("a_rows", [1, S], F32, sk)
        negB = p.sb("negB", [128, 256], F32, sk)
        ls = p.sb("ls", [128, 128], F32, sk)
        invl = p.sb("invl", [128, 16], F32, sk)
        invl_row = p.sb("invl_row", [1, 256], F32, sk)
        bc2_sb = p.sb("bc2_sb", [128, 128], F32, sk)
        p_bf = [p.sb(f"p_bf{i}", [128, 512], BF16, sk) for i in range(2)]
        pt_bf = [p.sb(f"pt_bf{i}", [128, 512], BF16, sk) for i in range(2)]
        sc_col = p.sb("sc_col", [128, 1], F32, sk)
        ps_s = [p.ps(f"ps_s{i}", [128, 512], F32, sk) for i in range(2)]
        pt_ps = [p.ps(f"pt_ps{i}", [128, 512], BF16, sk) for i in range(2)]
        out_ps = p.ps("out_ps", [128, 1024], F32, sk)
        ps_row = p.ps("ps_row", [1, 512], F32, sk)
        ps_col = p.ps("ps_col", [128, 512], F32, sk)
        p.load(k_sb[:, :], k_d[:, :, :])
        for kc in range(4):
            p.load(vt_sb[:, kc * S:(kc + 1) * S],
                   vt_d[kc, :, :].rearrange("(tc pt) d -> pt tc d", pt=128))
        p.barrier("gpsimd")
        p.on("vector", lambda e: e.memset(sc_col[:, :], SC))
        p.barrier("vector")

        for kc in range(4):
            p.on("scalar", lambda e, kc=kc: e.activation(
                sq_bf[:, :], k_sb[:, kc * S:(kc + 1) * S], AF.Square))
            p.barrier("scalar")
            for half in range(2):
                base = half * 64
                for blk in range(8):
                    def knm(e, kc=kc, base=base, blk=blk):
                        e.matmul(ps_row[0:1, :], consts["onesc_bf"][base:base + 64, 0:1],
                                 sq_bf[base:base + 64, blk * 512:(blk + 1) * 512],
                                 start=True, stop=True)
                    p.on("tensor", knm)
                    p.barrier("tensor")
                    p.on("vector", lambda e, blk=blk: e.tensor_copy(
                        rows[0:1, blk * 512:(blk + 1) * 512], ps_row[0:1, :]))
                    p.barrier("vector")
                h = kc * 2 + half
                p.on("vector", lambda e, h=h: e.tensor_reduce(
                    cells[0:1, 8 + h:9 + h], rows[0:1, :], mybir.AxisListType.X, AL.max))
                p.barrier("vector")
        p.on("scalar", lambda e: e.activation(cells[0:1, 8:16], cells[0:1, 8:16], AF.Sqrt))
        p.barrier("scalar")
        for oc in range(NCH):
            qg, hp = oc // 4, oc % 4
            p.on("scalar", lambda e, oc=oc: e.activation(
                sq_bf[:, 0:TLOC], q_sb[:, oc * TLOC:(oc + 1) * TLOC], AF.Square))
            p.barrier("scalar")
            for half in range(2):
                base = half * 64
                h = hp * 2 + half
                vh = qg * 8 + h
                for t in range(2):
                    def qnm(e, base=base, t=t):
                        e.matmul(ps_row[0:1, :], consts["onesc_bf"][base:base + 64, 0:1],
                                 sq_bf[base:base + 64, t * 512:(t + 1) * 512],
                                 start=True, stop=True)
                    p.on("tensor", qnm)
                    p.barrier("tensor")
                    p.on("scalar", lambda e, t=t: e.activation(
                        rows[0:1, t * 512:(t + 1) * 512], ps_row[0:1, :], AF.Sqrt))
                    p.barrier("scalar")
                p.on("vector", lambda e, h=h: e.tensor_scalar(
                    rows[0:1, 0:TLOC], rows[0:1, 0:TLOC], cells[0:1, 8 + h:9 + h], -SC,
                    AL.mult, AL.mult))
                p.barrier("vector")
                def trb(e):
                    for g in range(NTILE):
                        e.transpose(ps_col[:, g:g + 1], rows[0:1, g * 128:(g + 1) * 128],
                                    ident_f[0:1, 0:1])
                p.on("tensor", trb)
                p.barrier("tensor")
                p.on("vector", lambda e, vh=vh: e.tensor_copy(
                    negB[:, vh * 8:(vh + 1) * 8], ps_col[:, 0:8]))
                p.barrier("vector")

        for oc in range(NCH):
            qg, hp = oc // 4, oc % 4
            kc = hp
            p.on("vector", lambda e: e.memset(ls[:, :], 0.0))
            p.barrier("vector")
            for g in range(NTILE):
                for blk in range(8):
                    def scm(e, oc=oc, kc=kc, g=g, blk=blk):
                        for half in range(2):
                            b0 = half * 64
                            e.matmul(ps_s[half][:, :],
                                     q_sb[b0:b0 + 64,
                                          oc * TLOC + g * 128: oc * TLOC + g * 128 + 128],
                                     k_sb[b0:b0 + 64, kc * S + blk * 512: kc * S + blk * 512 + 512],
                                     start=True, stop=True)
                    p.on("tensor", scm)
                    p.barrier("tensor")
                    def expp(e, qg=qg, hp=hp, g=g, blk=blk):
                        for half in range(2):
                            vh = qg * 8 + hp * 2 + half
                            e.activation(p_bf[half][:, :], ps_s[half][:, :], AF.Exp,
                                         scale=sc_col[:, 0:1],
                                         bias=negB[:, vh * 8 + g: vh * 8 + g + 1],
                                         accum_out=ls[:, half * 64 + g * 8 + blk:
                                                      half * 64 + g * 8 + blk + 1])
                    p.on("scalar", expp)
                    p.barrier("scalar")
                    def trp(e):
                        for half in range(2):
                            for j in range(4):
                                e.transpose(pt_ps[half][:, j * 128:(j + 1) * 128],
                                            p_bf[half][:, j * 128:(j + 1) * 128],
                                            consts["ident_bf"][:, :])
                    p.on("tensor", trp)
                    p.barrier("tensor")
                    def ptc(e):
                        for half in range(2):
                            e.tensor_copy(pt_bf[half][:, :], pt_ps[half][:, :])
                    p.on("vector", ptc)
                    p.barrier("vector")
                    def pv(e, kc=kc, g=g, blk=blk):
                        for half in range(2):
                            b0 = half * 64
                            for j in range(4):
                                kt = blk * 4 + j
                                e.matmul(out_ps[b0:b0 + 64, g * 128:(g + 1) * 128],
                                         vt_sb[:, (kc * 32 + kt) * 128 + b0:
                                               (kc * 32 + kt) * 128 + b0 + 64],
                                         pt_bf[half][:, j * 128:(j + 1) * 128],
                                         start=(blk == 0 and j == 0),
                                         stop=(blk == 7 and j == 3))
                    p.on("tensor", pv)
                    p.barrier("tensor")
            p.on("vector", lambda e: e.tensor_reduce(
                invl[:, 0:16], ls[:, :].rearrange("p (h g b) -> p h g b", h=2, g=8),
                mybir.AxisListType.X, AL.add))
            p.barrier("vector")
            p.on("vector", lambda e: e.reciprocal(invl[:, 0:16], invl[:, 0:16]))
            p.barrier("vector")
            for g in range(NTILE):
                for half in range(2):
                    def tr1(e, half=half, g=g):
                        e.transpose(ps_row[0:1, 0:128], invl[:, half * 8 + g: half * 8 + g + 1],
                                    ident_f[:, :])
                    p.on("tensor", tr1)
                    p.barrier("tensor")
                    p.on("vector", lambda e, half=half: e.tensor_copy(
                        invl_row[0:1, half * 128:(half + 1) * 128], ps_row[0:1, 0:128]))
                    p.barrier("vector")
                def bcm(e):
                    for half in range(2):
                        b0 = half * 64
                        e.matmul(ps_col[b0:b0 + 64, 0:128], consts["onesr_f"][0:1, 0:64],
                                 invl_row[0:1, half * 128:(half + 1) * 128],
                                 start=True, stop=True)
                p.on("tensor", bcm)
                p.barrier("tensor")
                p.on("vector", lambda e: e.tensor_copy(bc2_sb[:, :], ps_col[:, 0:128]))
                p.barrier("vector")
                p.on("vector", lambda e, oc=oc, g=g: e.tensor_tensor(
                    attn_bf[:, oc * TLOC + g * 128: oc * TLOC + (g + 1) * 128],
                    out_ps[:, g * 128:(g + 1) * 128], bc2_sb[:, :], AL.mult))
                p.barrier("vector")

    x_sb = p.sb("x_sb", [128, NCH * TLOC], F32)
    scrc = p.sb("scrc", [128, TLOC], BF16)
    p.load(x_sb[:, :], x_d[:, :, :])
    p.barrier("gpsimd")
    for oc in range(NCH):
        sl = slice(oc * TLOC, (oc + 1) * TLOC)
        def comb(e, sl=sl):
            e.tensor_scalar(scrc[:, :], attn_bf[:, sl], gcol[:, 1:2], None, AL.mult)
            e.drain()
            e.scalar_tensor_tensor(scrc[:, :], amem_bf[:, sl], gcol[:, 0:1], scrc[:, :],
                                   AL.mult, AL.add)
            e.drain()
            e.tensor_tensor(x_sb[:, sl], scrc[:, :], x_sb[:, sl], AL.add)
        p.on("vector", comb)
        p.barrier("vector")
        p.store(xo_d[:, oc, :], x_sb[:, sl])
    return p.finalize()


_PROG_CACHE = {}

def _prog(key, fn):
    if key not in _PROG_CACHE:
        _PROG_CACHE[key] = fn()
    return _PROG_CACHE[key]

def run(nc, maps):
    return run_bass_kernel_spmd(nc, maps, core_ids=list(range(NCORES)))

def _percore(fn):
    return [fn(c) for c in range(NCORES)]

def kernel(**inp):
    f32 = np.float32
    consts = const_inputs()
    cos0, sin0, p0t0 = rope0_tables()
    cos1, sin1, p0t1 = rope1_tables()
    cc = [core_cols(c) for c in range(NCORES)]
    cb = [core_batch(c) for c in range(NCORES)]
    masks = [causal_diag_mask(c) for c in range(NCORES)]

    xfm = fm(inp["L0_attn_a_x"])
    xs = [pmaj(xfm[cb[c]][:, cc[c]]) for c in range(NCORES)]

    def x_full():
        out = [np.zeros((D, S), dtype=f32) for _ in range(B)]
        for c in range(NCORES):
            out[cb[c]][:, cc[c]] = un_pmaj(xs[c])
        return out

    def mha(wq, wk, wv, bq, bk, bv, wo, bo, cos, sin, p0t, scale):
        nonlocal xs
        nc1 = _prog("qkv", build_l0a_qkv)
        packs = [pack_lhsT(w) for w in (wq, wk, wv)]
        biases = np.concatenate([bq, bk, bv]).astype(f32)[None, :]
        r1 = run(nc1, _percore(lambda c: dict(
            x=xs[c], wq=packs[0], wk=packs[1], wv=packs[2], biases=biases,
            cosT=np.ascontiguousarray(cos[:, cc[c]]),
            sinT=np.ascontiguousarray(sin[:, cc[c]]), p0t=p0t, **consts)))
        kfull = [np.zeros((D, S), dtype=BF) for _ in range(B)]
        vtfull = [np.zeros((NCH, S, 128), dtype=BF) for _ in range(B)]
        for c in range(NCORES):
            kfull[cb[c]][:, cc[c]] = un_pmaj(r1.results[c]["k_out"])
            vtfull[cb[c]][:, cc[c], :] = r1.results[c]["vt_out"]
        kfull = [pmaj(k) for k in kfull]
        nc2 = _prog("attn", build_causal_attn)
        wo_p = pack_lhsT(wo)
        r2 = run(nc2, _percore(lambda c: dict(
            q=r1.results[c]["q_out"], kfull=kfull[cb[c]], vtfull=vtfull[cb[c]],
            x=xs[c], wo=wo_p, bo=bo.astype(f32)[None, :], masks=masks[c],
            scales=np.array([[scale, -scale]], dtype=f32),
            scale_col=np.full((128, 1), scale, dtype=f32),
            ident_f32=IDENT, **consts)))
        xs = [r2.results[c]["xo"] for c in range(NCORES)]

    def l0ffn():
        nonlocal xs
        nc3 = _prog("l0ffn", build_l0_ffn)
        gw = pack_lhsT(inp["L0_ffn_gate_w"])
        uw = pack_lhsT(inp["L0_ffn_up_w"])
        dw = pack_lhsT(inp["L0_ffn_down_w"])
        gs = np.ascontiguousarray(inp["L0_ffn_gate_s"].reshape(44, 128).T)
        us = np.ascontiguousarray(inp["L0_ffn_up_s"].reshape(44, 128).T)
        ds = np.ascontiguousarray(inp["L0_ffn_down_s"].reshape(16, 128).T)
        r = run(nc3, _percore(lambda c: dict(
            x=xs[c], gate_w=gw, up_w=uw, down_w=dw, gate_s=gs, up_s=us, down_s=ds,
            **consts)))
        xs = [r.results[c]["xo"] for c in range(NCORES)]

    def l1ffn(qfc, qpj, am):
        nonlocal xs
        nc11 = _prog("l1ffn", build_l1_ffn)
        r = run(nc11, _percore(lambda c: dict(
            x=xs[c], qfc=qfc, qpj=qpj, am=am, **consts)))
        xs = [r.results[c]["xo"] for c in range(NCORES)]

    mha(inp["L0_attn_a_Wq"], inp["L0_attn_a_Wk"], inp["L0_attn_a_Wv"],
        inp["L0_attn_a_bq"], inp["L0_attn_a_bk"], inp["L0_attn_a_bv"],
        inp["L0_attn_a_Wo"], inp["L0_attn_a_bo"], cos0, sin0, p0t0,
        1.0 / math.sqrt(128.0))
    l0ffn()
    nc4 = _prog("mix", build_rwkv_mix)
    xf = x_full()
    tm_all = np.concatenate([
        inp[f"L0_attn_b_time_mix_{n}"].reshape(16, 128).T for n in ("k", "v", "r")],
        axis=1).astype(f32)
    def halo_of(c):
        h = np.zeros((D, NTILE), dtype=f32)
        for g in range(NTILE):
            gc = 128 * (4 * g + c % 4)
            if gc > 0:
                h[:, g] = xf[cb[c]][:, gc - 1]
        return pmaj(h)
    r4 = run(nc4, _percore(lambda c: dict(
        x=xs[c], xhalo=halo_of(c), tmix=tm_all, **consts)))
    full = {}
    for nm in ("xk", "xv", "xr"):
        fa = np.zeros((D, B * S), dtype=BF)
        for c in range(NCORES):
            fa[:, cb[c] * S + cc[c]] = un_pmaj(r4.results[c][nm])
        full[nm] = pmaj(fa)
    nc5 = _prog("rwkvcore", build_rwkv_core)
    packs_b = {n: pack_lhsT(inp[f"L0_attn_b_{n}"]) for n in ("Wk", "Wv", "Wr")}
    td_pm = inp["L0_attn_b_time_decay"].reshape(16, 128).T
    tf_pm = inp["L0_attn_b_time_first"].reshape(16, 128).T
    r5 = run(nc5, _percore(lambda c: dict(
        xk=full["xk"], xv=full["xv"], xr=full["xr"],
        wk=packs_b["Wk"][2 * c:2 * c + 2], wv=packs_b["Wv"][2 * c:2 * c + 2],
        wr=packs_b["Wr"][2 * c:2 * c + 2],
        td=np.ascontiguousarray(td_pm[:, 2 * c:2 * c + 2]),
        tf=np.ascontiguousarray(tf_pm[:, 2 * c:2 * c + 2]))))
    g_full = np.zeros((D, B * S), dtype=BF)
    for c in range(NCORES):
        g = r5.results[c]["g_out"]
        for cg in range(2):
            g_full[(2 * c + cg) * 128:(2 * c + cg + 1) * 128, :] = g[:, cg, :]
    nc6 = _prog("rwkvout", build_rwkv_out)
    wo_b = pack_lhsT(inp["L0_attn_b_Wo"])
    r6 = run(nc6, _percore(lambda c: dict(
        g=pmaj(np.ascontiguousarray(g_full[:, cb[c] * S + cc[c]])),
        x=xs[c], wo=wo_b, **consts)))
    xs = [r6.results[c]["xo"] for c in range(NCORES)]
    l0ffn()
    zb = np.zeros(D, dtype=f32)
    wq1 = np.ascontiguousarray(
        inp["L1_attn_a_wq"].transpose(1, 2, 0).reshape(D, D))
    wk1 = np.ascontiguousarray(
        inp["L1_attn_a_wk"].transpose(1, 2, 0).reshape(D, D))
    wv1 = np.ascontiguousarray(
        inp["L1_attn_a_wv"].transpose(1, 2, 0).reshape(D, D))
    wo1 = np.ascontiguousarray(inp["L1_attn_a_wo"].reshape(D, D).T)
    mha(wq1, wk1, wv1, zb, zb, zb, wo1, zb, cos1, sin1, p0t1, 1.0 / 128.0)
    nc10 = _prog("tern", build_ternarize)
    wfc_p = pack_lhsT(inp["L1_ffn_w_fc"])
    wpj_p = pack_lhsT(inp["L1_ffn_w_proj"])
    r10 = run(nc10, _percore(lambda c: dict(
        wfc=wfc_p, wpj=wpj_p, wfc_sh=wfc_p[5 * c:5 * c + 5],
        wpj_sh=wpj_p[2 * c:2 * c + 2], **consts)))
    qfc = np.concatenate([r10.results[c]["qfc"] for c in range(NCORES)], axis=0)
    qpj = np.concatenate([r10.results[c]["qpj"] for c in range(NCORES)], axis=0)
    am = r10.results[0]["am"]
    l1ffn(qfc, qpj, am)
    nc12 = _prog("l1bqkv", build_l1b_qkv)
    r12 = run(nc12, _percore(lambda c: dict(
        x=xs[c], wq=pack_lhsT(inp["L1_attn_b_Wq"]),
        wk=pack_lhsT(inp["L1_attn_b_Wk"]), wv=pack_lhsT(inp["L1_attn_b_Wv"]),
        **consts)))
    kfull1 = [np.zeros((512, S), dtype=BF) for _ in range(B)]
    vtfull1 = [np.zeros((4, S, 128), dtype=BF) for _ in range(B)]
    for c in range(NCORES):
        kfull1[cb[c]][:, cc[c]] = un_pmaj(r12.results[c]["k_out"])
        vtfull1[cb[c]][:, cc[c], :] = r12.results[c]["vt_out"]
    kfull1 = [pmaj(k) for k in kfull1]
    nc13 = _prog("l1battn", build_l1b_attn)
    r13 = run(nc13, _percore(lambda c: dict(
        q=r12.results[c]["q_out"], kfull=kfull1[cb[c]], vtfull=vtfull1[cb[c]],
        x=xs[c], mem=pack_lhsT(np.ascontiguousarray(inp["L1_attn_b_memory"].T)),
        mnorm=np.ascontiguousarray(inp["L1_attn_b_memory_norm"].reshape(4, 128).T),
        mweight=inp["L1_attn_b_memory_weight"].reshape(1, 1).astype(f32),
        ident_f32=IDENT, **consts)))
    xs = [r13.results[c]["xo"] for c in range(NCORES)]
    l1ffn(qfc, qpj, am)

    out = np.zeros((B, S, D), dtype=f32)
    for c in range(NCORES):
        out[cb[c]][cc[c], :] = un_pmaj(xs[c]).T
    return out


# version 03
# speedup vs baseline: 1.1079x; 1.1079x over previous
import math
from contextlib import ExitStack

import numpy as np
import ml_dtypes

import concourse.bass as bass
import concourse.mybir as mybir
from concourse.alu_op_type import AluOpType as AL
from concourse.bass_utils import run_bass_kernel_spmd
from concourse.replica_groups import maybe_share_collective_output_space
from concourse.aot_env import allow_ici_all_to_all

F32 = mybir.dt.float32
BF16 = mybir.dt.bfloat16
AF = mybir.ActivationFunctionType
BF = ml_dtypes.bfloat16

D = 2048
NCH = D // 128
B = 2
S = 4096
NCORES = 8
TLOC = 1024
NTILE = 8
NEG = -1.0e30
EPS = 1e-5


def core_cols(core):
    cc = core % 4
    return np.concatenate([128 * (4 * g + cc) + np.arange(128) for g in range(NTILE)])

def core_batch(core):
    return core // 4

def fm(x):
    return [np.ascontiguousarray(np.asarray(x)[b].T) for b in range(B)]

def pmaj(a):
    d, t = a.shape
    nch = d // 128
    return np.ascontiguousarray(a.reshape(nch, 128, t).transpose(1, 0, 2))

def un_pmaj(a):
    p, nch, t = a.shape
    return np.ascontiguousarray(a.transpose(1, 0, 2).reshape(nch * 128, t))

def pack_lhsT(w):
    w = np.ascontiguousarray(np.asarray(w))
    O, I = w.shape
    return np.ascontiguousarray(
        w.reshape(O // 128, 128, I // 128, 128).transpose(0, 3, 2, 1).reshape(O // 128, 128, I))

def rope0_tables():
    d = 128
    theta = np.power(10000.0, np.repeat(-np.arange(0, d, 2, dtype=np.float64) / d, 2))
    ang = np.arange(S, dtype=np.float64)[:, None] * theta[None, :]
    cos = np.ascontiguousarray(np.cos(ang).T.astype(np.float32))
    sin = np.ascontiguousarray(np.sin(ang).T.astype(np.float32))
    P = np.zeros((d, d), dtype=np.float32)
    for i in range(0, d, 2):
        P[i, i + 1] = -1.0
        P[i + 1, i] = 1.0
    return cos, sin, np.ascontiguousarray(P.T)

def rope1_tables():
    d, half = 128, 64
    freqs = np.power(10000.0, -np.arange(half, dtype=np.float64) / half)
    fr = np.concatenate([freqs, freqs])
    ang = np.arange(S, dtype=np.float64)[:, None] * fr[None, :]
    cos = np.ascontiguousarray(np.cos(ang).T.astype(np.float32))
    sin = np.ascontiguousarray(np.sin(ang).T.astype(np.float32))
    P = np.zeros((d, d), dtype=np.float32)
    for j in range(half):
        P[j, j + half] = -1.0
        P[j + half, j] = 1.0
    return cos, sin, np.ascontiguousarray(P.T)

def causal_diag_mask(core):
    cc = core % 4
    m = np.zeros((NTILE, 128, 512), dtype=np.float32)
    for g in range(NTILE):
        q = 512 * g + 128 * cc + np.arange(128)[:, None]
        kt = 512 * g + np.arange(512)[None, :]
        m[g] = np.where(q < kt, NEG, 0.0).astype(np.float32)
    return m

IDENT = np.eye(128, dtype=np.float32)
ONES_COL = np.ones((128, 1), dtype=np.float32)
ONES_ROW = np.ones((1, 128), dtype=np.float32)


ENGINES = ("tensor", "vector", "scalar", "gpsimd", "sync")

class Prog:
    def __init__(self, name):
        self.name = name
        self.nc = bass.Bass("TRN2", target_bir_lowering=False)
        self.stack = ExitStack()
        self.q = {e: [] for e in ENGINES}
        self.barn = 0
        self.wait_at = {e: 0 for e in ENGINES}
        self.bar = self.stack.enter_context(self.nc.semaphore("bar"))
        self.stm = self.stack.enter_context(self.nc.semaphore("stm"))
        self.ldm = self.stack.enter_context(self.nc.semaphore("ldm"))
        self.ccm = self.stack.enter_context(self.nc.semaphore("ccm"))
        self.st_cnt = 0
        self.ld_cnt = 0
        self.cc_cnt = 0

    def dram(self, name, shape, dt, kind):
        return self.nc.dram_tensor(name, shape, dt, kind=kind)

    def _uniq(self, name):
        n = getattr(self, "_names", None)
        if n is None:
            n = self._names = {}
        k = n.get(name, 0)
        n[name] = k + 1
        return name if k == 0 else f"{name}_{k}"

    def sb(self, name, shape, dt, stack=None):
        return (stack or self.stack).enter_context(
            self.nc.sbuf_tensor(self._uniq(name), shape, dt))

    def ps(self, name, shape, dt=F32, stack=None):
        return (stack or self.stack).enter_context(
            self.nc.psum_tensor(self._uniq(name), shape, dt))

    def on(self, eng, f):
        if self.wait_at[eng] < self.barn:
            tgt, bar = self.barn, self.bar
            self.q[eng].append(lambda e, t=tgt, b=bar: e.wait_ge(b, t))
            self.wait_at[eng] = self.barn
        self.q[eng].append(f)

    def barrier(self, *producers):
        bar = self.bar
        for eng in producers:
            if self.wait_at[eng] < self.barn:
                tgt = self.barn
                self.q[eng].append(lambda e, t=tgt, b=bar: e.wait_ge(b, t))
                self.wait_at[eng] = self.barn
            def f(e, b=bar):
                e.drain()
                e.sem_inc(b, 1)
            self.q[eng].append(f)
        self.barn += len(producers)

    def load(self, sbuf_ap, dram_ap):
        self.ld_cnt += 16
        self.on("gpsimd", lambda e, o=sbuf_ap, i=dram_ap, s=self.ldm:
                e.dma_start(out=o, in_=i, max_dma_last_dim=8192).then_inc(s, 16))

    def store(self, dram_ap, sbuf_ap):
        self.st_cnt += 16
        self.on("gpsimd", lambda e, o=dram_ap, i=sbuf_ap, s=self.stm:
                e.dma_start(out=o, in_=i, max_dma_last_dim=8192).then_inc(s, 16))
        self.barrier("gpsimd")

    def collective(self, kind, op, rg, in_ap, out_ap, cc_dim=None):
        self.barrier("gpsimd")
        self.cc_cnt += 1
        n = self.cc_cnt
        def f(e, kind=kind, op=op, rg=rg, i=in_ap, o=out_ap, cd=cc_dim, s=self.ccm):
            kw = dict(cc_dim=cd) if cd else {}
            if kind == "AllToAll":
                with allow_ici_all_to_all():
                    e.collective_compute(kind, op, replica_groups=rg, ins=[i],
                                         outs=[o], **kw).then_inc(s)
            else:
                e.collective_compute(kind, op, replica_groups=rg, ins=[i],
                                     outs=[o], **kw).then_inc(s)
        self.on("gpsimd", f)
        self.on("gpsimd", lambda e, s=self.ccm, t=n: e.wait_ge(s, t))
        self.barrier("gpsimd")

    def finalize(self):
        n = self.st_cnt
        if n:
            self.on("gpsimd", lambda e, s=self.stm, t=n: e.wait_ge(s, t))
        block = self.stack.enter_context(self.nc.Block())
        for name in ENGINES:
            fs = self.q[name]
            if not fs:
                continue
            getattr(block, name)(lambda e, fs=fs: [f(e) for f in fs])
        self.stack.close()
        return self.nc


def alloc_consts(p):
    c = {}
    c["onesc_bf"] = p.sb("onesc_bf", [128, 1], BF16)
    c["onesc_f"] = p.sb("onesc_f", [128, 1], F32)
    c["onesr_f"] = p.sb("onesr_f", [1, 128], F32)
    c["onesr_bf"] = p.sb("onesr_bf", [1, 128], BF16)
    c["ident_bf"] = p.sb("ident_bf", [128, 128], BF16)
    onesc_d = p.dram("ones_col", [128, 1], F32, "ExternalInput")
    onesr_d = p.dram("ones_row", [1, 128], F32, "ExternalInput")
    ident_d = p.dram("ident", [128, 128], F32, "ExternalInput")
    p.load(c["onesc_bf"][:, :], onesc_d[:, :])
    p.load(c["onesc_f"][:, :], onesc_d[:, :])
    p.load(c["onesr_f"][0:1, :], onesr_d[0:1, :])
    p.load(c["onesr_bf"][0:1, :], onesr_d[0:1, :])
    p.load(c["ident_bf"][:, :], ident_d[:, :])
    return c

def const_inputs():
    return dict(ones_col=ONES_COL, ones_row=ONES_ROW, ident=IDENT)

def emit_ln(p, consts, x_dram, x_sb, xc_bf, r_b, invr, T=TLOC, nch=NCH, load_x=True):
    with ExitStack() as sk:
        sq_f = p.sb("ln_sq", [128, T], F32, sk)
        ps_sum = p.ps("ln_ps1", [1, T], F32, sk)
        ps_sum2 = p.ps("ln_ps2", [1, T], F32, sk)
        ps_bc = p.ps("ln_bc", [128, T], F32, sk)
        rows = p.sb("ln_rows", [1, 2 * T], F32, sk)
        eps_cell = p.sb("eps_cell", [1, 1], F32, sk)
        p.on("vector", lambda e: e.memset(eps_cell[0:1, 0:1], float(EPS)))
        if load_x:
            p.load(x_sb[:, :], x_dram[:, :, :])
            p.barrier("gpsimd")
        onesc = consts["onesc_f"]
        grps = [(s, min(512, T - s)) for s in range(0, T, 512)]
        def sums1(e):
            for s0, n in grps:
                for cch in range(nch):
                    e.matmul(ps_sum[0:1, s0:s0 + n], onesc[:, 0:1],
                             x_sb[:, cch * T + s0: cch * T + s0 + n],
                             start=(cch == 0), stop=(cch == nch - 1))
        p.on("tensor", sums1)
        p.barrier("tensor")
        for cch in range(nch):
            p.on("scalar", lambda e, cch=cch: e.activation(
                sq_f[:, :], x_sb[:, cch * T:(cch + 1) * T], AF.Square))
            p.barrier("scalar")
            def sums2(e, cch=cch):
                for s0, n in grps:
                    e.matmul(ps_sum2[0:1, s0:s0 + n], onesc[:, 0:1], sq_f[:, s0:s0 + n],
                             start=(cch == 0), stop=(cch == nch - 1))
            p.on("tensor", sums2)
            p.barrier("tensor")
        negmu = rows[0:1, 0:T]
        var = rows[0:1, T:2 * T]
        p.on("vector", lambda e: e.tensor_scalar(negmu, ps_sum[0:1, :], -1.0 / D, None, AL.mult))
        p.on("vector", lambda e: e.drain())
        p.on("vector", lambda e: e.tensor_tensor(var, negmu, negmu, AL.mult))
        p.on("vector", lambda e: e.drain())
        p.on("vector", lambda e: e.scalar_tensor_tensor(var, ps_sum2[0:1, :], 1.0 / D, var,
                                                        AL.mult, AL.subtract))
        p.barrier("vector")
        p.on("scalar", lambda e: e.activation(invr[0:1, :], var, AF.Sqrt, bias=eps_cell[0:1, 0:1]))
        p.barrier("scalar")
        rstd = rows[0:1, T:2 * T]
        p.on("vector", lambda e: e.reciprocal(rstd, invr[0:1, :]))
        p.barrier("vector")
        onesr = consts["onesr_f"]
        def bc1(e):
            for s0, n in grps:
                e.matmul(ps_bc[:, s0:s0 + n], onesr[0:1, :], rstd[0:1, s0:s0 + n],
                         start=True, stop=True)
        p.on("tensor", bc1)
        p.barrier("tensor")
        p.on("vector", lambda e: e.tensor_copy(r_b[:, :], ps_bc[:, :]))
        p.barrier("vector")
        def bc2(e):
            for s0, n in grps:
                e.matmul(ps_bc[:, s0:s0 + n], onesr[0:1, :], negmu[0:1, s0:s0 + n],
                         start=True, stop=True)
        p.on("tensor", bc2)
        p.barrier("tensor")
        def center(e):
            for cch in range(nch):
                sl = slice(cch * T, (cch + 1) * T)
                e.tensor_tensor(xc_bf[:, sl], x_sb[:, sl], ps_bc[:, :], AL.add)
        p.on("vector", center)
        p.barrier("vector")

def emit_proj_stream(p, w_d, n_oc, xc_bf, wbuf, psums, out_cb, rank1=None, T=TLOC,
                     nch=NCH, xc_T=None):
    xT = xc_T or T
    ngr = T // 512
    for oc in range(n_oc):
        p.load(wbuf[:, :], w_d[oc, :, :])
        p.barrier("gpsimd")
        def mms(e, oc=oc):
            for t in range(ngr):
                for cch in range(nch):
                    e.matmul(psums[t][:, :], wbuf[:, cch * 128:(cch + 1) * 128],
                             xc_bf[:, cch * xT + t * 512: cch * xT + t * 512 + 512],
                             start=(cch == 0), stop=(rank1 is None and cch == nch - 1))
                if rank1 is not None:
                    lhs_row, rhs_row = rank1
                    e.matmul(psums[t][:, :], lhs_row(oc), rhs_row(t), start=False, stop=True)
        p.on("tensor", mms)
        p.barrier("tensor")
        out_cb(oc)


def build_l0a_qkv(rope_first=True, has_bias=True, tag="l0a"):
    p = Prog(f"{tag}_qkv")
    x_d = p.dram("x", [128, NCH, TLOC], F32, "ExternalInput")
    wq_d = p.dram("wq", [NCH, 128, D], F32, "ExternalInput")
    wk_d = p.dram("wk", [NCH, 128, D], F32, "ExternalInput")
    wv_d = p.dram("wv", [NCH, 128, D], F32, "ExternalInput")
    bias_d = p.dram("biases", [1, 3 * D], F32, "ExternalInput")
    cos_d = p.dram("cosT", [128, TLOC], F32, "ExternalInput")
    sin_d = p.dram("sinT", [128, TLOC], F32, "ExternalInput")
    p0t_d = p.dram("p0t", [128, 128], F32, "ExternalInput")
    q_o = p.dram("q_out", [128, NCH, TLOC], BF16, "ExternalOutput")
    k_o = p.dram("k_out", [128, NCH, TLOC], BF16, "ExternalOutput")
    vt_o = p.dram("vt_out", [NCH, TLOC, 128], BF16, "ExternalOutput")

    consts = alloc_consts(p)
    x_sb = p.sb("x_sb", [128, NCH * TLOC], F32)
    xc_bf = p.sb("xc", [128, NCH * TLOC], BF16)
    r_b = p.sb("r_b", [128, TLOC], F32)
    invr = p.sb("invr", [1, TLOC], F32)
    emit_ln(p, consts, x_d, x_sb, xc_bf, r_b, invr)

    cos_bf = p.sb("cosb", [128, TLOC], BF16)
    sin_bf = p.sb("sinb", [128, TLOC], BF16)
    p0t_bf = p.sb("p0tb", [128, 128], BF16)
    b_sb = p.sb("b_sb", [1, 3 * D], BF16)
    invr_bf = p.sb("invr_bf", [1, TLOC], BF16)
    p.load(cos_bf[:, :], cos_d[:, :])
    p.load(sin_bf[:, :], sin_d[:, :])
    p.load(p0t_bf[:, :], p0t_d[:, :])
    p.load(b_sb[0:1, :], bias_d[0:1, :])
    p.barrier("gpsimd")
    p.on("vector", lambda e: e.tensor_copy(invr_bf[0:1, :], invr[0:1, :]))
    p.barrier("vector")

    wbuf = p.sb("wbuf", [128, D], BF16)
    ps_p = [p.ps(f"ps_p{i}", [128, 512]) for i in range(2)]
    ps_r = [p.ps(f"ps_r{i}", [128, 512]) for i in range(2)]
    vt_ps = p.ps("vt_ps", [128, 1024], BF16)
    qn_bf = p.sb("qn", [128, TLOC], BF16)
    t1_bf = p.sb("t1", [128, TLOC], BF16)
    t2_bf = p.sb("t2", [128, TLOC], BF16)
    ro_bf = p.sb("ro", [128, TLOC], BF16)

    def rank1(boff):
        if not has_bias:
            return None
        return (lambda oc, bo=boff: b_sb[0:1, bo + oc * 128: bo + (oc + 1) * 128],
                lambda t: invr_bf[0:1, t * 512:(t + 1) * 512])

    def evac(oc):
        def f(e):
            for t in range(2):
                sl = slice(t * 512, (t + 1) * 512)
                e.tensor_tensor(qn_bf[:, sl], ps_p[t][:, :], r_b[:, sl], AL.mult)
        p.on("vector", f)
        p.barrier("vector")

    def rope_store(oc, out_dram):
        def shift(e):
            for t in range(2):
                e.matmul(ps_r[t][:, :], p0t_bf[:, :], qn_bf[:, t * 512:(t + 1) * 512],
                         start=True, stop=True)
        p.on("tensor", shift)
        p.barrier("tensor")
        def rope(e):
            for t in range(2):
                sl = slice(t * 512, (t + 1) * 512)
                e.tensor_tensor(t1_bf[:, sl], qn_bf[:, sl], cos_bf[:, sl], AL.mult)
                e.tensor_tensor(t2_bf[:, sl], ps_r[t][:, :], sin_bf[:, sl], AL.mult)
                e.drain()
                e.tensor_tensor(ro_bf[:, sl], t1_bf[:, sl], t2_bf[:, sl], AL.add)
        p.on("vector", rope)
        p.barrier("vector")
        p.store(out_dram[:, oc, :], ro_bf[:, :])

    def q_cb(oc):
        evac(oc)
        rope_store(oc, q_o)

    def k_cb(oc):
        evac(oc)
        rope_store(oc, k_o)

    def v_cb(oc):
        evac(oc)
        def vtr(e):
            for tc in range(8):
                e.transpose(vt_ps[:, tc * 128:(tc + 1) * 128],
                            qn_bf[:, tc * 128:(tc + 1) * 128], consts["ident_bf"][:, :])
        p.on("tensor", vtr)
        p.barrier("tensor")
        p.on("vector", lambda e: e.tensor_copy(ro_bf[:, :], vt_ps[:, :]))
        p.barrier("vector")
        p.store(vt_o[oc, :, :].rearrange("(tc pt) d -> pt tc d", pt=128), ro_bf[:, :])

    emit_proj_stream(p, wq_d, NCH, xc_bf, wbuf, ps_p, q_cb, rank1=rank1(0))
    emit_proj_stream(p, wk_d, NCH, xc_bf, wbuf, ps_p, k_cb, rank1=rank1(D))
    emit_proj_stream(p, wv_d, NCH, xc_bf, wbuf, ps_p, v_cb, rank1=rank1(2 * D))
    return p.finalize()

def build_causal_attn(tag="causal"):
    p = Prog(f"{tag}_attn")
    q_d = p.dram("q", [128, NCH, TLOC], BF16, "ExternalInput")
    k_d = p.dram("kfull", [128, NCH, S], BF16, "ExternalInput")
    vt_d = p.dram("vtfull", [NCH, S, 128], BF16, "ExternalInput")
    x_d = p.dram("x", [128, NCH, TLOC], F32, "ExternalInput")
    wo_d = p.dram("wo", [NCH, 128, D], F32, "ExternalInput")
    bo_d = p.dram("bo", [1, D], F32, "ExternalInput")
    mask_d = p.dram("masks", [NTILE, 128, 512], F32, "ExternalInput")
    scales_d = p.dram("scales", [1, 2], F32, "ExternalInput")
    scalecol_d = p.dram("scale_col", [128, 1], F32, "ExternalInput")
    xo_d = p.dram("xo", [128, NCH, TLOC], F32, "ExternalOutput")

    consts = alloc_consts(p)
    ident_f = p.sb("ident_f", [128, 128], F32)
    ident_d = p.dram("ident_f32", [128, 128], F32, "ExternalInput")
    p.load(ident_f[:, :], ident_d[:, :])

    q_sb = p.sb("q_sb", [128, NCH * TLOC], BF16)
    mask_sb = p.sb("mask_sb", [128, NTILE * 512], F32)
    kh = p.sb("kh", [128, S], BF16)
    vth = p.sb("vth", [128, S], BF16)
    attn_bf = p.sb("attn_bf", [128, NCH * TLOC], BF16)
    p_bf = p.sb("p_bf", [128, 512], BF16)
    pt_bf = p.sb("pt_bf", [128, 512], BF16)
    sq_bf = p.sb("sq_bf", [128, S], BF16)
    rows = p.sb("a_rows", [1, S], F32)
    cells = p.sb("cells", [1, 8], F32)
    negB = p.sb("negB", [128, 8], F32)
    ls = p.sb("ls", [128, 64], F32)
    lred = p.sb("lred", [128, 16], F32)
    invl_row = p.sb("invl_row", [1, TLOC], F32)
    bcast_sb = p.sb("bcast_sb", [128, 128], F32)

    attn_psum_stack = ExitStack()
    ps_s = p.ps("ps_s", [128, 512], F32, attn_psum_stack)
    pt_ps = p.ps("pt_ps", [128, 512], BF16, attn_psum_stack)
    out_ps = p.ps("out_ps", [128, 1024], F32, attn_psum_stack)
    ps_row = p.ps("ps_row", [1, 512], F32, attn_psum_stack)
    ps_col = p.ps("ps_col", [128, 512], F32, attn_psum_stack)

    sc_cells = p.sb("sc_cells", [1, 2], F32)
    sc_col = p.sb("sc_col", [128, 1], F32)
    p.load(sc_cells[0:1, :], scales_d[0:1, :])
    p.load(sc_col[:, :], scalecol_d[:, :])
    p.load(q_sb[:, :], q_d[:, :, :])
    p.load(mask_sb[:, :], mask_d.rearrange("g p t -> p g t"))
    p.barrier("gpsimd")

    for h in range(NCH):
        p.load(kh[:, :], k_d[:, h, :])
        p.load(vth[:, :], vt_d[h, :, :].rearrange("(tc pt) d -> pt tc d", pt=128))
        p.barrier("gpsimd")
        p.on("scalar", lambda e: e.activation(sq_bf[:, :], kh[:, :], AF.Square))
        p.barrier("scalar")
        for blk in range(8):
            def knm(e, blk=blk):
                e.matmul(ps_row[0:1, :], consts["onesc_bf"][:, 0:1],
                         sq_bf[:, blk * 512:(blk + 1) * 512], start=True, stop=True)
            p.on("tensor", knm)
            p.barrier("tensor")
            p.on("vector", lambda e, blk=blk: e.tensor_copy(
                rows[0:1, blk * 512:(blk + 1) * 512], ps_row[0:1, :]))
            p.barrier("vector")
        p.on("vector", lambda e: e.tensor_reduce(cells[0:1, 0:1], rows[0:1, :],
                                                 mybir.AxisListType.X, AL.max))
        p.barrier("vector")
        p.on("scalar", lambda e: e.activation(cells[0:1, 1:2], cells[0:1, 0:1], AF.Sqrt))
        p.barrier("scalar")
        p.on("scalar", lambda e, h=h: e.activation(
            sq_bf[:, 0:TLOC], q_sb[:, h * TLOC:(h + 1) * TLOC], AF.Square))
        p.barrier("scalar")
        for t in range(2):
            def qnm(e, t=t):
                e.matmul(ps_row[0:1, :], consts["onesc_bf"][:, 0:1],
                         sq_bf[:, t * 512:(t + 1) * 512], start=True, stop=True)
            p.on("tensor", qnm)
            p.barrier("tensor")
            p.on("scalar", lambda e, t=t: e.activation(
                rows[0:1, t * 512:(t + 1) * 512], ps_row[0:1, :], AF.Sqrt))
            p.barrier("scalar")
        p.on("vector", lambda e: e.tensor_scalar(
            rows[0:1, 0:TLOC], rows[0:1, 0:TLOC], cells[0:1, 1:2], sc_cells[0:1, 1:2],
            AL.mult, AL.mult))
        p.barrier("vector")
        for g in range(NTILE):
            def trb(e, g=g):
                e.transpose(ps_col[:, g:g + 1], rows[0:1, g * 128:(g + 1) * 128],
                            ident_f[0:1, 0:1])
            p.on("tensor", trb)
        p.barrier("tensor")
        p.on("vector", lambda e: e.tensor_copy(negB[:, :], ps_col[:, 0:8]))
        p.on("vector", lambda e: e.memset(ls[:, :], 0.0))
        p.barrier("vector")
        for g in range(NTILE):
            for blk in range(g + 1):
                def sc_mm(e, h=h, g=g, blk=blk):
                    e.matmul(ps_s[:, :], q_sb[:, h * TLOC + g * 128: h * TLOC + g * 128 + 128],
                             kh[:, blk * 512:(blk + 1) * 512], start=True, stop=True)
                p.on("tensor", sc_mm)
                p.barrier("tensor")
                if blk == g:
                    p.on("vector", lambda e, g=g: e.tensor_tensor(
                        ps_s[:, :], ps_s[:, :], mask_sb[:, g * 512:(g + 1) * 512], AL.add))
                    p.barrier("vector")
                def expp(e, g=g, blk=blk):
                    e.activation(p_bf[:, :], ps_s[:, :], AF.Exp, scale=sc_col[:, 0:1],
                                 bias=negB[:, g:g + 1], accum_out=ls[:, g * 8 + blk: g * 8 + blk + 1])
                p.on("scalar", expp)
                p.barrier("scalar")
                def trp(e):
                    for j in range(4):
                        e.transpose(pt_ps[:, j * 128:(j + 1) * 128],
                                    p_bf[:, j * 128:(j + 1) * 128], consts["ident_bf"][:, :])
                p.on("tensor", trp)
                p.barrier("tensor")
                p.on("vector", lambda e: e.tensor_copy(pt_bf[:, :], pt_ps[:, :]))
                p.barrier("vector")
                def pv(e, g=g, blk=blk):
                    for j in range(4):
                        kt = blk * 4 + j
                        e.matmul(out_ps[:, g * 128:(g + 1) * 128],
                                 vth[:, kt * 128:(kt + 1) * 128],
                                 pt_bf[:, j * 128:(j + 1) * 128],
                                 start=(blk == 0 and j == 0), stop=(blk == g and j == 3))
                p.on("tensor", pv)
                p.barrier("tensor")
        p.on("vector", lambda e: e.tensor_reduce(
            lred[:, 0:8], ls[:, :].rearrange("p (g b) -> p g b", g=8),
            mybir.AxisListType.X, AL.add))
        p.barrier("vector")
        p.on("vector", lambda e: e.reciprocal(lred[:, 8:16], lred[:, 0:8]))
        p.barrier("vector")
        for g in range(NTILE):
            def tri(e, g=g):
                e.transpose(ps_row[0:1, 0:128], lred[:, 8 + g:9 + g], ident_f[:, :])
            p.on("tensor", tri)
            p.barrier("tensor")
            p.on("vector", lambda e, g=g: e.tensor_copy(invl_row[0:1, g * 128:(g + 1) * 128],
                                                        ps_row[0:1, 0:128]))
            p.barrier("vector")
        for g in range(NTILE):
            def bcm(e, g=g):
                e.matmul(ps_col[:, 0:128], consts["onesr_f"][0:1, :],
                         invl_row[0:1, g * 128:(g + 1) * 128], start=True, stop=True)
            p.on("tensor", bcm)
            p.barrier("tensor")
            p.on("vector", lambda e: e.tensor_copy(bcast_sb[:, :], ps_col[:, 0:128]))
            p.barrier("vector")
            p.on("vector", lambda e, h=h, g=g: e.tensor_tensor(
                attn_bf[:, h * TLOC + g * 128: h * TLOC + (g + 1) * 128],
                out_ps[:, g * 128:(g + 1) * 128], bcast_sb[:, :], AL.mult))
            p.barrier("vector")

    attn_psum_stack.close()
    x_sb = p.sb("x_sb", [128, NCH * TLOC], F32)
    bo_sb = p.sb("bo_sb", [1, D], BF16)
    ones512 = p.sb("ones512", [1, 512], BF16)
    p.load(x_sb[:, :], x_d[:, :, :])
    p.load(bo_sb[0:1, :], bo_d[0:1, :])
    p.on("vector", lambda e: e.memset(ones512[0:1, :], 1.0))
    p.barrier("gpsimd", "vector")
    wbuf = p.sb("wbuf", [128, D], BF16)
    ps_p = [p.ps(f"ps_p{i}", [128, 512]) for i in range(2)]
    rank1 = (lambda oc: bo_sb[0:1, oc * 128:(oc + 1) * 128],
             lambda t: ones512[0:1, :])
    def out_cb(oc):
        def f(e, oc=oc):
            for t in range(2):
                sl = slice(oc * TLOC + t * 512, oc * TLOC + t * 512 + 512)
                e.tensor_tensor(x_sb[:, sl], ps_p[t][:, :], x_sb[:, sl], AL.add)
        p.on("vector", f)
        p.barrier("vector")
        p.store(xo_d[:, oc, :], x_sb[:, oc * TLOC:(oc + 1) * TLOC])
    emit_proj_stream(p, wo_d, NCH, attn_bf, wbuf, ps_p, out_cb, rank1=rank1)
    return p.finalize()

def build_l0_ffn():
    NI = 5632 // 128
    p = Prog("l0_ffn")
    x_d = p.dram("x", [128, NCH, TLOC], F32, "ExternalInput")
    gw_d = p.dram("gate_w", [NI, 128, D], F32, "ExternalInput")
    uw_d = p.dram("up_w", [NI, 128, D], F32, "ExternalInput")
    dw_d = p.dram("down_w", [NCH, 128, 5632], F32, "ExternalInput")
    gs_d = p.dram("gate_s", [128, NI], F32, "ExternalInput")
    us_d = p.dram("up_s", [128, NI], F32, "ExternalInput")
    ds_d = p.dram("down_s", [128, NCH], F32, "ExternalInput")
    xo_d = p.dram("xo", [128, NCH, TLOC], F32, "ExternalOutput")

    consts = alloc_consts(p)
    x_sb = p.sb("x_sb", [128, NCH * TLOC], F32)
    r_b = p.sb("r_b", [128, TLOC], F32)
    invr = p.sb("invr", [1, TLOC], F32)
    gs_sb = p.sb("gs_sb", [128, NI], F32)
    us_sb = p.sb("us_sb", [128, NI], F32)
    ds_sb = p.sb("ds_sb", [128, NCH], F32)
    p.load(gs_sb[:, :], gs_d[:, :])
    p.load(us_sb[:, :], us_d[:, :])
    p.load(ds_sb[:, :], ds_d[:, :])

    xc_bf = p.sb("xc", [128, NCH * TLOC], BF16)
    h_sb = p.sb("h_sb", [128, NI * TLOC], BF16)
    emit_ln(p, consts, x_d, x_sb, xc_bf, r_b, invr)

    sk1 = ExitStack()
    gbuf = p.sb("gbuf", [128, D], BF16, sk1)
    ubuf = p.sb("ubuf", [128, D], BF16, sk1)
    ps_g = [p.ps(f"ps_g{i}", [128, 512], F32, sk1) for i in range(2)]
    ps_u = [p.ps(f"ps_u{i}", [128, 512], F32, sk1) for i in range(2)]
    g_bf = p.sb("g_bf", [128, TLOC], BF16, sk1)
    s_bf = p.sb("s_bf", [128, TLOC], BF16, sk1)
    u_bf = p.sb("u_bf", [128, TLOC], BF16, sk1)

    for oc in range(NI):
        p.load(gbuf[:, :], gw_d[oc, :, :])
        p.load(ubuf[:, :], uw_d[oc, :, :])
        p.barrier("gpsimd")
        def mms(e, oc=oc):
            for t in range(2):
                for cch in range(NCH):
                    rhs = xc_bf[:, cch * TLOC + t * 512: cch * TLOC + t * 512 + 512]
                    e.matmul(ps_g[t][:, :], gbuf[:, cch * 128:(cch + 1) * 128], rhs,
                             start=(cch == 0), stop=(cch == NCH - 1))
                    e.matmul(ps_u[t][:, :], ubuf[:, cch * 128:(cch + 1) * 128], rhs,
                             start=(cch == 0), stop=(cch == NCH - 1))
        p.on("tensor", mms)
        p.barrier("tensor")
        def scg(e, oc=oc):
            for t in range(2):
                sl = slice(t * 512, (t + 1) * 512)
                e.scalar_tensor_tensor(g_bf[:, sl], ps_g[t][:, :], gs_sb[:, oc:oc + 1],
                                       r_b[:, sl], AL.mult, AL.mult)
                e.scalar_tensor_tensor(u_bf[:, sl], ps_u[t][:, :], us_sb[:, oc:oc + 1],
                                       r_b[:, sl], AL.mult, AL.mult)
        p.on("vector", scg)
        p.barrier("vector")
        p.on("scalar", lambda e: e.activation(s_bf[:, :], g_bf[:, :], AF.Silu))
        p.barrier("scalar")
        p.on("vector", lambda e, oc=oc: e.tensor_tensor(
            h_sb[:, oc * TLOC:(oc + 1) * TLOC], s_bf[:, :], u_bf[:, :], AL.mult))
        p.barrier("vector")
    sk1.close()

    dbuf = p.sb("dbuf", [128, 5632], BF16)
    ps_p = [p.ps(f"ps_d{i}", [128, 512]) for i in range(2)]
    def out_cb(oc):
        def f(e, oc=oc):
            for t in range(2):
                sl = slice(oc * TLOC + t * 512, oc * TLOC + t * 512 + 512)
                e.scalar_tensor_tensor(x_sb[:, sl], ps_p[t][:, :], ds_sb[:, oc:oc + 1],
                                       x_sb[:, sl], AL.mult, AL.add)
        p.on("vector", f)
        p.barrier("vector")
        p.store(xo_d[:, oc, :], x_sb[:, oc * TLOC:(oc + 1) * TLOC])
    emit_proj_stream(p, dw_d, NCH, h_sb, dbuf, ps_p, out_cb, nch=44)
    return p.finalize()

def build_rwkv_mix():
    p = Prog("rwkv_mix")
    x_d = p.dram("x", [128, NCH, TLOC], F32, "ExternalInput")
    xh_d = p.dram("xhalo", [128, NCH, NTILE], F32, "ExternalInput")
    tm_d = p.dram("tmix", [128, 3 * NCH], F32, "ExternalInput")
    outs = [p.dram(n, [128, NCH, TLOC], BF16, "ExternalOutput") for n in ("xk", "xv", "xr")]

    consts = alloc_consts(p)
    x_sb = p.sb("x_sb", [128, NCH * TLOC], F32)
    xc_bf = p.sb("xc", [128, NCH * TLOC], BF16)
    r_b = p.sb("r_b", [128, TLOC], F32)
    invr = p.sb("invr", [1, TLOC], F32)
    xln = p.sb("xln", [128, NCH * TLOC], BF16)
    xs = p.sb("xs", [128, NCH * TLOC], BF16)
    tm_sb = p.sb("tm_sb", [128, 3 * NCH], F32)
    omt_sb = p.sb("omt_sb", [128, 3 * NCH], F32)
    scr = p.sb("scr", [128, TLOC], BF16)
    p.load(tm_sb[:, :], tm_d[:, :])
    emit_ln(p, consts, x_d, x_sb, xc_bf, r_b, invr)
    def mkxln(e):
        for cch in range(NCH):
            sl = slice(cch * TLOC, (cch + 1) * TLOC)
            e.tensor_tensor(xln[:, sl], xc_bf[:, sl], r_b[:, :], AL.mult)
    p.on("vector", mkxln)
    p.on("vector", lambda e: e.tensor_scalar(omt_sb[:, :], tm_sb[:, :], -1.0, 1.0,
                                             AL.mult, AL.add))
    p.barrier("vector")
    with ExitStack() as sk:
        xh_sb = p.sb("xh_sb", [128, NCH * NTILE], F32, sk)
        xch_bf = p.sb("xch", [128, NCH * NTILE], BF16, sk)
        rh_b = p.sb("rh_b", [128, NTILE], F32, sk)
        invrh = p.sb("invrh", [1, NTILE], F32, sk)
        emit_ln(p, consts, xh_d, xh_sb, xch_bf, rh_b, invrh, T=NTILE)
        def mkxlnh(e):
            for cch in range(NCH):
                sl = slice(cch * NTILE, (cch + 1) * NTILE)
                e.tensor_tensor(xch_bf[:, sl], xch_bf[:, sl], rh_b[:, :], AL.mult)
        p.on("vector", mkxlnh)
        p.barrier("vector")
        def shift(e):
            for cch in range(NCH):
                for g in range(NTILE):
                    o = cch * TLOC + g * 128
                    e.tensor_copy(xs[:, o + 1:o + 128], xln[:, o:o + 127])
                    e.tensor_copy(xs[:, o:o + 1], xch_bf[:, cch * NTILE + g:cch * NTILE + g + 1])
        p.on("vector", shift)
        p.barrier("vector")
    for w in range(3):
        for cch in range(NCH):
            sl = slice(cch * TLOC, (cch + 1) * TLOC)
            def mix(e, w=w, cch=cch, sl=sl):
                e.tensor_scalar(scr[:, :], xs[:, sl], omt_sb[:, w * NCH + cch: w * NCH + cch + 1],
                                None, AL.mult)
                e.drain()
                e.scalar_tensor_tensor(scr[:, :], xln[:, sl],
                                       tm_sb[:, w * NCH + cch: w * NCH + cch + 1],
                                       scr[:, :], AL.mult, AL.add)
            p.on("vector", mix)
            p.barrier("vector")
            p.store(outs[w][:, cch, :], scr[:, :])
    return p.finalize()

def build_rwkv_core():
    BT = B * S
    p = Prog("rwkv_core")
    xk_d = p.dram("xk", [128, NCH, BT], BF16, "ExternalInput")
    xv_d = p.dram("xv", [128, NCH, BT], BF16, "ExternalInput")
    xr_d = p.dram("xr", [128, NCH, BT], BF16, "ExternalInput")
    wk_d = p.dram("wk", [2, 128, D], F32, "ExternalInput")
    wv_d = p.dram("wv", [2, 128, D], F32, "ExternalInput")
    wr_d = p.dram("wr", [2, 128, D], F32, "ExternalInput")
    td_d = p.dram("td", [128, 2], F32, "ExternalInput")
    tf_d = p.dram("tf", [128, 2], F32, "ExternalInput")
    g_o = p.dram("g_out", [128, 2, BT], BF16, "ExternalOutput")

    wk_sb = p.sb("wk_sb", [128, 2 * D], BF16)
    wv_sb = p.sb("wv_sb", [128, 2 * D], BF16)
    wr_sb = p.sb("wr_sb", [128, 2 * D], BF16)
    td_sb = p.sb("td_sb", [128, 2], F32)
    tf_sb = p.sb("tf_sb", [128, 2], F32)
    cells = p.sb("cells", [128, 8], F32)
    xbuf = p.sb("xbuf", [128, 3 * 512], BF16)
    k_sb = [p.sb(f"k_sb{i}", [128, S], F32) for i in range(2)]
    v_sb = [p.sb(f"v_sb{i}", [128, S], F32) for i in range(2)]
    sr_bf = [p.sb(f"sr_bf{i}", [128, S], BF16) for i in range(2)]
    dwt = p.sb("dwt", [128, S], F32)
    scan = p.sb("scan", [128, S + 4], F32)
    g_bf = p.sb("g_bf", [128, S], BF16)
    ps = [p.ps(f"ps_kvr{i}", [128, 512]) for i in range(6)]

    for i, wd, ws in ((0, wk_d, wk_sb), (1, wv_d, wv_sb), (2, wr_d, wr_sb)):
        for cg in range(2):
            p.load(ws[:, cg * D:(cg + 1) * D], wd[cg, :, :])
    p.load(td_sb[:, :], td_d[:, :])
    p.load(tf_sb[:, :], tf_d[:, :])
    p.barrier("gpsimd")

    for b in range(B):
        for tg in range(8):
            t0 = b * S + tg * 512
            p.load(xbuf[:, 0:512], xk_d[:, 0, t0:t0 + 512])
            for cch in range(NCH):
                if cch > 0:
                    p.load(xbuf[:, 0:512], xk_d[:, cch, t0:t0 + 512])
                p.load(xbuf[:, 512:1024], xv_d[:, cch, t0:t0 + 512])
                p.load(xbuf[:, 1024:1536], xr_d[:, cch, t0:t0 + 512])
                p.barrier("gpsimd")
                def mms(e, cch=cch):
                    for cg in range(2):
                        for w, ws in ((0, wk_sb), (1, wv_sb), (2, wr_sb)):
                            e.matmul(ps[cg * 3 + w][:, :],
                                     ws[:, cg * D + cch * 128: cg * D + (cch + 1) * 128],
                                     xbuf[:, w * 512:(w + 1) * 512],
                                     start=(cch == 0), stop=(cch == NCH - 1))
                p.on("tensor", mms)
                p.barrier("tensor")
            sl = slice(tg * 512, (tg + 1) * 512)
            def evac(e, sl=sl):
                for cg in range(2):
                    e.tensor_copy(k_sb[cg][:, sl], ps[cg * 3 + 0][:, :])
                    e.tensor_copy(v_sb[cg][:, sl], ps[cg * 3 + 1][:, :])
            p.on("vector", evac)
            def sig(e, sl=sl):
                for cg in range(2):
                    e.activation(sr_bf[cg][:, sl], ps[cg * 3 + 2][:, :], AF.Sigmoid)
            p.on("scalar", sig)
            p.barrier("vector", "scalar")
        for cg in range(2):
            p.on("vector", lambda e, cg=cg: e.tensor_reduce(
                cells[:, cg:cg + 1], k_sb[cg][:, :], mybir.AxisListType.X, AL.max,
                negate=True))
            p.barrier("vector")
            p.on("scalar", lambda e, cg=cg: e.activation(cells[:, 2 + cg:3 + cg],
                                                         td_sb[:, cg:cg + 1], AF.Exp))
            p.barrier("scalar")
            p.on("scalar", lambda e, cg=cg: e.activation(cells[:, 4 + cg:5 + cg],
                                                         cells[:, 2 + cg:3 + cg], AF.Exp,
                                                         scale=-1.0))
            p.on("scalar", lambda e, cg=cg: e.activation(cells[:, 6 + cg:7 + cg],
                                                         tf_sb[:, cg:cg + 1], AF.Exp))
            p.barrier("scalar")
            p.on("scalar", lambda e, cg=cg: e.activation(k_sb[cg][:, :], k_sb[cg][:, :],
                                                         AF.Exp, bias=cells[:, cg:cg + 1]))
            p.barrier("scalar")
            p.on("vector", lambda e, cg=cg: e.tensor_tensor(v_sb[cg][:, :], k_sb[cg][:, :],
                                                            v_sb[cg][:, :], AL.mult))
            p.on("vector", lambda e, cg=cg: e.tensor_scalar(dwt[:, :], k_sb[cg][:, :], 0.0,
                                                            cells[:, 4 + cg:5 + cg],
                                                            AL.mult, AL.add))
            p.on("vector", lambda e: e.memset(scan[:, 0:1], 0.0))
            p.barrier("vector")
            p.on("vector", lambda e, cg=cg: e.tensor_tensor_scan(
                scan[:, 1:S + 1], dwt[:, :], v_sb[cg][:, :], 0.0, AL.mult, AL.add))
            p.barrier("vector")
            p.on("vector", lambda e, cg=cg: e.scalar_tensor_tensor(
                v_sb[cg][:, :], v_sb[cg][:, :], cells[:, 6 + cg:7 + cg], scan[:, 0:S],
                AL.mult, AL.add))
            p.barrier("vector")
            p.on("vector", lambda e, cg=cg: e.tensor_tensor_scan(
                scan[:, 1:S + 1], dwt[:, :], k_sb[cg][:, :], 0.0, AL.mult, AL.add))
            p.barrier("vector")
            p.on("vector", lambda e, cg=cg: e.scalar_tensor_tensor(
                k_sb[cg][:, :], k_sb[cg][:, :], cells[:, 6 + cg:7 + cg], scan[:, 0:S],
                AL.mult, AL.add))
            p.barrier("vector")
            p.on("vector", lambda e, cg=cg: e.reciprocal(k_sb[cg][:, :], k_sb[cg][:, :]))
            p.barrier("vector")
            p.on("vector", lambda e, cg=cg: e.tensor_tensor(v_sb[cg][:, :], v_sb[cg][:, :],
                                                            k_sb[cg][:, :], AL.mult))
            p.barrier("vector")
            p.on("vector", lambda e, cg=cg: e.tensor_tensor(g_bf[:, :], v_sb[cg][:, :],
                                                            sr_bf[cg][:, :], AL.mult))
            p.barrier("vector")
            p.store(g_o[:, cg, b * S:(b + 1) * S], g_bf[:, :])
    return p.finalize()

def build_rwkv_out():
    p = Prog("rwkv_out")
    g_d = p.dram("g", [128, NCH, TLOC], BF16, "ExternalInput")
    x_d = p.dram("x", [128, NCH, TLOC], F32, "ExternalInput")
    wo_d = p.dram("wo", [NCH, 128, D], F32, "ExternalInput")
    xo_d = p.dram("xo", [128, NCH, TLOC], F32, "ExternalOutput")
    consts = alloc_consts(p)
    g_sb = p.sb("g_sb", [128, NCH * TLOC], BF16)
    x_sb = p.sb("x_sb", [128, NCH * TLOC], F32)
    p.load(g_sb[:, :], g_d[:, :, :])
    p.load(x_sb[:, :], x_d[:, :, :])
    p.barrier("gpsimd")
    wbuf = p.sb("wbuf", [128, D], BF16)
    ps_p = [p.ps(f"ps_p{i}", [128, 512]) for i in range(2)]
    def out_cb(oc):
        def f(e, oc=oc):
            for t in range(2):
                sl = slice(oc * TLOC + t * 512, oc * TLOC + t * 512 + 512)
                e.tensor_tensor(x_sb[:, sl], ps_p[t][:, :], x_sb[:, sl], AL.add)
        p.on("vector", f)
        p.barrier("vector")
        p.store(xo_d[:, oc, :], x_sb[:, oc * TLOC:(oc + 1) * TLOC])
    emit_proj_stream(p, wo_d, NCH, g_sb, wbuf, ps_p, out_cb)
    return p.finalize()

def build_ternarize():
    NFC, NPJ = 40, 16
    SFC, SPJ = NFC // 8, NPJ // 8
    p = Prog("ternarize")
    wfc_d = p.dram("wfc", [NFC, 128, D], F32, "ExternalInput")
    wpj_d = p.dram("wpj", [NPJ, 128, 5120], F32, "ExternalInput")
    wfc_sh = p.dram("wfc_sh", [SFC, 128, D], F32, "ExternalInput")
    wpj_sh = p.dram("wpj_sh", [SPJ, 128, 5120], F32, "ExternalInput")
    qfc_o = p.dram("qfc", [SFC, 128, D], BF16, "ExternalOutput")
    qpj_o = p.dram("qpj", [SPJ, 128, 5120], BF16, "ExternalOutput")
    am_o = p.dram("am", [1, 2], F32, "ExternalOutput")

    consts = alloc_consts(p)
    wbuf = p.sb("wbuf", [128, 5120], F32)
    scr = p.sb("scr", [128, 5120], BF16)
    t2 = p.sb("t2", [128, 5120], BF16)
    qb = p.sb("qb", [128, 5120], BF16)
    parts = p.sb("parts", [128, NFC + NPJ], F32)
    cells = p.sb("cells", [1, 16], F32)
    cols = p.sb("cols", [128, 4], F32)
    ps_row = p.ps("ps_row", [1, 512])
    ps_col = p.ps("ps_col", [128, 8])

    for wi, (wd, nb, wid) in enumerate(((wfc_d, NFC, D), (wpj_d, NPJ, 5120))):
        for b in range(nb):
            p.load(wbuf[:, 0:wid], wd[b, :, :])
            p.barrier("gpsimd")
            off = b if wi == 0 else NFC + b
            p.on("scalar", lambda e, off=off, wid=wid: e.activation(
                scr[:, 0:wid], wbuf[:, 0:wid], AF.Abs,
                accum_out=parts[:, off:off + 1]))
            p.barrier("scalar")
    def tot(e):
        e.matmul(ps_row[0:1, 0:NFC + NPJ], consts["onesc_f"][:, 0:1],
                 parts[:, 0:NFC + NPJ], start=True, stop=True)
    p.on("tensor", tot)
    p.barrier("tensor")
    p.on("vector", lambda e: e.tensor_reduce(cells[0:1, 0:1], ps_row[0:1, 0:NFC],
                                             mybir.AxisListType.X, AL.add))
    p.on("vector", lambda e: e.tensor_reduce(cells[0:1, 1:2], ps_row[0:1, NFC:NFC + NPJ],
                                             mybir.AxisListType.X, AL.add))
    p.barrier("vector")
    def amop(e):
        e.tensor_scalar(cells[0:1, 0:1], cells[0:1, 0:1], 1.0 / (5120.0 * D), 1e-5,
                        AL.mult, AL.max)
        e.tensor_scalar(cells[0:1, 1:2], cells[0:1, 1:2], 1.0 / (5120.0 * D), 1e-5,
                        AL.mult, AL.max)
        e.drain()
        e.tensor_scalar(cells[0:1, 2:3], cells[0:1, 0:1], 0.7, None, AL.mult)
        e.tensor_scalar(cells[0:1, 3:4], cells[0:1, 1:2], 0.7, None, AL.mult)
        e.drain()
        e.tensor_scalar(cells[0:1, 4:5], cells[0:1, 2:3], -1.0, None, AL.mult)
        e.tensor_scalar(cells[0:1, 5:6], cells[0:1, 3:4], -1.0, None, AL.mult)
    p.on("vector", amop)
    p.barrier("vector")
    p.store(am_o[0:1, :], cells[0:1, 0:2])
    def bc(e):
        for j in range(4):
            e.matmul(ps_col[:, j:j + 1], consts["onesr_f"][0:1, :], cells[0:1, 2 + j:3 + j],
                     start=True, stop=True)
    p.on("tensor", bc)
    p.barrier("tensor")
    p.on("vector", lambda e: e.tensor_copy(cols[:, :], ps_col[:, 0:4]))
    p.barrier("vector")
    for wi, (wd, qo, nb, wid) in enumerate(((wfc_sh, qfc_o, SFC, D),
                                            (wpj_sh, qpj_o, SPJ, 5120))):
        thr = cols[:, wi:wi + 1]
        nthr = cols[:, 2 + wi:3 + wi]
        for b in range(nb):
            p.load(wbuf[:, 0:wid], wd[b, :, :])
            p.barrier("gpsimd")
            def tern(e, wid=wid, thr=thr, nthr=nthr):
                e.tensor_scalar(t2[:, 0:wid], wbuf[:, 0:wid], nthr, None, AL.is_lt)
                e.drain()
                e.scalar_tensor_tensor(qb[:, 0:wid], wbuf[:, 0:wid], thr, t2[:, 0:wid],
                                       AL.is_gt, AL.subtract)
            p.on("vector", tern)
            p.barrier("vector")
            p.store(qo[b, :, :], qb[:, 0:wid])
    return p.finalize()

def build_l1_ffn():
    NI = 5120 // 128
    p = Prog("l1_ffn")
    x_d = p.dram("x", [128, NCH, TLOC], F32, "ExternalInput")
    qfc_d = p.dram("qfc", [NI, 128, D], BF16, "ExternalInput")
    qpj_d = p.dram("qpj", [NCH, 128, 5120], BF16, "ExternalInput")
    am_d = p.dram("am", [1, 2], F32, "ExternalInput")
    xo_d = p.dram("xo", [128, NCH, TLOC], F32, "ExternalOutput")

    consts = alloc_consts(p)
    x_sb = p.sb("x_sb", [128, NCH * TLOC], F32)
    r_b = p.sb("r_b", [128, TLOC], F32)
    invr = p.sb("invr", [1, TLOC], F32)
    cells = p.sb("cells", [1, 4], F32)
    r2s = p.sb("r2s", [128, TLOC], F32)
    xc_bf = p.sb("xc", [128, NCH * TLOC], BF16)
    h_sb = p.sb("h_sb", [128, NI * TLOC], BF16)
    emit_ln(p, consts, x_d, x_sb, xc_bf, r_b, invr)

    am_sb = cells[0:1, 0:2]
    p.load(am_sb, am_d[0:1, :])
    p.barrier("gpsimd")
    ps_c = p.ps("ps_c", [128, 8])
    def sstar(e):
        e.tensor_tensor(cells[0:1, 2:3], cells[0:1, 0:1], cells[0:1, 0:1], AL.mult)
        e.drain()
        e.tensor_tensor(cells[0:1, 2:3], cells[0:1, 2:3], cells[0:1, 1:2], AL.mult)
    p.on("vector", sstar)
    p.barrier("vector")
    p.on("tensor", lambda e: e.matmul(ps_c[:, 0:1], consts["onesr_f"][0:1, :],
                                      cells[0:1, 2:3], start=True, stop=True))
    p.barrier("tensor")
    def r2(e):
        e.tensor_tensor(r2s[:, :], r_b[:, :], r_b[:, :], AL.mult)
        e.drain()
        e.tensor_scalar(r2s[:, :], r2s[:, :], ps_c[:, 0:1], None, AL.mult)
    p.on("vector", r2)
    p.barrier("vector")

    sk1 = ExitStack()
    fbuf = p.sb("fbuf", [128, D], BF16, sk1)
    ps_z = [p.ps(f"ps_z{i}", [128, 512], F32, sk1) for i in range(2)]
    rl_bf = p.sb("rl_bf", [128, TLOC], BF16, sk1)
    for oc in range(NI):
        p.load(fbuf[:, :], qfc_d[oc, :, :])
        p.barrier("gpsimd")
        def mms(e, oc=oc):
            for t in range(2):
                for cch in range(NCH):
                    e.matmul(ps_z[t][:, :], fbuf[:, cch * 128:(cch + 1) * 128],
                             xc_bf[:, cch * TLOC + t * 512: cch * TLOC + t * 512 + 512],
                             start=(cch == 0), stop=(cch == NCH - 1))
        p.on("tensor", mms)
        p.barrier("tensor")
        def relu(e):
            for t in range(2):
                e.activation(rl_bf[:, t * 512:(t + 1) * 512], ps_z[t][:, :], AF.Relu)
        p.on("scalar", relu)
        p.barrier("scalar")
        p.on("vector", lambda e, oc=oc: e.tensor_tensor(
            h_sb[:, oc * TLOC:(oc + 1) * TLOC], rl_bf[:, :], rl_bf[:, :], AL.mult))
        p.barrier("vector")
    sk1.close()

    pbuf = p.sb("pbuf", [128, 5120], BF16)
    scr = p.sb("oscr", [128, TLOC], F32)
    ps_p = [p.ps(f"ps_o{i}", [128, 512]) for i in range(2)]
    def out_cb(oc):
        def f(e, oc=oc):
            for t in range(2):
                sl = slice(oc * TLOC + t * 512, oc * TLOC + t * 512 + 512)
                tsl = slice(t * 512, (t + 1) * 512)
                e.tensor_tensor(scr[:, tsl], ps_p[t][:, :], r2s[:, tsl], AL.mult)
                e.drain()
                e.tensor_tensor(x_sb[:, sl], scr[:, tsl], x_sb[:, sl], AL.add)
        p.on("vector", f)
        p.barrier("vector")
        p.store(xo_d[:, oc, :], x_sb[:, oc * TLOC:(oc + 1) * TLOC])
    emit_proj_stream(p, qpj_d, NCH, h_sb, pbuf, ps_p, out_cb, nch=NI)
    return p.finalize()

def build_l1b_qkv():
    p = Prog("l1b_qkv")
    x_d = p.dram("x", [128, NCH, TLOC], F32, "ExternalInput")
    wq_d = p.dram("wq", [NCH, 128, D], F32, "ExternalInput")
    wk_d = p.dram("wk", [4, 128, D], F32, "ExternalInput")
    wv_d = p.dram("wv", [4, 128, D], F32, "ExternalInput")
    q_o = p.dram("q_out", [128, NCH, TLOC], BF16, "ExternalOutput")
    k_o = p.dram("k_out", [128, 4, TLOC], BF16, "ExternalOutput")
    vt_o = p.dram("vt_out", [4, TLOC, 128], BF16, "ExternalOutput")

    consts = alloc_consts(p)
    x_sb = p.sb("x_sb", [128, NCH * TLOC], F32)
    xc_bf = p.sb("xc", [128, NCH * TLOC], BF16)
    r_b = p.sb("r_b", [128, TLOC], F32)
    invr = p.sb("invr", [1, TLOC], F32)
    emit_ln(p, consts, x_d, x_sb, xc_bf, r_b, invr)

    wbuf = p.sb("wbuf", [128, D], BF16)
    ps_p = [p.ps(f"ps_p{i}", [128, 512]) for i in range(2)]
    vt_ps = p.ps("vt_ps", [128, 1024], BF16)
    qn_bf = p.sb("qn", [128, TLOC], BF16)
    ro_bf = p.sb("ro", [128, TLOC], BF16)

    def evac(oc):
        def f(e):
            for t in range(2):
                sl = slice(t * 512, (t + 1) * 512)
                e.tensor_tensor(qn_bf[:, sl], ps_p[t][:, :], r_b[:, sl], AL.mult)
        p.on("vector", f)
        p.barrier("vector")

    def q_cb(oc):
        evac(oc)
        p.store(q_o[:, oc, :], qn_bf[:, :])

    def k_cb(oc):
        evac(oc)
        p.store(k_o[:, oc, :], qn_bf[:, :])

    def v_cb(oc):
        evac(oc)
        def vtr(e):
            for tc in range(8):
                e.transpose(vt_ps[:, tc * 128:(tc + 1) * 128],
                            qn_bf[:, tc * 128:(tc + 1) * 128], consts["ident_bf"][:, :])
        p.on("tensor", vtr)
        p.barrier("tensor")
        p.on("vector", lambda e: e.tensor_copy(ro_bf[:, :], vt_ps[:, :]))
        p.barrier("vector")
        p.store(vt_o[oc, :, :].rearrange("(tc pt) d -> pt tc d", pt=128), ro_bf[:, :])

    emit_proj_stream(p, wq_d, NCH, xc_bf, wbuf, ps_p, q_cb)
    emit_proj_stream(p, wk_d, 4, xc_bf, wbuf, ps_p, k_cb)
    emit_proj_stream(p, wv_d, 4, xc_bf, wbuf, ps_p, v_cb)
    return p.finalize()

def build_l1b_attn():
    SC = 0.125
    p = Prog("l1b_attn")
    q_d = p.dram("q", [128, NCH, TLOC], BF16, "ExternalInput")
    k_d = p.dram("kfull", [128, 4, S], BF16, "ExternalInput")
    vt_d = p.dram("vtfull", [4, S, 128], BF16, "ExternalInput")
    x_d = p.dram("x", [128, NCH, TLOC], F32, "ExternalInput")
    mem_d = p.dram("mem", [4, 128, 512], F32, "ExternalInput")
    mn_d = p.dram("mnorm", [128, 4], F32, "ExternalInput")
    mw_d = p.dram("mweight", [1, 1], F32, "ExternalInput")
    ident_d = p.dram("ident_f32", [128, 128], F32, "ExternalInput")
    xo_d = p.dram("xo", [128, NCH, TLOC], F32, "ExternalOutput")

    consts = alloc_consts(p)
    ident_f = p.sb("ident_f", [128, 128], F32)
    p.load(ident_f[:, :], ident_d[:, :])
    q_sb = p.sb("q_sb", [128, NCH * TLOC], BF16)
    attn_bf = p.sb("attn_bf", [128, NCH * TLOC], BF16)
    amem_bf = p.sb("amem_bf", [128, NCH * TLOC], BF16)
    cells = p.sb("cells", [1, 16], F32)
    gcol = p.sb("gcol", [128, 2], F32)
    p.load(q_sb[:, :], q_d[:, :, :])
    p.load(cells[0:1, 0:1], mw_d[0:1, :])
    p.barrier("gpsimd")

    with ExitStack() as skg:
        ps_c = p.ps("ps_c", [128, 8], F32, skg)
        p.on("scalar", lambda e: e.activation(cells[0:1, 1:2], cells[0:1, 0:1], AF.Sigmoid))
        p.barrier("scalar")
        p.on("vector", lambda e: e.tensor_scalar(cells[0:1, 2:3], cells[0:1, 1:2], -1.0, 1.0,
                                                 AL.mult, AL.add))
        p.barrier("vector")
        def gbc(e):
            e.matmul(ps_c[:, 0:1], consts["onesr_f"][0:1, :], cells[0:1, 1:2], start=True, stop=True)
            e.matmul(ps_c[:, 1:2], consts["onesr_f"][0:1, :], cells[0:1, 2:3], start=True, stop=True)
        p.on("tensor", gbc)
        p.barrier("tensor")
        p.on("vector", lambda e: e.tensor_copy(gcol[:, :], ps_c[:, 0:2]))
        p.barrier("vector")

    with ExitStack() as sk:
        sig_bf = p.sb("sig_bf", [128, NCH * TLOC], BF16, sk)
        scr1 = p.sb("m_scr1", [128, TLOC], BF16, sk)
        scr2 = p.sb("m_scr2", [128, TLOC], BF16, sk)
        membuf = p.sb("membuf", [128, 4 * 512], BF16, sk)
        mn_bf = p.sb("mn_bf", [128, 4], BF16, sk)
        den_row = p.sb("den_row", [1, TLOC], F32, sk)
        bc_sb = p.sb("m_bc", [128, TLOC], F32, sk)
        ps_m = [p.ps(f"ps_m{i}", [128, 512], F32, sk) for i in range(2)]
        ps_d = p.ps("ps_d", [1, TLOC], F32, sk)
        ps_bc = p.ps("ps_bc", [128, TLOC], F32, sk)
        for ob in range(4):
            p.load(membuf[:, ob * 512:(ob + 1) * 512], mem_d[ob, :, :])
        p.load(mn_bf[:, :], mn_d[:, :])
        p.barrier("gpsimd")
        for oc in range(NCH):
            qsl = q_sb[:, oc * TLOC:(oc + 1) * TLOC]
            def elu(e, qsl=qsl):
                e.activation(scr1[:, :], qsl, AF.Relu)
                e.drain()
                e.activation(scr2[:, :], qsl, AF.Relu, scale=-1.0)
                e.drain()
                e.activation(scr2[:, :], scr2[:, :], AF.Exp, scale=-1.0)
            p.on("scalar", elu)
            p.barrier("scalar")
            p.on("vector", lambda e, oc=oc: e.scalar_tensor_tensor(
                sig_bf[:, oc * TLOC:(oc + 1) * TLOC], scr1[:, :], -1.0, scr2[:, :],
                AL.add, AL.add))
            p.barrier("vector")
        for qg in range(4):
            def denm(e, qg=qg):
                for t in range(2):
                    for ic in range(4):
                        e.matmul(ps_d[0:1, t * 512:(t + 1) * 512], mn_bf[:, ic:ic + 1],
                                 sig_bf[:, (qg * 4 + ic) * TLOC + t * 512:
                                        (qg * 4 + ic) * TLOC + t * 512 + 512],
                                 start=(ic == 0), stop=(ic == 3))
            p.on("tensor", denm)
            p.barrier("tensor")
            p.on("vector", lambda e: e.reciprocal(den_row[0:1, :], ps_d[0:1, :]))
            p.barrier("vector")
            def dbc(e):
                for t in range(2):
                    e.matmul(ps_bc[:, t * 512:(t + 1) * 512], consts["onesr_f"][0:1, :],
                             den_row[0:1, t * 512:(t + 1) * 512], start=True, stop=True)
            p.on("tensor", dbc)
            p.barrier("tensor")
            p.on("vector", lambda e: e.tensor_copy(bc_sb[:, :], ps_bc[:, :]))
            p.barrier("vector")
            for ob in range(4):
                def amm(e, qg=qg, ob=ob):
                    for t in range(2):
                        for ic in range(4):
                            e.matmul(ps_m[t][:, :], membuf[:, ob * 512 + ic * 128:
                                                           ob * 512 + (ic + 1) * 128],
                                     sig_bf[:, (qg * 4 + ic) * TLOC + t * 512:
                                            (qg * 4 + ic) * TLOC + t * 512 + 512],
                                     start=(ic == 0), stop=(ic == 3))
                p.on("tensor", amm)
                p.barrier("tensor")
                def amn(e, qg=qg, ob=ob):
                    for t in range(2):
                        sl = slice((qg * 4 + ob) * TLOC + t * 512,
                                   (qg * 4 + ob) * TLOC + t * 512 + 512)
                        e.tensor_tensor(amem_bf[:, sl], ps_m[t][:, :],
                                        bc_sb[:, t * 512:(t + 1) * 512], AL.mult)
                p.on("vector", amn)
                p.barrier("vector")

    with ExitStack() as sk:
        k_sb = p.sb("k_sb", [128, 4 * S], BF16, sk)
        vt_sb = p.sb("vt_sb", [128, 4 * S], BF16, sk)
        sq_bf = p.sb("a_sq", [128, S], BF16, sk)
        rows = p.sb("a_rows", [1, S], F32, sk)
        negB = p.sb("negB", [128, 256], F32, sk)
        ls = p.sb("ls", [128, 128], F32, sk)
        invl = p.sb("invl", [128, 16], F32, sk)
        invl_row = p.sb("invl_row", [1, 256], F32, sk)
        bc2_sb = p.sb("bc2_sb", [128, 128], F32, sk)
        p_bf = [p.sb(f"p_bf{i}", [128, 512], BF16, sk) for i in range(2)]
        pt_bf = [p.sb(f"pt_bf{i}", [128, 512], BF16, sk) for i in range(2)]
        sc_col = p.sb("sc_col", [128, 1], F32, sk)
        ps_s = [p.ps(f"ps_s{i}", [128, 512], F32, sk) for i in range(2)]
        pt_ps = [p.ps(f"pt_ps{i}", [128, 512], BF16, sk) for i in range(2)]
        out_ps = p.ps("out_ps", [128, 1024], F32, sk)
        ps_row = p.ps("ps_row", [1, 512], F32, sk)
        ps_col = p.ps("ps_col", [128, 512], F32, sk)
        p.load(k_sb[:, :], k_d[:, :, :])
        for kc in range(4):
            p.load(vt_sb[:, kc * S:(kc + 1) * S],
                   vt_d[kc, :, :].rearrange("(tc pt) d -> pt tc d", pt=128))
        p.barrier("gpsimd")
        p.on("vector", lambda e: e.memset(sc_col[:, :], SC))
        p.barrier("vector")

        for kc in range(4):
            p.on("scalar", lambda e, kc=kc: e.activation(
                sq_bf[:, :], k_sb[:, kc * S:(kc + 1) * S], AF.Square))
            p.barrier("scalar")
            for half in range(2):
                base = half * 64
                for blk in range(8):
                    def knm(e, kc=kc, base=base, blk=blk):
                        e.matmul(ps_row[0:1, :], consts["onesc_bf"][base:base + 64, 0:1],
                                 sq_bf[base:base + 64, blk * 512:(blk + 1) * 512],
                                 start=True, stop=True)
                    p.on("tensor", knm)
                    p.barrier("tensor")
                    p.on("vector", lambda e, blk=blk: e.tensor_copy(
                        rows[0:1, blk * 512:(blk + 1) * 512], ps_row[0:1, :]))
                    p.barrier("vector")
                h = kc * 2 + half
                p.on("vector", lambda e, h=h: e.tensor_reduce(
                    cells[0:1, 8 + h:9 + h], rows[0:1, :], mybir.AxisListType.X, AL.max))
                p.barrier("vector")
        p.on("scalar", lambda e: e.activation(cells[0:1, 8:16], cells[0:1, 8:16], AF.Sqrt))
        p.barrier("scalar")
        for oc in range(NCH):
            qg, hp = oc // 4, oc % 4
            p.on("scalar", lambda e, oc=oc: e.activation(
                sq_bf[:, 0:TLOC], q_sb[:, oc * TLOC:(oc + 1) * TLOC], AF.Square))
            p.barrier("scalar")
            for half in range(2):
                base = half * 64
                h = hp * 2 + half
                vh = qg * 8 + h
                for t in range(2):
                    def qnm(e, base=base, t=t):
                        e.matmul(ps_row[0:1, :], consts["onesc_bf"][base:base + 64, 0:1],
                                 sq_bf[base:base + 64, t * 512:(t + 1) * 512],
                                 start=True, stop=True)
                    p.on("tensor", qnm)
                    p.barrier("tensor")
                    p.on("scalar", lambda e, t=t: e.activation(
                        rows[0:1, t * 512:(t + 1) * 512], ps_row[0:1, :], AF.Sqrt))
                    p.barrier("scalar")
                p.on("vector", lambda e, h=h: e.tensor_scalar(
                    rows[0:1, 0:TLOC], rows[0:1, 0:TLOC], cells[0:1, 8 + h:9 + h], -SC,
                    AL.mult, AL.mult))
                p.barrier("vector")
                def trb(e):
                    for g in range(NTILE):
                        e.transpose(ps_col[:, g:g + 1], rows[0:1, g * 128:(g + 1) * 128],
                                    ident_f[0:1, 0:1])
                p.on("tensor", trb)
                p.barrier("tensor")
                p.on("vector", lambda e, vh=vh: e.tensor_copy(
                    negB[:, vh * 8:(vh + 1) * 8], ps_col[:, 0:8]))
                p.barrier("vector")

        for oc in range(NCH):
            qg, hp = oc // 4, oc % 4
            kc = hp
            p.on("vector", lambda e: e.memset(ls[:, :], 0.0))
            p.barrier("vector")
            for g in range(NTILE):
                for blk in range(8):
                    def scm(e, oc=oc, kc=kc, g=g, blk=blk):
                        for half in range(2):
                            b0 = half * 64
                            e.matmul(ps_s[half][:, :],
                                     q_sb[b0:b0 + 64,
                                          oc * TLOC + g * 128: oc * TLOC + g * 128 + 128],
                                     k_sb[b0:b0 + 64, kc * S + blk * 512: kc * S + blk * 512 + 512],
                                     start=True, stop=True)
                    p.on("tensor", scm)
                    p.barrier("tensor")
                    def expp(e, qg=qg, hp=hp, g=g, blk=blk):
                        for half in range(2):
                            vh = qg * 8 + hp * 2 + half
                            e.activation(p_bf[half][:, :], ps_s[half][:, :], AF.Exp,
                                         scale=sc_col[:, 0:1],
                                         bias=negB[:, vh * 8 + g: vh * 8 + g + 1],
                                         accum_out=ls[:, half * 64 + g * 8 + blk:
                                                      half * 64 + g * 8 + blk + 1])
                    p.on("scalar", expp)
                    p.barrier("scalar")
                    def trp(e):
                        for half in range(2):
                            for j in range(4):
                                e.transpose(pt_ps[half][:, j * 128:(j + 1) * 128],
                                            p_bf[half][:, j * 128:(j + 1) * 128],
                                            consts["ident_bf"][:, :])
                    p.on("tensor", trp)
                    p.barrier("tensor")
                    def ptc(e):
                        for half in range(2):
                            e.tensor_copy(pt_bf[half][:, :], pt_ps[half][:, :])
                    p.on("vector", ptc)
                    p.barrier("vector")
                    def pv(e, kc=kc, g=g, blk=blk):
                        for half in range(2):
                            b0 = half * 64
                            for j in range(4):
                                kt = blk * 4 + j
                                e.matmul(out_ps[b0:b0 + 64, g * 128:(g + 1) * 128],
                                         vt_sb[:, (kc * 32 + kt) * 128 + b0:
                                               (kc * 32 + kt) * 128 + b0 + 64],
                                         pt_bf[half][:, j * 128:(j + 1) * 128],
                                         start=(blk == 0 and j == 0),
                                         stop=(blk == 7 and j == 3))
                    p.on("tensor", pv)
                    p.barrier("tensor")
            p.on("vector", lambda e: e.tensor_reduce(
                invl[:, 0:16], ls[:, :].rearrange("p (h g b) -> p h g b", h=2, g=8),
                mybir.AxisListType.X, AL.add))
            p.barrier("vector")
            p.on("vector", lambda e: e.reciprocal(invl[:, 0:16], invl[:, 0:16]))
            p.barrier("vector")
            for g in range(NTILE):
                for half in range(2):
                    def tr1(e, half=half, g=g):
                        e.transpose(ps_row[0:1, 0:128], invl[:, half * 8 + g: half * 8 + g + 1],
                                    ident_f[:, :])
                    p.on("tensor", tr1)
                    p.barrier("tensor")
                    p.on("vector", lambda e, half=half: e.tensor_copy(
                        invl_row[0:1, half * 128:(half + 1) * 128], ps_row[0:1, 0:128]))
                    p.barrier("vector")
                def bcm(e):
                    for half in range(2):
                        b0 = half * 64
                        e.matmul(ps_col[b0:b0 + 64, 0:128], consts["onesr_f"][0:1, 0:64],
                                 invl_row[0:1, half * 128:(half + 1) * 128],
                                 start=True, stop=True)
                p.on("tensor", bcm)
                p.barrier("tensor")
                p.on("vector", lambda e: e.tensor_copy(bc2_sb[:, :], ps_col[:, 0:128]))
                p.barrier("vector")
                p.on("vector", lambda e, oc=oc, g=g: e.tensor_tensor(
                    attn_bf[:, oc * TLOC + g * 128: oc * TLOC + (g + 1) * 128],
                    out_ps[:, g * 128:(g + 1) * 128], bc2_sb[:, :], AL.mult))
                p.barrier("vector")

    x_sb = p.sb("x_sb", [128, NCH * TLOC], F32)
    scrc = p.sb("scrc", [128, TLOC], BF16)
    p.load(x_sb[:, :], x_d[:, :, :])
    p.barrier("gpsimd")
    for oc in range(NCH):
        sl = slice(oc * TLOC, (oc + 1) * TLOC)
        def comb(e, sl=sl):
            e.tensor_scalar(scrc[:, :], attn_bf[:, sl], gcol[:, 1:2], None, AL.mult)
            e.drain()
            e.scalar_tensor_tensor(scrc[:, :], amem_bf[:, sl], gcol[:, 0:1], scrc[:, :],
                                   AL.mult, AL.add)
            e.drain()
            e.tensor_tensor(x_sb[:, sl], scrc[:, :], x_sb[:, sl], AL.add)
        p.on("vector", comb)
        p.barrier("vector")
        p.store(xo_d[:, oc, :], x_sb[:, sl])
    return p.finalize()


RG4 = [[0, 1, 2, 3], [4, 5, 6, 7]]
RG8 = [list(range(8))]

def causal_diag_maskT(core):
    cc = core % 4
    m = np.zeros((NTILE, 128, 512), dtype=np.float32)
    for g in range(NTILE):
        for r in range(4):
            kt = 128 * (4 * g + r) + np.arange(128)[:, None]
            q = 512 * g + 128 * cc + np.arange(128)[None, :]
            m[g, :, r * 128:(r + 1) * 128] = np.where(kt > q, NEG, 0.0)
    return m

def sec_ln_resident(p, env, xc_bf, r_b, invr):
    emit_ln(p, env["consts"], None, env["x_sb"], xc_bf, r_b, invr, load_x=False)

def sec_causal(p, env, H):
    consts = env["consts"]
    x_sb = env["x_sb"]
    with ExitStack() as sk:
        q_sb = p.sb("ca_q", [128, NCH * TLOC], BF16, sk)
        attn_bf = p.sb("ca_attn", [128, NCH * TLOC], BF16, sk)

        with ExitStack() as sk2:
            xc_bf = p.sb("ca_xc", [128, NCH * TLOC], BF16, sk2)
            r_b = p.sb("ca_rb", [128, TLOC], F32, sk2)
            invr = p.sb("ca_invr", [1, TLOC], F32, sk2)
            cos_bf = p.sb("ca_cos", [128, TLOC], BF16, sk2)
            sin_bf = p.sb("ca_sin", [128, TLOC], BF16, sk2)
            p0t_bf = p.sb("ca_p0t", [128, 128], BF16, sk2)
            b_sb = p.sb("ca_b", [1, 3 * D], BF16, sk2)
            invr_bf = p.sb("ca_invrbf", [1, TLOC], BF16, sk2)
            sec_ln_resident(p, env, xc_bf, r_b, invr)
            p.load(cos_bf[:, :], H["cosT"][:, :])
            p.load(sin_bf[:, :], H["sinT"][:, :])
            p.load(p0t_bf[:, :], H["p0t"][:, :])
            p.load(b_sb[0:1, :], H["b3"][0:1, :])
            p.barrier("gpsimd")
            p.on("vector", lambda e: e.tensor_copy(invr_bf[0:1, :], invr[0:1, :]))
            p.barrier("vector")
            wbuf = p.sb("ca_wbuf", [128, D], BF16, sk2)
            ps_p = [p.ps(f"ca_psp{i}", [128, 512], F32, sk2) for i in range(2)]
            ps_r = [p.ps(f"ca_psr{i}", [128, 512], F32, sk2) for i in range(2)]
            vt_ps = p.ps("ca_vtps", [128, 1024], BF16, sk2)
            qn_bf = p.sb("ca_qn", [128, TLOC], BF16, sk2)
            t1_bf = p.sb("ca_t1", [128, TLOC], BF16, sk2)
            t2_bf = p.sb("ca_t2", [128, TLOC], BF16, sk2)
            ro_bf = p.sb("ca_ro", [128, TLOC], BF16, sk2)
            def mk_rank1(boff):
                return (lambda oc, bo=boff: b_sb[0:1, bo + oc * 128: bo + (oc + 1) * 128],
                        lambda t: invr_bf[0:1, t * 512:(t + 1) * 512])
            def evac(oc):
                def f(e):
                    for t in range(2):
                        sl = slice(t * 512, (t + 1) * 512)
                        e.tensor_tensor(qn_bf[:, sl], ps_p[t][:, :], r_b[:, sl], AL.mult)
                p.on("vector", f)
                p.barrier("vector")
            def rope(oc, dest_sb, dest_dram):
                def shift(e):
                    for t in range(2):
                        e.matmul(ps_r[t][:, :], p0t_bf[:, :],
                                 qn_bf[:, t * 512:(t + 1) * 512], start=True, stop=True)
                p.on("tensor", shift)
                p.barrier("tensor")
                out = dest_sb if dest_sb is not None else ro_bf
                def rp(e, oc=oc, out=out):
                    for t in range(2):
                        sl = slice(t * 512, (t + 1) * 512)
                        osl = slice(oc * TLOC + t * 512, oc * TLOC + t * 512 + 512) \
                            if dest_sb is not None else sl
                        e.tensor_tensor(t1_bf[:, sl], qn_bf[:, sl], cos_bf[:, sl], AL.mult)
                        e.tensor_tensor(t2_bf[:, sl], ps_r[t][:, :], sin_bf[:, sl], AL.mult)
                        e.drain()
                        e.tensor_tensor(out[:, osl], t1_bf[:, sl], t2_bf[:, sl], AL.add)
                p.on("vector", rp)
                p.barrier("vector")
                if dest_dram is not None:
                    p.store(dest_dram[oc * 128:(oc + 1) * 128, :], ro_bf[:, :])
            def q_cb(oc):
                evac(oc)
                rope(oc, q_sb, None)
            def k_cb(oc):
                evac(oc)
                rope(oc, None, H["kc"])
            def v_cb(oc):
                evac(oc)
                def vtr(e):
                    for tc in range(8):
                        e.transpose(vt_ps[:, tc * 128:(tc + 1) * 128],
                                    qn_bf[:, tc * 128:(tc + 1) * 128],
                                    consts["ident_bf"][:, :])
                p.on("tensor", vtr)
                p.barrier("tensor")
                p.on("vector", lambda e: e.tensor_copy(ro_bf[:, :], vt_ps[:, :]))
                p.barrier("vector")
                p.store(H["vtc"][oc * TLOC:(oc + 1) * TLOC, :]
                        .rearrange("(tc pt) d -> pt tc d", pt=128), ro_bf[:, :])
            emit_proj_stream(p, H["wq"], NCH, xc_bf, wbuf, ps_p, q_cb, rank1=mk_rank1(0))
            emit_proj_stream(p, H["wk"], NCH, xc_bf, wbuf, ps_p, k_cb, rank1=mk_rank1(D))
            emit_proj_stream(p, H["wv"], NCH, xc_bf, wbuf, ps_p, v_cb, rank1=mk_rank1(2 * D))
        p.collective("AllGather", AL.bypass, RG4, H["kc"].ap().opt(), H["kag"].ap().opt())
        p.collective("AllGather", AL.bypass, RG4, H["vtc"].ap().opt(), H["vtag"].ap().opt())

        with ExitStack() as sk2:
            kh = p.sb("ca_kh", [128, S], BF16, sk2)
            vth = p.sb("ca_vth", [128, S], BF16, sk2)
            sq_bf = p.sb("ca_sq", [128, S], BF16, sk2)
            rows = p.sb("ca_rows", [1, S], F32, sk2)
            cells = p.sb("ca_cells", [1, 8], F32, sk2)
            negB = p.sb("ca_negB", [128, 1], F32, sk2)
            pT_bf = p.sb("ca_pT", [128, 512], BF16, sk2)
            invl_row = p.sb("ca_invl", [1, TLOC], F32, sk2)
            bc_sb = p.sb("ca_bc", [128, TLOC], F32, sk2)
            mask_sb = p.sb("ca_mask", [128, NTILE * 512], BF16, sk2)
            ps_sT = p.ps("ca_psT", [128, 512], F32, sk2)
            out_ps = p.ps("ca_out", [128, 1024], F32, sk2)
            ps_den = p.ps("ca_den", [1, 1024], F32, sk2)
            ps_bc = p.ps("ca_psbc", [128, 1024], F32, sk2)
            p.load(mask_sb[:, :], H["maskT"].rearrange("g p t -> p g t"))
            p.barrier("gpsimd")
            for h in range(NCH):
                def ldk(e, h=h):
                    for r in range(4):
                        e.dma_start(
                            out=kh.rearrange("p (g r j) -> p g r j", g=8, r=4, j=128)[:, :, r, :],
                            in_=H["kag"][r * D + h * 128: r * D + (h + 1) * 128, :],
                            max_dma_last_dim=8192).then_inc(p.ldm, 16)
                        e.dma_start(
                            out=vth.rearrange("p (g r d) -> p g r d", g=8, r=4, d=128)[:, :, r, :],
                            in_=H["vtag"][(r * NCH + h) * TLOC: (r * NCH + h + 1) * TLOC, :]
                            .rearrange("(g pt) d -> pt g d", pt=128),
                            max_dma_last_dim=8192).then_inc(p.ldm, 16)
                p.on("gpsimd", ldk)
                p.barrier("gpsimd")
                p.on("scalar", lambda e: e.activation(sq_bf[:, :], kh[:, :], AF.Square))
                p.barrier("scalar")
                for rnd in range(4):
                    def knm(e, rnd=rnd):
                        for i in range(2):
                            blk = rnd * 2 + i
                            e.matmul(ps_den[0:1, i * 512:(i + 1) * 512],
                                     consts["onesc_bf"][:, 0:1],
                                     sq_bf[:, blk * 512:(blk + 1) * 512],
                                     start=True, stop=True)
                    p.on("tensor", knm)
                    p.barrier("tensor")
                    p.on("vector", lambda e, rnd=rnd: e.tensor_copy(
                        rows[0:1, rnd * 1024:(rnd + 1) * 1024], ps_den[0:1, :]))
                    p.barrier("vector")
                p.on("vector", lambda e: e.tensor_reduce(cells[0:1, 0:1], rows[0:1, :],
                                                         mybir.AxisListType.X, AL.max))
                p.barrier("vector")
                p.on("scalar", lambda e, h=h: e.activation(
                    sq_bf[:, 0:TLOC], q_sb[:, h * TLOC:(h + 1) * TLOC], AF.Square))
                p.barrier("scalar")
                def qnm(e):
                    for t in range(2):
                        e.matmul(ps_den[0:1, t * 512:(t + 1) * 512], consts["onesc_bf"][:, 0:1],
                                 sq_bf[:, t * 512:(t + 1) * 512], start=True, stop=True)
                p.on("tensor", qnm)
                p.barrier("tensor")
                p.on("vector", lambda e: e.tensor_reduce(cells[0:1, 1:2], ps_den[0:1, :],
                                                         mybir.AxisListType.X, AL.max))
                p.barrier("vector")
                p.on("vector", lambda e: e.tensor_tensor(cells[0:1, 2:3], cells[0:1, 0:1],
                                                         cells[0:1, 1:2], AL.mult))
                p.barrier("vector")
                p.on("scalar", lambda e: e.activation(cells[0:1, 3:4], cells[0:1, 2:3], AF.Sqrt))
                p.barrier("scalar")
                p.on("vector", lambda e: e.tensor_scalar(cells[0:1, 4:5], cells[0:1, 3:4],
                                                         H["nsc"], None, AL.mult))
                p.barrier("vector")
                def nbc2(e):
                    e.matmul(ps_bc[:, 0:1], consts["onesr_f"][0:1, :], cells[0:1, 4:5],
                             start=True, stop=True)
                p.on("tensor", nbc2)
                p.barrier("tensor")
                p.on("vector", lambda e: e.tensor_copy(negB[:, 0:1], ps_bc[:, 0:1]))
                p.barrier("vector")
                for g in range(NTILE):
                    for kg in range(g + 1):
                        def scm(e, h=h, g=g, kg=kg):
                            for j in range(4):
                                ct = kg * 4 + j
                                e.matmul(ps_sT[:, j * 128:(j + 1) * 128],
                                         kh[:, ct * 128:(ct + 1) * 128],
                                         q_sb[:, h * TLOC + g * 128: h * TLOC + g * 128 + 128],
                                         start=True, stop=True)
                        p.on("tensor", scm)
                        p.barrier("tensor")
                        if kg == g:
                            p.on("vector", lambda e, g=g: e.tensor_tensor(
                                ps_sT[:, :], ps_sT[:, :], mask_sb[:, g * 512:(g + 1) * 512],
                                AL.add))
                            p.barrier("vector")
                        def expp(e):
                            e.activation(pT_bf[:, :], ps_sT[:, :], AF.Exp,
                                         scale=H["scale_col"], bias=negB[:, 0:1])
                        p.on("scalar", expp)
                        p.barrier("scalar")
                        def pv(e, g=g, kg=kg):
                            first = (kg == 0)
                            last = (kg == g)
                            for j in range(4):
                                ct = kg * 4 + j
                                e.matmul(out_ps[:, g * 128:(g + 1) * 128],
                                         vth[:, ct * 128:(ct + 1) * 128],
                                         pT_bf[:, j * 128:(j + 1) * 128],
                                         start=(first and j == 0), stop=(last and j == 3))
                            for j in range(4):
                                e.matmul(ps_den[0:1, g * 128:(g + 1) * 128],
                                         consts["onesc_bf"][:, 0:1],
                                         pT_bf[:, j * 128:(j + 1) * 128],
                                         start=(first and j == 0), stop=(last and j == 3))
                        p.on("tensor", pv)
                        p.barrier("tensor")
                p.on("vector", lambda e: e.reciprocal(invl_row[0:1, :], ps_den[0:1, :]))
                p.barrier("vector")
                def bcm(e):
                    for t in range(2):
                        e.matmul(ps_bc[:, t * 512:(t + 1) * 512], consts["onesr_f"][0:1, :],
                                 invl_row[0:1, t * 512:(t + 1) * 512], start=True, stop=True)
                p.on("tensor", bcm)
                p.barrier("tensor")
                p.on("vector", lambda e: e.tensor_copy(bc_sb[:, :], ps_bc[:, :]))
                p.barrier("vector")
                p.on("vector", lambda e, h=h: e.tensor_tensor(
                    attn_bf[:, h * TLOC:(h + 1) * TLOC], out_ps[:, :], bc_sb[:, :], AL.mult))
                p.barrier("vector")
        with ExitStack() as sk2:
            bo_sb = p.sb("ca_bo", [1, D], BF16, sk2)
            ones512 = p.sb("ca_ones512", [1, 512], BF16, sk2)
            wbuf = p.sb("ca_wbuf2", [128, D], BF16, sk2)
            ps_p = [p.ps(f"ca_pso{i}", [128, 512], F32, sk2) for i in range(2)]
            p.load(bo_sb[0:1, :], H["bo"][0:1, :])
            p.on("vector", lambda e: e.memset(ones512[0:1, :], 1.0))
            p.barrier("gpsimd", "vector")
            rank1 = (lambda oc: bo_sb[0:1, oc * 128:(oc + 1) * 128],
                     lambda t: ones512[0:1, :])
            def out_cb(oc):
                def f(e, oc=oc):
                    for t in range(2):
                        sl = slice(oc * TLOC + t * 512, oc * TLOC + t * 512 + 512)
                        e.tensor_tensor(x_sb[:, sl], ps_p[t][:, :], x_sb[:, sl], AL.add)
                p.on("vector", f)
                p.barrier("vector")
            emit_proj_stream(p, H["wo"], NCH, attn_bf, wbuf, ps_p, out_cb, rank1=rank1)

def sec_l0ffn(p, env, H):
    NI = 44
    x_sb = env["x_sb"]
    with ExitStack() as sk:
        r_b = p.sb("f0_rb", [128, TLOC], F32, sk)
        invr = p.sb("f0_invr", [1, TLOC], F32, sk)
        gs_sb = p.sb("f0_gs", [128, NI], F32, sk)
        us_sb = p.sb("f0_us", [128, NI], F32, sk)
        ds_sb = p.sb("f0_ds", [128, NCH], F32, sk)
        xc_bf = p.sb("f0_xc", [128, NCH * TLOC], BF16, sk)
        h_sb = p.sb("f0_h", [128, NI * TLOC], BF16, sk)
        p.load(gs_sb[:, :], H["gate_s"][:, :])
        p.load(us_sb[:, :], H["up_s"][:, :])
        p.load(ds_sb[:, :], H["down_s"][:, :])
        sec_ln_resident(p, env, xc_bf, r_b, invr)
        with ExitStack() as sk1:
            gbuf = p.sb("f0_gbuf", [128, D], BF16, sk1)
            ubuf = p.sb("f0_ubuf", [128, D], BF16, sk1)
            ps_g = [p.ps(f"f0_psg{i}", [128, 512], F32, sk1) for i in range(2)]
            ps_u = [p.ps(f"f0_psu{i}", [128, 512], F32, sk1) for i in range(2)]
            g_bf = p.sb("f0_gbf", [128, TLOC], BF16, sk1)
            s_bf = p.sb("f0_sbf", [128, TLOC], BF16, sk1)
            u_bf = p.sb("f0_ubf", [128, TLOC], BF16, sk1)
            for oc in range(NI):
                p.load(gbuf[:, :], H["gate_w"][oc, :, :])
                p.load(ubuf[:, :], H["up_w"][oc, :, :])
                p.barrier("gpsimd")
                def mms(e, oc=oc):
                    for t in range(2):
                        for cch in range(NCH):
                            rhs = xc_bf[:, cch * TLOC + t * 512: cch * TLOC + t * 512 + 512]
                            e.matmul(ps_g[t][:, :], gbuf[:, cch * 128:(cch + 1) * 128], rhs,
                                     start=(cch == 0), stop=(cch == NCH - 1))
                            e.matmul(ps_u[t][:, :], ubuf[:, cch * 128:(cch + 1) * 128], rhs,
                                     start=(cch == 0), stop=(cch == NCH - 1))
                p.on("tensor", mms)
                p.barrier("tensor")
                def scg(e, oc=oc):
                    for t in range(2):
                        sl = slice(t * 512, (t + 1) * 512)
                        e.scalar_tensor_tensor(g_bf[:, sl], ps_g[t][:, :], gs_sb[:, oc:oc + 1],
                                               r_b[:, sl], AL.mult, AL.mult)
                        e.scalar_tensor_tensor(u_bf[:, sl], ps_u[t][:, :], us_sb[:, oc:oc + 1],
                                               r_b[:, sl], AL.mult, AL.mult)
                p.on("vector", scg)
                p.barrier("vector")
                p.on("scalar", lambda e: e.activation(s_bf[:, :], g_bf[:, :], AF.Silu))
                p.barrier("scalar")
                p.on("vector", lambda e, oc=oc: e.tensor_tensor(
                    h_sb[:, oc * TLOC:(oc + 1) * TLOC], s_bf[:, :], u_bf[:, :], AL.mult))
                p.barrier("vector")
        with ExitStack() as sk1:
            dbuf = p.sb("f0_dbuf", [128, 5632], BF16, sk1)
            ps_p = [p.ps(f"f0_psd{i}", [128, 512], F32, sk1) for i in range(2)]
            def out_cb(oc):
                def f(e, oc=oc):
                    for t in range(2):
                        sl = slice(oc * TLOC + t * 512, oc * TLOC + t * 512 + 512)
                        e.scalar_tensor_tensor(x_sb[:, sl], ps_p[t][:, :], ds_sb[:, oc:oc + 1],
                                               x_sb[:, sl], AL.mult, AL.add)
                p.on("vector", f)
                p.barrier("vector")
            emit_proj_stream(p, H["down_w"], NCH, h_sb, dbuf, ps_p, out_cb, nch=44)

def sec_rwkv(p, env, H):
    consts = env["consts"]
    x_sb = env["x_sb"]
    with ExitStack() as sk:
        r_b = p.sb("rk_rb", [128, TLOC], F32, sk)
        invr = p.sb("rk_invr", [1, TLOC], F32, sk)
        xc_bf = p.sb("rk_xc", [128, NCH * TLOC], BF16, sk)
        sec_ln_resident(p, env, xc_bf, r_b, invr)
        def mkxln(e):
            for cch in range(NCH):
                sl = slice(cch * TLOC, (cch + 1) * TLOC)
                e.tensor_tensor(xc_bf[:, sl], xc_bf[:, sl], r_b[:, :], AL.mult)
        p.on("vector", mkxln)
        p.barrier("vector")
        p.store(H["xlnc"].rearrange("(c pp) t -> pp c t", pp=128), xc_bf[:, :])
    p.collective("AllGather", AL.bypass, RG8, H["xlnc"].ap().opt(), H["xag"].ap().opt())
    with ExitStack() as sk:
        w_sb = {}
        tm_sb = p.sb("rk_tm", [128, 3 * NCH], F32, sk)
        td_sb = p.sb("rk_td", [128, 2], F32, sk)
        tf_sb = p.sb("rk_tf", [128, 2], F32, sk)
        p.load(tm_sb[:, :], H["tmix"][:, :])
        p.load(td_sb[:, :], H["td"][:, :])
        p.load(tf_sb[:, :], H["tf"][:, :])
        for wi, nm in enumerate(("wk", "wv", "wr")):
            w1 = p.sb(f"rk_{nm}1", [128, 2 * D], BF16, sk)
            w2 = p.sb(f"rk_{nm}2", [128, 2 * D], BF16, sk)
            for cg in range(2):
                p.load(w2[:, cg * D:(cg + 1) * D], H[nm][cg, :, :])
            w_sb[nm] = (w1, w2, wi)
        p.barrier("gpsimd")
        def fold(e):
            for nm, (w1, w2, wi) in w_sb.items():
                for cg in range(2):
                    for cch in range(NCH):
                        sl = slice(cg * D + cch * 128, cg * D + (cch + 1) * 128)
                        tm = tm_sb[:, wi * NCH + cch: wi * NCH + cch + 1]
                        e.tensor_scalar(w1[:, sl], w2[:, sl], tm, None, AL.mult)
                e.drain()
                for cg in range(2):
                    sl = slice(cg * D, (cg + 1) * D)
                    e.tensor_tensor(w2[:, sl], w2[:, sl], w1[:, sl], AL.subtract)
                e.drain()
        p.on("vector", fold)
        p.barrier("vector")
        xslab = p.sb("rk_xslab", [128, 513], BF16, sk)
        halo_sb = p.sb("rk_halo", [128, NCH], BF16, sk)
        k_sb = p.sb("rk_k", [128, S], F32, sk)
        v_sb = p.sb("rk_v", [128, S], F32, sk)
        sr_bf = p.sb("rk_sr", [128, S], BF16, sk)
        dwt = p.sb("rk_dwt", [128, S], F32, sk)
        scan = p.sb("rk_scan", [128, S + 4], F32, sk)
        g_bf = p.sb("rk_g", [128, S], BF16, sk)
        cells = p.sb("rk_cells", [128, 8], F32, sk)
        ps = [p.ps(f"rk_ps{i}", [128, 512], F32, sk) for i in range(3)]
        for b in range(B):
            for cg in range(2):
                for tg in range(8):
                    for cch in range(NCH):
                        def lds2(e, b=b, tg=tg, cch=cch):
                            base_row = 4 * b * D + cch * 128
                            for r in range(4):
                                e.dma_start(
                                    out=xslab[:, 1:].rearrange("p (r j) -> p r j", r=4, j=128)[:, r, :],
                                    in_=H["xag"][base_row + r * D: base_row + r * D + 128,
                                                 tg * 128:(tg + 1) * 128],
                                    max_dma_last_dim=8192).then_inc(p.ldm, 16)
                        p.on("gpsimd", lds2)
                        p.barrier("gpsimd")
                        def halo(e, tg=tg, cch=cch):
                            if tg == 0:
                                e.memset(xslab[:, 0:1], 0.0)
                            else:
                                e.tensor_copy(xslab[:, 0:1], halo_sb[:, cch:cch + 1])
                            e.drain()
                            e.tensor_copy(halo_sb[:, cch:cch + 1], xslab[:, 512:513])
                        p.on("vector", halo)
                        p.barrier("vector")
                        def mms(e, cg=cg, cch=cch):
                            for nm, pi in (("wk", 0), ("wv", 1), ("wr", 2)):
                                w1, w2, _ = w_sb[nm]
                                sl = slice(cg * D + cch * 128, cg * D + (cch + 1) * 128)
                                e.matmul(ps[pi][:, :], w1[:, sl], xslab[:, 1:513],
                                         start=(cch == 0), stop=False)
                                e.matmul(ps[pi][:, :], w2[:, sl], xslab[:, 0:512],
                                         start=False, stop=(cch == NCH - 1))
                        p.on("tensor", mms)
                        p.barrier("tensor")
                    sl = slice(tg * 512, (tg + 1) * 512)
                    def evac(e, sl=sl):
                        e.tensor_copy(k_sb[:, sl], ps[0][:, :])
                        e.tensor_copy(v_sb[:, sl], ps[1][:, :])
                    p.on("vector", evac)
                    def sig(e, sl=sl):
                        e.activation(sr_bf[:, sl], ps[2][:, :], AF.Sigmoid)
                    p.on("scalar", sig)
                    p.barrier("vector", "scalar")
                p.on("vector", lambda e: e.tensor_reduce(
                    cells[:, 0:1], k_sb[:, :], mybir.AxisListType.X, AL.max, negate=True))
                p.barrier("vector")
                p.on("scalar", lambda e, cg=cg: e.activation(cells[:, 2:3],
                                                             td_sb[:, cg:cg + 1], AF.Exp))
                p.barrier("scalar")
                p.on("scalar", lambda e: e.activation(cells[:, 4:5], cells[:, 2:3],
                                                      AF.Exp, scale=-1.0))
                p.on("scalar", lambda e, cg=cg: e.activation(cells[:, 6:7],
                                                             tf_sb[:, cg:cg + 1], AF.Exp))
                p.barrier("scalar")
                p.on("scalar", lambda e: e.activation(k_sb[:, :], k_sb[:, :],
                                                      AF.Exp, bias=cells[:, 0:1]))
                p.barrier("scalar")
                p.on("vector", lambda e: e.tensor_tensor(v_sb[:, :], k_sb[:, :],
                                                         v_sb[:, :], AL.mult))
                p.on("vector", lambda e: e.tensor_scalar(dwt[:, :], k_sb[:, :], 0.0,
                                                         cells[:, 4:5], AL.mult, AL.add))
                p.on("vector", lambda e: e.memset(scan[:, 0:1], 0.0))
                p.barrier("vector")
                p.on("vector", lambda e: e.tensor_tensor_scan(
                    scan[:, 1:S + 1], dwt[:, :], v_sb[:, :], 0.0, AL.mult, AL.add))
                p.barrier("vector")
                p.on("vector", lambda e: e.scalar_tensor_tensor(
                    v_sb[:, :], v_sb[:, :], cells[:, 6:7], scan[:, 0:S], AL.mult, AL.add))
                p.barrier("vector")
                p.on("vector", lambda e: e.tensor_tensor_scan(
                    scan[:, 1:S + 1], dwt[:, :], k_sb[:, :], 0.0, AL.mult, AL.add))
                p.barrier("vector")
                p.on("vector", lambda e: e.scalar_tensor_tensor(
                    k_sb[:, :], k_sb[:, :], cells[:, 6:7], scan[:, 0:S], AL.mult, AL.add))
                p.barrier("vector")
                p.on("vector", lambda e: e.reciprocal(k_sb[:, :], k_sb[:, :]))
                p.barrier("vector")
                p.on("vector", lambda e: e.tensor_tensor(v_sb[:, :], v_sb[:, :],
                                                         k_sb[:, :], AL.mult))
                p.barrier("vector")
                p.on("vector", lambda e: e.tensor_tensor(g_bf[:, :], v_sb[:, :],
                                                         sr_bf[:, :], AL.mult))
                p.barrier("vector")
                for ccx in range(4):
                    row0 = (4 * b + ccx) * 256 + cg * 128
                    p.store(H["gsh"][row0:row0 + 128, :]
                            .rearrange("pp (g j) -> pp g j", g=8),
                            g_bf.rearrange("p (g cc j) -> p cc g j", g=8, cc=4)[:, ccx, :, :])
    p.collective("AllToAll", AL.bypass, RG8, H["gsh"].ap().opt(), H["ga2a"].ap().opt(),
                 cc_dim="Partition")
    with ExitStack() as sk:
        g_sb = p.sb("rk_gsb", [128, NCH * TLOC], BF16, sk)
        wbuf = p.sb("rk_wbuf", [128, D], BF16, sk)
        ps_p = [p.ps(f"rk_pso{i}", [128, 512], F32, sk) for i in range(2)]
        def ldg(e):
            e.dma_start(out=g_sb.rearrange("p (c t) -> p c t", c=NCH),
                        in_=H["ga2a"].rearrange("(c pp) t -> pp c t", pp=128),
                        max_dma_last_dim=8192).then_inc(p.ldm, 16)
        p.on("gpsimd", ldg)
        p.barrier("gpsimd")
        def out_cb(oc):
            def f(e, oc=oc):
                for t in range(2):
                    sl = slice(oc * TLOC + t * 512, oc * TLOC + t * 512 + 512)
                    e.tensor_tensor(x_sb[:, sl], ps_p[t][:, :], x_sb[:, sl], AL.add)
            p.on("vector", f)
            p.barrier("vector")
        emit_proj_stream(p, H["wo"], NCH, g_sb, wbuf, ps_p, out_cb)

def sec_tern(p, env, H):
    consts = env["consts"]
    SFC, SPJ = 5, 2
    with ExitStack() as sk:
        wbuf = p.sb("tn_wbuf", [128, 5120], F32, sk)
        scr = p.sb("tn_scr", [128, 5120], BF16, sk)
        t2 = p.sb("tn_t2", [128, 5120], BF16, sk)
        qb = p.sb("tn_qb", [128, 5120], BF16, sk)
        parts = p.sb("tn_parts", [128, 8], F32, sk)
        cells = p.sb("tn_cells", [1, 8], F32, sk)
        cols = p.sb("tn_cols", [128, 4], F32, sk)
        ps_row = p.ps("tn_psrow", [1, 512], F32, sk)
        ps_col = p.ps("tn_pscol", [128, 8], F32, sk)
        for wi, (wd, nb, wid) in enumerate(((H["wfc_sh"], SFC, D), (H["wpj_sh"], SPJ, 5120))):
            for bk in range(nb):
                p.load(wbuf[:, 0:wid], wd[bk, :, :])
                p.barrier("gpsimd")
                off = bk if wi == 0 else SFC + bk
                p.on("scalar", lambda e, off=off, wid=wid: e.activation(
                    scr[:, 0:wid], wbuf[:, 0:wid], AF.Abs, accum_out=parts[:, off:off + 1]))
                p.barrier("scalar")
        def tot(e):
            e.matmul(ps_row[0:1, 0:8], consts["onesc_f"][:, 0:1], parts[:, 0:8],
                     start=True, stop=True)
        p.on("tensor", tot)
        p.barrier("tensor")
        p.on("vector", lambda e: e.tensor_reduce(cells[0:1, 0:1], ps_row[0:1, 0:SFC],
                                                 mybir.AxisListType.X, AL.add))
        p.on("vector", lambda e: e.tensor_reduce(cells[0:1, 1:2], ps_row[0:1, SFC:SFC + SPJ],
                                                 mybir.AxisListType.X, AL.add))
        p.barrier("vector")
        p.store(H["arin"][0:1, :], cells[0:1, 0:2])
        p.collective("AllReduce", AL.add, RG8, H["arin"].ap().opt(), H["arout"].ap().opt())
        p.load(cells[0:1, 0:2], H["arout"][0:1, :])
        p.barrier("gpsimd")
        def amop(e):
            for j in range(2):
                e.tensor_scalar(cells[0:1, j:j + 1], cells[0:1, j:j + 1],
                                1.0 / (5120.0 * D), 1e-5, AL.mult, AL.max)
            e.drain()
            for j in range(2):
                e.tensor_scalar(cells[0:1, 2 + j:3 + j], cells[0:1, j:j + 1], 0.7, None, AL.mult)
            e.drain()
            for j in range(2):
                e.tensor_scalar(cells[0:1, 4 + j:5 + j], cells[0:1, 2 + j:3 + j], -1.0,
                                None, AL.mult)
        p.on("vector", amop)
        p.barrier("vector")
        p.store(H["amout"][0:1, :], cells[0:1, 0:2])
        def bc(e):
            for j in range(4):
                e.matmul(ps_col[:, j:j + 1], consts["onesr_f"][0:1, :],
                         cells[0:1, 2 + j:3 + j], start=True, stop=True)
        p.on("tensor", bc)
        p.barrier("tensor")
        p.on("vector", lambda e: e.tensor_copy(cols[:, :], ps_col[:, 0:4]))
        p.barrier("vector")
        for wi, (wd, qo, nb, wid) in enumerate(((H["wfc_sh"], H["qfcc"], SFC, D),
                                                (H["wpj_sh"], H["qpjc"], SPJ, 5120))):
            thr = cols[:, wi:wi + 1]
            nthr = cols[:, 2 + wi:3 + wi]
            for bk in range(nb):
                p.load(wbuf[:, 0:wid], wd[bk, :, :])
                p.barrier("gpsimd")
                def tern(e, wid=wid, thr=thr, nthr=nthr):
                    e.tensor_scalar(t2[:, 0:wid], wbuf[:, 0:wid], nthr, None, AL.is_lt)
                    e.drain()
                    e.scalar_tensor_tensor(qb[:, 0:wid], wbuf[:, 0:wid], thr, t2[:, 0:wid],
                                           AL.is_gt, AL.subtract)
                p.on("vector", tern)
                p.barrier("vector")
                p.store(qo[bk * 128:(bk + 1) * 128, :], qb[:, 0:wid])
    p.collective("AllGather", AL.bypass, RG8, H["qfcc"].ap().opt(), H["qfcag"].ap().opt())
    p.collective("AllGather", AL.bypass, RG8, H["qpjc"].ap().opt(), H["qpjag"].ap().opt())

def sec_l1ffn(p, env, H):
    NI = 40
    x_sb = env["x_sb"]
    consts = env["consts"]
    with ExitStack() as sk:
        r_b = p.sb("f1_rb", [128, TLOC], F32, sk)
        invr = p.sb("f1_invr", [1, TLOC], F32, sk)
        cells = p.sb("f1_cells", [1, 4], F32, sk)
        r2s = p.sb("f1_r2s", [128, TLOC], F32, sk)
        xc_bf = p.sb("f1_xc", [128, NCH * TLOC], BF16, sk)
        h_sb = p.sb("f1_h", [128, NI * TLOC], BF16, sk)
        sec_ln_resident(p, env, xc_bf, r_b, invr)
        p.load(cells[0:1, 0:2], H["amout"][0:1, :])
        p.barrier("gpsimd")
        ps_c = p.ps("f1_psc", [128, 8], F32, sk)
        def sstar(e):
            e.tensor_tensor(cells[0:1, 2:3], cells[0:1, 0:1], cells[0:1, 0:1], AL.mult)
            e.drain()
            e.tensor_tensor(cells[0:1, 2:3], cells[0:1, 2:3], cells[0:1, 1:2], AL.mult)
        p.on("vector", sstar)
        p.barrier("vector")
        p.on("tensor", lambda e: e.matmul(ps_c[:, 0:1], consts["onesr_f"][0:1, :],
                                          cells[0:1, 2:3], start=True, stop=True))
        p.barrier("tensor")
        def r2(e):
            e.tensor_tensor(r2s[:, :], r_b[:, :], r_b[:, :], AL.mult)
            e.drain()
            e.tensor_scalar(r2s[:, :], r2s[:, :], ps_c[:, 0:1], None, AL.mult)
        p.on("vector", r2)
        p.barrier("vector")
        with ExitStack() as sk1:
            fbuf = p.sb("f1_fbuf", [128, D], BF16, sk1)
            ps_z = [p.ps(f"f1_psz{i}", [128, 512], F32, sk1) for i in range(2)]
            rl_bf = p.sb("f1_rl", [128, TLOC], BF16, sk1)
            for oc in range(NI):
                def ldw(e, oc=oc):
                    e.dma_start(out=fbuf[:, :],
                                in_=H["qfcag"][oc * 128:(oc + 1) * 128, :],
                                max_dma_last_dim=8192).then_inc(p.ldm, 16)
                p.on("gpsimd", ldw)
                p.barrier("gpsimd")
                def mms(e, oc=oc):
                    for t in range(2):
                        for cch in range(NCH):
                            e.matmul(ps_z[t][:, :], fbuf[:, cch * 128:(cch + 1) * 128],
                                     xc_bf[:, cch * TLOC + t * 512: cch * TLOC + t * 512 + 512],
                                     start=(cch == 0), stop=(cch == NCH - 1))
                p.on("tensor", mms)
                p.barrier("tensor")
                def relu(e):
                    for t in range(2):
                        e.activation(rl_bf[:, t * 512:(t + 1) * 512], ps_z[t][:, :], AF.Relu)
                p.on("scalar", relu)
                p.barrier("scalar")
                p.on("vector", lambda e, oc=oc: e.tensor_tensor(
                    h_sb[:, oc * TLOC:(oc + 1) * TLOC], rl_bf[:, :], rl_bf[:, :], AL.mult))
                p.barrier("vector")
        with ExitStack() as sk1:
            pbuf = p.sb("f1_pbuf", [128, 5120], BF16, sk1)
            oscr = p.sb("f1_oscr", [128, TLOC], F32, sk1)
            ps_p = [p.ps(f"f1_pso{i}", [128, 512], F32, sk1) for i in range(2)]
            for oc in range(NCH):
                def ldw(e, oc=oc):
                    e.dma_start(out=pbuf[:, :],
                                in_=H["qpjag"][oc * 128:(oc + 1) * 128, :],
                                max_dma_last_dim=8192).then_inc(p.ldm, 16)
                p.on("gpsimd", ldw)
                p.barrier("gpsimd")
                def mms(e, oc=oc):
                    for t in range(2):
                        for cch in range(NI):
                            e.matmul(ps_p[t][:, :], pbuf[:, cch * 128:(cch + 1) * 128],
                                     h_sb[:, cch * TLOC + t * 512: cch * TLOC + t * 512 + 512],
                                     start=(cch == 0), stop=(cch == NI - 1))
                p.on("tensor", mms)
                p.barrier("tensor")
                def f(e, oc=oc):
                    for t in range(2):
                        sl = slice(oc * TLOC + t * 512, oc * TLOC + t * 512 + 512)
                        tsl = slice(t * 512, (t + 1) * 512)
                        e.tensor_tensor(oscr[:, tsl], ps_p[t][:, :], r2s[:, tsl], AL.mult)
                        e.drain()
                        e.tensor_tensor(x_sb[:, sl], oscr[:, tsl], x_sb[:, sl], AL.add)
                p.on("vector", f)
                p.barrier("vector")

def sec_l1b(p, env, H):
    SC = 0.125
    consts = env["consts"]
    x_sb = env["x_sb"]
    with ExitStack() as sk:
        q_sb = p.sb("lb_q", [128, NCH * TLOC], BF16, sk)
        gcol = p.sb("lb_gcol", [128, 2], F32, sk)
        cells = p.sb("lb_cells", [1, 16], F32, sk)
        negB = p.sb("lb_negB", [128, 32], F32, sk)
        with ExitStack() as sk2:
            xc_bf = p.sb("lb_xc", [128, NCH * TLOC], BF16, sk2)
            r_b = p.sb("lb_rb", [128, TLOC], F32, sk2)
            invr = p.sb("lb_invr", [1, TLOC], F32, sk2)
            sec_ln_resident(p, env, xc_bf, r_b, invr)
            wbuf = p.sb("lb_wbuf", [128, D], BF16, sk2)
            ps_p = [p.ps(f"lb_psp{i}", [128, 512], F32, sk2) for i in range(2)]
            vt_ps = p.ps("lb_vtps", [128, 1024], BF16, sk2)
            qn_bf = p.sb("lb_qn", [128, TLOC], BF16, sk2)
            ro_bf = p.sb("lb_ro", [128, TLOC], BF16, sk2)
            def evac(dest):
                def f(e, dest=dest):
                    for t in range(2):
                        sl = slice(t * 512, (t + 1) * 512)
                        e.tensor_tensor(dest[:, sl], ps_p[t][:, :], r_b[:, sl], AL.mult)
                p.on("vector", f)
                p.barrier("vector")
            def q_cb(oc):
                evac(q_sb[:, oc * TLOC:(oc + 1) * TLOC])
            def k_cb(oc):
                evac(qn_bf)
                p.store(H["k1c"][oc * 128:(oc + 1) * 128, :], qn_bf[:, :])
            def v_cb(oc):
                evac(qn_bf)
                def vtr(e):
                    for tc in range(8):
                        e.transpose(vt_ps[:, tc * 128:(tc + 1) * 128],
                                    qn_bf[:, tc * 128:(tc + 1) * 128],
                                    consts["ident_bf"][:, :])
                p.on("tensor", vtr)
                p.barrier("tensor")
                p.on("vector", lambda e: e.tensor_copy(ro_bf[:, :], vt_ps[:, :]))
                p.barrier("vector")
                p.store(H["vt1c"][oc * TLOC:(oc + 1) * TLOC, :]
                        .rearrange("(tc pt) d -> pt tc d", pt=128), ro_bf[:, :])
            emit_proj_stream(p, H["wq1b"], NCH, xc_bf, wbuf, ps_p, q_cb)
            emit_proj_stream(p, H["wk1b"], 4, xc_bf, wbuf, ps_p, k_cb)
            emit_proj_stream(p, H["wv1b"], 4, xc_bf, wbuf, ps_p, v_cb)
        p.collective("AllGather", AL.bypass, RG4, H["k1c"].ap().opt(), H["k1ag"].ap().opt())
        p.collective("AllGather", AL.bypass, RG4, H["vt1c"].ap().opt(), H["vt1ag"].ap().opt())
        with ExitStack() as skg:
            ps_c = p.ps("lb_psc", [128, 8], F32, skg)
            p.load(cells[0:1, 0:1], H["mw"][0:1, :])
            p.barrier("gpsimd")
            p.on("scalar", lambda e: e.activation(cells[0:1, 1:2], cells[0:1, 0:1], AF.Sigmoid))
            p.barrier("scalar")
            p.on("vector", lambda e: e.tensor_scalar(cells[0:1, 2:3], cells[0:1, 1:2],
                                                     -1.0, 1.0, AL.mult, AL.add))
            p.barrier("vector")
            def gbc(e):
                e.matmul(ps_c[:, 0:1], consts["onesr_f"][0:1, :], cells[0:1, 1:2],
                         start=True, stop=True)
                e.matmul(ps_c[:, 1:2], consts["onesr_f"][0:1, :], cells[0:1, 2:3],
                         start=True, stop=True)
            p.on("tensor", gbc)
            p.barrier("tensor")
            p.on("vector", lambda e: e.tensor_copy(gcol[:, :], ps_c[:, 0:2]))
            p.barrier("vector")
        with ExitStack() as sk2:
            sig_bf = p.sb("lb_sig", [128, NCH * TLOC], BF16, sk2)
            scr1 = p.sb("lb_scr1", [128, TLOC], BF16, sk2)
            scr2 = p.sb("lb_scr2", [128, TLOC], BF16, sk2)
            membuf = p.sb("lb_mem", [128, 4 * 512], BF16, sk2)
            mn_bf = p.sb("lb_mn", [128, 4], BF16, sk2)
            den_row = p.sb("lb_den", [1, TLOC], F32, sk2)
            bc_sb = p.sb("lb_bcm", [128, TLOC], F32, sk2)
            am_scr = p.sb("lb_amscr", [128, TLOC], BF16, sk2)
            ps_m = [p.ps(f"lb_psm{i}", [128, 512], F32, sk2) for i in range(2)]
            ps_d = p.ps("lb_psd", [1, TLOC], F32, sk2)
            ps_bcm = p.ps("lb_psbcm", [128, TLOC], F32, sk2)
            for ob in range(4):
                p.load(membuf[:, ob * 512:(ob + 1) * 512], H["mem"][ob, :, :])
            p.load(mn_bf[:, :], H["mnorm"][:, :])
            p.barrier("gpsimd")
            for oc in range(NCH):
                qsl = q_sb[:, oc * TLOC:(oc + 1) * TLOC]
                def elu(e, qsl=qsl):
                    e.activation(scr1[:, :], qsl, AF.Relu)
                    e.drain()
                    e.activation(scr2[:, :], qsl, AF.Relu, scale=-1.0)
                    e.drain()
                    e.activation(scr2[:, :], scr2[:, :], AF.Exp, scale=-1.0)
                p.on("scalar", elu)
                p.barrier("scalar")
                p.on("vector", lambda e, oc=oc: e.scalar_tensor_tensor(
                    sig_bf[:, oc * TLOC:(oc + 1) * TLOC], scr1[:, :], -1.0, scr2[:, :],
                    AL.add, AL.add))
                p.barrier("vector")
            for qg in range(4):
                def denm(e, qg=qg):
                    for t in range(2):
                        for ic in range(4):
                            e.matmul(ps_d[0:1, t * 512:(t + 1) * 512], mn_bf[:, ic:ic + 1],
                                     sig_bf[:, (qg * 4 + ic) * TLOC + t * 512:
                                            (qg * 4 + ic) * TLOC + t * 512 + 512],
                                     start=(ic == 0), stop=(ic == 3))
                p.on("tensor", denm)
                p.barrier("tensor")
                p.on("vector", lambda e: e.reciprocal(den_row[0:1, :], ps_d[0:1, :]))
                p.barrier("vector")
                def dbc(e):
                    for t in range(2):
                        e.matmul(ps_bcm[:, t * 512:(t + 1) * 512], consts["onesr_f"][0:1, :],
                                 den_row[0:1, t * 512:(t + 1) * 512], start=True, stop=True)
                p.on("tensor", dbc)
                p.barrier("tensor")
                p.on("vector", lambda e: e.tensor_copy(bc_sb[:, :], ps_bcm[:, :]))
                p.barrier("vector")
                for ob in range(4):
                    def amm(e, qg=qg, ob=ob):
                        for t in range(2):
                            for ic in range(4):
                                e.matmul(ps_m[t][:, :],
                                         membuf[:, ob * 512 + ic * 128:
                                                ob * 512 + (ic + 1) * 128],
                                         sig_bf[:, (qg * 4 + ic) * TLOC + t * 512:
                                                (qg * 4 + ic) * TLOC + t * 512 + 512],
                                         start=(ic == 0), stop=(ic == 3))
                    p.on("tensor", amm)
                    p.barrier("tensor")
                    def amn(e):
                        for t in range(2):
                            e.tensor_tensor(am_scr[:, t * 512:(t + 1) * 512], ps_m[t][:, :],
                                            bc_sb[:, t * 512:(t + 1) * 512], AL.mult)
                    p.on("vector", amn)
                    p.barrier("vector")
                    ch = qg * 4 + ob
                    p.store(H["amem"][ch * 128:(ch + 1) * 128, :], am_scr[:, :])
        attn_bf = p.sb("lb_attn", [128, NCH * TLOC], BF16, sk)
        with ExitStack() as sk2:
            kh = p.sb("lb_kh", [128, S], BF16, sk2)
            pT_bf = p.sb("lb_pT", [128, 2 * 1024], BF16, sk2)
            invl_row = p.sb("lb_invl", [1, 1024], F32, sk2)
            bc2_sb = p.sb("lb_bc2", [128, 512], F32, sk2)
            sc_col = p.sb("lb_sccol", [128, 1], F32, sk2)
            p.on("vector", lambda e: e.memset(sc_col[:, :], SC))
            p.barrier("vector")
            def ldk(kc):
                def f(e, kc=kc):
                    for r in range(4):
                        e.dma_start(
                            out=kh.rearrange("p (g r j) -> p g r j", g=8, r=4, j=128)[:, :, r, :],
                            in_=H["k1ag"][r * 512 + kc * 128: r * 512 + (kc + 1) * 128, :],
                            max_dma_last_dim=8192).then_inc(p.ldm, 16)
                p.on("gpsimd", f)
                p.barrier("gpsimd")
            with ExitStack() as sk3:
                sq_bf = p.sb("lb_sq", [128, S], BF16, sk3)
                rows = p.sb("lb_rows", [1, S], F32, sk3)
                ps_r2 = p.ps("lb_psr2", [1, 512], F32, sk3)
                ps_b1 = p.ps("lb_psb1", [128, 8], F32, sk3)
                for kc in range(4):
                    ldk(kc)
                    p.on("scalar", lambda e: e.activation(sq_bf[:, :], kh[:, :], AF.Square))
                    p.barrier("scalar")
                    for half in range(2):
                        base = half * 64
                        h = kc * 2 + half
                        for blk in range(8):
                            def knm(e, base=base, blk=blk):
                                e.matmul(ps_r2[0:1, 0:512],
                                         consts["onesc_bf"][base:base + 64, 0:1],
                                         sq_bf[base:base + 64, blk * 512:(blk + 1) * 512],
                                         start=True, stop=True)
                            p.on("tensor", knm)
                            p.barrier("tensor")
                            p.on("vector", lambda e, blk=blk: e.tensor_copy(
                                rows[0:1, blk * 512:(blk + 1) * 512], ps_r2[0:1, :]))
                            p.barrier("vector")
                        p.on("vector", lambda e, h=h: e.tensor_reduce(
                            cells[0:1, 8 + h:9 + h], rows[0:1, :],
                            mybir.AxisListType.X, AL.max))
                        p.barrier("vector")
                for oc in range(NCH):
                    qg, hp = oc // 4, oc % 4
                    p.on("scalar", lambda e, oc=oc: e.activation(
                        sq_bf[:, 0:TLOC], q_sb[:, oc * TLOC:(oc + 1) * TLOC], AF.Square))
                    p.barrier("scalar")
                    for half in range(2):
                        base = half * 64
                        h = hp * 2 + half
                        vh = qg * 8 + h
                        for t in range(2):
                            def qn1(e, base=base, t=t):
                                e.matmul(ps_r2[0:1, 0:512],
                                         consts["onesc_bf"][base:base + 64, 0:1],
                                         sq_bf[base:base + 64, t * 512:(t + 1) * 512],
                                         start=True, stop=True)
                            p.on("tensor", qn1)
                            p.barrier("tensor")
                            p.on("vector", lambda e, t=t: e.tensor_copy(
                                rows[0:1, t * 512:(t + 1) * 512], ps_r2[0:1, :]))
                            p.barrier("vector")
                        def nb(e, h=h):
                            e.tensor_reduce(cells[0:1, 0:1], rows[0:1, 0:TLOC],
                                            mybir.AxisListType.X, AL.max)
                            e.drain()
                            e.tensor_tensor(cells[0:1, 1:2], cells[0:1, 0:1],
                                            cells[0:1, 8 + h:9 + h], AL.mult)
                        p.on("vector", nb)
                        p.barrier("vector")
                        p.on("scalar", lambda e: e.activation(cells[0:1, 2:3],
                                                              cells[0:1, 1:2], AF.Sqrt))
                        p.barrier("scalar")
                        p.on("vector", lambda e: e.tensor_scalar(
                            cells[0:1, 3:4], cells[0:1, 2:3], -SC, None, AL.mult))
                        p.barrier("vector")
                        def nbc(e):
                            e.matmul(ps_b1[:, 0:1], consts["onesr_f"][0:1, :],
                                     cells[0:1, 3:4], start=True, stop=True)
                        p.on("tensor", nbc)
                        p.barrier("tensor")
                        p.on("vector", lambda e, vh=vh: e.tensor_copy(
                            negB[:, vh:vh + 1], ps_b1[:, 0:1]))
                        p.barrier("vector")
            vt65 = p.sb("lb_vt65", [128, 32 * 130], BF16, sk2)
            ps_sT = [p.ps(f"lb_psT{i}", [128, 1024], F32, sk2) for i in range(2)]
            ps_o = [p.ps(f"lb_pso{i}", [65, 512], F32, sk2) for i in range(2)]
            ps_bc2 = p.ps("lb_psbc2", [128, 512], F32, sk2)
            for oc in range(NCH):
                qg, hp = oc // 4, oc % 4
                kc = hp
                ldk(kc)
                def ldvt(e, kc=kc):
                    for r in range(4):
                        for half in range(2):
                            e.dma_start(
                                out=vt65.rearrange("p (g r w) -> p g r w", g=8, r=4, w=130)
                                [:, :, r, half * 65: half * 65 + 64],
                                in_=H["vt1ag"][(r * 4 + kc) * TLOC: (r * 4 + kc + 1) * TLOC, :]
                                .rearrange("(g pt) d -> pt g d", pt=128)
                                [:, :, half * 64: half * 64 + 64],
                                max_dma_last_dim=8192).then_inc(p.ldm, 16)
                p.on("gpsimd", ldvt)
                p.barrier("gpsimd")
                def setones(e):
                    e.memset(vt65.rearrange("p (ct w) -> p ct w", w=130)[:, :, 64:65], 1.0)
                    e.memset(vt65.rearrange("p (ct w) -> p ct w", w=130)[:, :, 129:130], 1.0)
                p.on("vector", setones)
                p.barrier("vector")
                for qb in range(2):
                    qsl = slice(oc * TLOC + qb * 512, oc * TLOC + qb * 512 + 512)
                    for ktg in range(16):
                        def scm(e, kc=kc, ktg=ktg, qsl=qsl):
                            for half in range(2):
                                b0 = half * 64
                                for j in range(2):
                                    ct = ktg * 2 + j
                                    e.matmul(ps_sT[half][:, j * 512:(j + 1) * 512],
                                             kh[b0:b0 + 64, ct * 128:(ct + 1) * 128],
                                             q_sb[b0:b0 + 64, qsl],
                                             start=True, stop=True)
                        p.on("tensor", scm)
                        p.barrier("tensor")
                        def expp(e, qg=qg, hp=hp):
                            for half in range(2):
                                vh = qg * 8 + hp * 2 + half
                                e.activation(pT_bf[:, half * 1024:(half + 1) * 1024],
                                             ps_sT[half][:, :], AF.Exp,
                                             scale=sc_col[:, 0:1], bias=negB[:, vh:vh + 1])
                        p.on("scalar", expp)
                        p.barrier("scalar")
                        def pv(e, ktg=ktg):
                            for half in range(2):
                                for j in range(2):
                                    ct = ktg * 2 + j
                                    e.matmul(ps_o[half][:, :],
                                             vt65[:, ct * 130 + half * 65:
                                                  ct * 130 + half * 65 + 65],
                                             pT_bf[:, half * 1024 + j * 512:
                                                   half * 1024 + (j + 1) * 512],
                                             start=(ktg == 0 and j == 0),
                                             stop=(ktg == 15 and j == 1))
                        p.on("tensor", pv)
                        p.barrier("tensor")
                    def rec(e):
                        e.reciprocal(invl_row[0:1, 0:512], ps_o[0][64:65, :])
                        e.reciprocal(invl_row[0:1, 512:1024], ps_o[1][64:65, :])
                    p.on("vector", rec)
                    p.barrier("vector")
                    def bcm2(e):
                        for half in range(2):
                            b0 = half * 64
                            e.matmul(ps_bc2[b0:b0 + 64, :], consts["onesr_f"][0:1, 0:64],
                                     invl_row[0:1, half * 512:(half + 1) * 512],
                                     start=True, stop=True)
                    p.on("tensor", bcm2)
                    p.barrier("tensor")
                    p.on("vector", lambda e: e.tensor_copy(bc2_sb[:, :], ps_bc2[:, :]))
                    p.barrier("vector")
                    def nrm(e, qsl=qsl):
                        for half in range(2):
                            b0 = half * 64
                            e.tensor_tensor(attn_bf[b0:b0 + 64, qsl],
                                            ps_o[half][0:64, :], bc2_sb[b0:b0 + 64, :],
                                            AL.mult)
                    p.on("vector", nrm)
                    p.barrier("vector")
        with ExitStack() as sk2:
            scrc = p.sb("lb_scrc", [128, TLOC], BF16, sk2)
            am_sb = p.sb("lb_amsb", [128, TLOC], BF16, sk2)
            for oc in range(NCH):
                sl = slice(oc * TLOC, (oc + 1) * TLOC)
                p.load(am_sb[:, :], H["amem"][oc * 128:(oc + 1) * 128, :])
                p.barrier("gpsimd")
                def comb(e, sl=sl):
                    e.tensor_scalar(scrc[:, :], attn_bf[:, sl], gcol[:, 1:2], None, AL.mult)
                    e.drain()
                    e.scalar_tensor_tensor(scrc[:, :], am_sb[:, :], gcol[:, 0:1],
                                           scrc[:, :], AL.mult, AL.add)
                    e.drain()
                    e.tensor_tensor(x_sb[:, sl], scrc[:, :], x_sb[:, sl], AL.add)
                p.on("vector", comb)
                p.barrier("vector")

def build_fused(upto):
    p = Prog("fused")
    x_d = p.dram("x", [128, NCH, TLOC], F32, "ExternalInput")
    xo_d = p.dram("xo", [128, NCH, TLOC], F32, "ExternalOutput")
    consts = alloc_consts(p)
    x_sb = p.sb("x_sb", [128, NCH * TLOC], F32)
    p.load(x_sb[:, :], x_d[:, :, :])
    p.barrier("gpsimd")
    env = dict(consts=consts, x_sb=x_sb)

    sp4_ag = maybe_share_collective_output_space("AllGather", RG4)
    sp8_ag = maybe_share_collective_output_space("AllGather", RG8)
    sp8_ar = maybe_share_collective_output_space("AllReduce", RG8)
    sp8_a2a = maybe_share_collective_output_space("AllToAll", RG8)

    if upto >= 1:
        kc_d = p.dram("i_kc", [D, TLOC], BF16, "Internal")
        vtc_d = p.dram("i_vtc", [NCH * TLOC, 128], BF16, "Internal")
        kag_d = nc_dram(p, "i_kag", [4 * D, TLOC], BF16, sp4_ag)
        vtag_d = nc_dram(p, "i_vtag", [4 * NCH * TLOC, 128], BF16, sp4_ag)
        maskT_d = p.dram("maskT", [NTILE, 128, 512], F32, "ExternalInput")
        scl_d = p.dram("scl", [1, 4], F32, "ExternalInput")
        sccol_d = p.dram("sccol", [128, 2], F32, "ExternalInput")
        scl_sb = p.sb("scl_sb", [1, 4], F32)
        sccol_sb = p.sb("sccol_sb", [128, 2], F32)
        p.load(scl_sb[0:1, :], scl_d[0:1, :])
        p.load(sccol_sb[:, :], sccol_d[:, :])
        p.barrier("gpsimd")

    def causal_handles(pref, boff):
        return dict(
            wq=p.dram(f"{pref}_wq", [NCH, 128, D], F32, "ExternalInput"),
            wk=p.dram(f"{pref}_wk", [NCH, 128, D], F32, "ExternalInput"),
            wv=p.dram(f"{pref}_wv", [NCH, 128, D], F32, "ExternalInput"),
            b3=p.dram(f"{pref}_b3", [1, 3 * D], F32, "ExternalInput"),
            wo=p.dram(f"{pref}_wo", [NCH, 128, D], F32, "ExternalInput"),
            bo=p.dram(f"{pref}_bo", [1, D], F32, "ExternalInput"),
            cosT=p.dram(f"{pref}_cos", [128, TLOC], F32, "ExternalInput"),
            sinT=p.dram(f"{pref}_sin", [128, TLOC], F32, "ExternalInput"),
            p0t=p.dram(f"{pref}_p0t", [128, 128], F32, "ExternalInput"),
            maskT=maskT_d, kc=kc_d, vtc=vtc_d, kag=kag_d, vtag=vtag_d,
            scale_col=sccol_sb[:, boff:boff + 1],
            nsc=scl_sb[0:1, 2 * boff + 1: 2 * boff + 2])

    def ffn0_handles():
        return dict(
            gate_w=p.dram("f0_gate_w", [44, 128, D], F32, "ExternalInput"),
            up_w=p.dram("f0_up_w", [44, 128, D], F32, "ExternalInput"),
            down_w=p.dram("f0_down_w", [NCH, 128, 5632], F32, "ExternalInput"),
            gate_s=p.dram("f0_gate_s", [128, 44], F32, "ExternalInput"),
            up_s=p.dram("f0_up_s", [128, 44], F32, "ExternalInput"),
            down_s=p.dram("f0_down_s", [128, NCH], F32, "ExternalInput"))

    H0 = causal_handles("l0a", 0) if upto >= 1 else None
    Hf0 = ffn0_handles() if upto >= 2 else None
    if upto >= 3:
        H3 = dict(
            xlnc=p.dram("i_xlnc", [D, TLOC], BF16, "Internal"),
            xag=nc_dram(p, "i_xag", [8 * D, TLOC], BF16, sp8_ag),
            gsh=p.dram("i_gsh", [D, TLOC], BF16, "Internal"),
            ga2a=nc_dram(p, "i_ga2a", [D, TLOC], BF16, sp8_a2a),
            tmix=p.dram("rk_tmix", [128, 3 * NCH], F32, "ExternalInput"),
            td=p.dram("rk_td_d", [128, 2], F32, "ExternalInput"),
            tf=p.dram("rk_tf_d", [128, 2], F32, "ExternalInput"),
            wk=p.dram("rk_wk_d", [2, 128, D], F32, "ExternalInput"),
            wv=p.dram("rk_wv_d", [2, 128, D], F32, "ExternalInput"),
            wr=p.dram("rk_wr_d", [2, 128, D], F32, "ExternalInput"),
            wo=p.dram("rk_wo_d", [NCH, 128, D], F32, "ExternalInput"))
    H5 = causal_handles("l1a", 1) if upto >= 5 else None
    if upto >= 6:
        H6 = dict(
            wfc_sh=p.dram("tn_wfc", [5, 128, D], F32, "ExternalInput"),
            wpj_sh=p.dram("tn_wpj", [2, 128, 5120], F32, "ExternalInput"),
            arin=p.dram("i_arin", [1, 2], F32, "Internal"),
            arout=nc_dram(p, "i_arout", [1, 2], F32, sp8_ar),
            amout=p.dram("i_amout", [1, 2], F32, "Internal"),
            qfcc=p.dram("i_qfcc", [640, D], BF16, "Internal"),
            qfcag=nc_dram(p, "i_qfcag", [5120, D], BF16, sp8_ag),
            qpjc=p.dram("i_qpjc", [256, 5120], BF16, "Internal"),
            qpjag=nc_dram(p, "i_qpjag", [D, 5120], BF16, sp8_ag))
    if upto >= 7:
        H7 = dict(
            wq1b=p.dram("lb_wq_d", [NCH, 128, D], F32, "ExternalInput"),
            wk1b=p.dram("lb_wk_d", [4, 128, D], F32, "ExternalInput"),
            wv1b=p.dram("lb_wv_d", [4, 128, D], F32, "ExternalInput"),
            mem=p.dram("lb_mem_d", [4, 128, 512], F32, "ExternalInput"),
            mnorm=p.dram("lb_mn_d", [128, 4], F32, "ExternalInput"),
            mw=p.dram("lb_mw_d", [1, 1], F32, "ExternalInput"),
            k1c=p.dram("i_k1c", [512, TLOC], BF16, "Internal"),
            k1ag=nc_dram(p, "i_k1ag", [2048, TLOC], BF16, sp4_ag),
            vt1c=p.dram("i_vt1c", [4 * TLOC, 128], BF16, "Internal"),
            vt1ag=nc_dram(p, "i_vt1ag", [16 * TLOC, 128], BF16, sp4_ag),
            amem=p.dram("i_amem", [D, TLOC], BF16, "Internal"))

    if upto >= 1:
        sec_causal(p, env, H0)
    if upto >= 2:
        sec_l0ffn(p, env, Hf0)
    if upto >= 3:
        sec_rwkv(p, env, H3)
    if upto >= 4:
        sec_l0ffn(p, env, Hf0)
    if upto >= 5:
        sec_causal(p, env, H5)
    if upto >= 6:
        sec_tern(p, env, H6)
        sec_l1ffn(p, env, H6)
    if upto >= 7:
        sec_l1b(p, env, H7)
    if upto >= 8:
        sec_l1ffn(p, env, H6)

    p.store(xo_d[:, :, :], x_sb[:, :])
    return p.finalize()

def nc_dram(p, name, shape, dt, addr_space):
    return p.nc.dram_tensor(name, shape, dt, kind="Internal", addr_space=addr_space)

FUSE_UPTO = 8

def fused_in_maps(inp, xs, upto):
    f32 = np.float32
    cos0, sin0, p0t0 = rope0_tables()
    cos1, sin1, p0t1 = rope1_tables()
    sc0, sc1 = 1.0 / math.sqrt(128.0), 1.0 / 128.0
    maps = []
    packs = {}
    def P(key, fn):
        if key not in packs:
            packs[key] = fn()
        return packs[key]
    for c in range(NCORES):
        cols = core_cols(c)
        m = dict(x=xs[c], **const_inputs())
        if upto >= 1:
            m.update(
                l0a_wq=P("awq", lambda: pack_lhsT(inp["L0_attn_a_Wq"])),
                l0a_wk=P("awk", lambda: pack_lhsT(inp["L0_attn_a_Wk"])),
                l0a_wv=P("awv", lambda: pack_lhsT(inp["L0_attn_a_Wv"])),
                l0a_wo=P("awo", lambda: pack_lhsT(inp["L0_attn_a_Wo"])),
                l0a_b3=P("ab3", lambda: np.concatenate(
                    [inp["L0_attn_a_bq"], inp["L0_attn_a_bk"],
                     inp["L0_attn_a_bv"]]).astype(f32)[None, :]),
                l0a_bo=inp["L0_attn_a_bo"].astype(f32)[None, :],
                l0a_cos=np.ascontiguousarray(cos0[:, cols]),
                l0a_sin=np.ascontiguousarray(sin0[:, cols]),
                l0a_p0t=p0t0,
                maskT=causal_diag_maskT(c),
                scl=np.array([[sc0, -sc0, sc1, -sc1]], dtype=f32),
                sccol=np.concatenate([np.full((128, 1), sc0, f32),
                                      np.full((128, 1), sc1, f32)], axis=1))
        if upto >= 2:
            m.update(
                f0_gate_w=P("gw", lambda: pack_lhsT(inp["L0_ffn_gate_w"])),
                f0_up_w=P("uw", lambda: pack_lhsT(inp["L0_ffn_up_w"])),
                f0_down_w=P("dw", lambda: pack_lhsT(inp["L0_ffn_down_w"])),
                f0_gate_s=P("gs", lambda: np.ascontiguousarray(
                    inp["L0_ffn_gate_s"].reshape(44, 128).T)),
                f0_up_s=P("us", lambda: np.ascontiguousarray(
                    inp["L0_ffn_up_s"].reshape(44, 128).T)),
                f0_down_s=P("ds", lambda: np.ascontiguousarray(
                    inp["L0_ffn_down_s"].reshape(16, 128).T)))
        if upto >= 3:
            m.update(
                rk_tmix=P("tm", lambda: np.concatenate(
                    [inp[f"L0_attn_b_time_mix_{n}"].reshape(16, 128).T
                     for n in ("k", "v", "r")], axis=1).astype(f32)),
                rk_td_d=np.ascontiguousarray(
                    inp["L0_attn_b_time_decay"].reshape(16, 128).T[:, 2 * c:2 * c + 2]),
                rk_tf_d=np.ascontiguousarray(
                    inp["L0_attn_b_time_first"].reshape(16, 128).T[:, 2 * c:2 * c + 2]),
                rk_wk_d=P("bwk", lambda: pack_lhsT(inp["L0_attn_b_Wk"]))[2 * c:2 * c + 2],
                rk_wv_d=P("bwv", lambda: pack_lhsT(inp["L0_attn_b_Wv"]))[2 * c:2 * c + 2],
                rk_wr_d=P("bwr", lambda: pack_lhsT(inp["L0_attn_b_Wr"]))[2 * c:2 * c + 2],
                rk_wo_d=P("bwo", lambda: pack_lhsT(inp["L0_attn_b_Wo"])))
        if upto >= 5:
            m.update(
                l1a_wq=P("1wq", lambda: pack_lhsT(np.ascontiguousarray(
                    inp["L1_attn_a_wq"].transpose(1, 2, 0).reshape(D, D)))),
                l1a_wk=P("1wk", lambda: pack_lhsT(np.ascontiguousarray(
                    inp["L1_attn_a_wk"].transpose(1, 2, 0).reshape(D, D)))),
                l1a_wv=P("1wv", lambda: pack_lhsT(np.ascontiguousarray(
                    inp["L1_attn_a_wv"].transpose(1, 2, 0).reshape(D, D)))),
                l1a_wo=P("1wo", lambda: pack_lhsT(np.ascontiguousarray(
                    inp["L1_attn_a_wo"].reshape(D, D).T))),
                l1a_b3=P("zb3", lambda: np.zeros((1, 3 * D), f32)),
                l1a_bo=P("zbo", lambda: np.zeros((1, D), f32)),
                l1a_cos=np.ascontiguousarray(cos1[:, cols]),
                l1a_sin=np.ascontiguousarray(sin1[:, cols]),
                l1a_p0t=p0t1)
        if upto >= 6:
            m.update(
                tn_wfc=P("wfc", lambda: pack_lhsT(inp["L1_ffn_w_fc"]))[5 * c:5 * c + 5],
                tn_wpj=P("wpj", lambda: pack_lhsT(inp["L1_ffn_w_proj"]))[2 * c:2 * c + 2])
        if upto >= 7:
            m.update(
                lb_wq_d=P("bq1", lambda: pack_lhsT(inp["L1_attn_b_Wq"])),
                lb_wk_d=P("bk1", lambda: pack_lhsT(inp["L1_attn_b_Wk"])),
                lb_wv_d=P("bv1", lambda: pack_lhsT(inp["L1_attn_b_Wv"])),
                lb_mem_d=P("bm1", lambda: pack_lhsT(
                    np.ascontiguousarray(inp["L1_attn_b_memory"].T))),
                lb_mn_d=P("bn1", lambda: np.ascontiguousarray(
                    inp["L1_attn_b_memory_norm"].reshape(4, 128).T)),
                lb_mw_d=inp["L1_attn_b_memory_weight"].reshape(1, 1).astype(f32))
        maps.append(m)
    return maps


_PROG_CACHE = {}

def _prog(key, fn):
    if key not in _PROG_CACHE:
        _PROG_CACHE[key] = fn()
    return _PROG_CACHE[key]

def run(nc, maps):
    return run_bass_kernel_spmd(nc, maps, core_ids=list(range(NCORES)))

def _percore(fn):
    return [fn(c) for c in range(NCORES)]

def kernel(**inp):
    f32 = np.float32
    consts = const_inputs()
    cos0, sin0, p0t0 = rope0_tables()
    cos1, sin1, p0t1 = rope1_tables()
    cc = [core_cols(c) for c in range(NCORES)]
    cb = [core_batch(c) for c in range(NCORES)]
    masks = [causal_diag_mask(c) for c in range(NCORES)]

    xfm = fm(inp["L0_attn_a_x"])
    xs = [pmaj(xfm[cb[c]][:, cc[c]]) for c in range(NCORES)]

    upto = FUSE_UPTO
    if upto >= 1:
        ncf = _prog(f"fused{upto}", lambda: build_fused(upto))
        rf = run(ncf, fused_in_maps(inp, xs, upto))
        xs = [rf.results[c]["xo"] for c in range(NCORES)]

    def x_full():
        out = [np.zeros((D, S), dtype=f32) for _ in range(B)]
        for c in range(NCORES):
            out[cb[c]][:, cc[c]] = un_pmaj(xs[c])
        return out

    def mha(wq, wk, wv, bq, bk, bv, wo, bo, cos, sin, p0t, scale):
        nonlocal xs
        nc1 = _prog("qkv", build_l0a_qkv)
        packs = [pack_lhsT(w) for w in (wq, wk, wv)]
        biases = np.concatenate([bq, bk, bv]).astype(f32)[None, :]
        r1 = run(nc1, _percore(lambda c: dict(
            x=xs[c], wq=packs[0], wk=packs[1], wv=packs[2], biases=biases,
            cosT=np.ascontiguousarray(cos[:, cc[c]]),
            sinT=np.ascontiguousarray(sin[:, cc[c]]), p0t=p0t, **consts)))
        kfull = [np.zeros((D, S), dtype=BF) for _ in range(B)]
        vtfull = [np.zeros((NCH, S, 128), dtype=BF) for _ in range(B)]
        for c in range(NCORES):
            kfull[cb[c]][:, cc[c]] = un_pmaj(r1.results[c]["k_out"])
            vtfull[cb[c]][:, cc[c], :] = r1.results[c]["vt_out"]
        kfull = [pmaj(k) for k in kfull]
        nc2 = _prog("attn", build_causal_attn)
        wo_p = pack_lhsT(wo)
        r2 = run(nc2, _percore(lambda c: dict(
            q=r1.results[c]["q_out"], kfull=kfull[cb[c]], vtfull=vtfull[cb[c]],
            x=xs[c], wo=wo_p, bo=bo.astype(f32)[None, :], masks=masks[c],
            scales=np.array([[scale, -scale]], dtype=f32),
            scale_col=np.full((128, 1), scale, dtype=f32),
            ident_f32=IDENT, **consts)))
        xs = [r2.results[c]["xo"] for c in range(NCORES)]

    def l0ffn():
        nonlocal xs
        nc3 = _prog("l0ffn", build_l0_ffn)
        gw = pack_lhsT(inp["L0_ffn_gate_w"])
        uw = pack_lhsT(inp["L0_ffn_up_w"])
        dw = pack_lhsT(inp["L0_ffn_down_w"])
        gs = np.ascontiguousarray(inp["L0_ffn_gate_s"].reshape(44, 128).T)
        us = np.ascontiguousarray(inp["L0_ffn_up_s"].reshape(44, 128).T)
        ds = np.ascontiguousarray(inp["L0_ffn_down_s"].reshape(16, 128).T)
        r = run(nc3, _percore(lambda c: dict(
            x=xs[c], gate_w=gw, up_w=uw, down_w=dw, gate_s=gs, up_s=us, down_s=ds,
            **consts)))
        xs = [r.results[c]["xo"] for c in range(NCORES)]

    def l1ffn(qfc, qpj, am):
        nonlocal xs
        nc11 = _prog("l1ffn", build_l1_ffn)
        r = run(nc11, _percore(lambda c: dict(
            x=xs[c], qfc=qfc, qpj=qpj, am=am, **consts)))
        xs = [r.results[c]["xo"] for c in range(NCORES)]

    if upto < 1:
        mha(inp["L0_attn_a_Wq"], inp["L0_attn_a_Wk"], inp["L0_attn_a_Wv"],
            inp["L0_attn_a_bq"], inp["L0_attn_a_bk"], inp["L0_attn_a_bv"],
            inp["L0_attn_a_Wo"], inp["L0_attn_a_bo"], cos0, sin0, p0t0,
            1.0 / math.sqrt(128.0))
    if upto < 2:
        l0ffn()
    if upto < 3:
      nc4 = _prog("mix", build_rwkv_mix)
      xf = x_full()
      tm_all = np.concatenate([
          inp[f"L0_attn_b_time_mix_{n}"].reshape(16, 128).T for n in ("k", "v", "r")],
          axis=1).astype(f32)
      def halo_of(c):
          h = np.zeros((D, NTILE), dtype=f32)
          for g in range(NTILE):
              gc = 128 * (4 * g + c % 4)
              if gc > 0:
                  h[:, g] = xf[cb[c]][:, gc - 1]
          return pmaj(h)
      r4 = run(nc4, _percore(lambda c: dict(
          x=xs[c], xhalo=halo_of(c), tmix=tm_all, **consts)))
      full = {}
      for nm in ("xk", "xv", "xr"):
          fa = np.zeros((D, B * S), dtype=BF)
          for c in range(NCORES):
              fa[:, cb[c] * S + cc[c]] = un_pmaj(r4.results[c][nm])
          full[nm] = pmaj(fa)
      nc5 = _prog("rwkvcore", build_rwkv_core)
      packs_b = {n: pack_lhsT(inp[f"L0_attn_b_{n}"]) for n in ("Wk", "Wv", "Wr")}
      td_pm = inp["L0_attn_b_time_decay"].reshape(16, 128).T
      tf_pm = inp["L0_attn_b_time_first"].reshape(16, 128).T
      r5 = run(nc5, _percore(lambda c: dict(
          xk=full["xk"], xv=full["xv"], xr=full["xr"],
          wk=packs_b["Wk"][2 * c:2 * c + 2], wv=packs_b["Wv"][2 * c:2 * c + 2],
          wr=packs_b["Wr"][2 * c:2 * c + 2],
          td=np.ascontiguousarray(td_pm[:, 2 * c:2 * c + 2]),
          tf=np.ascontiguousarray(tf_pm[:, 2 * c:2 * c + 2]))))
      g_full = np.zeros((D, B * S), dtype=BF)
      for c in range(NCORES):
          g = r5.results[c]["g_out"]
          for cg in range(2):
              g_full[(2 * c + cg) * 128:(2 * c + cg + 1) * 128, :] = g[:, cg, :]
      nc6 = _prog("rwkvout", build_rwkv_out)
      wo_b = pack_lhsT(inp["L0_attn_b_Wo"])
      r6 = run(nc6, _percore(lambda c: dict(
          g=pmaj(np.ascontiguousarray(g_full[:, cb[c] * S + cc[c]])),
          x=xs[c], wo=wo_b, **consts)))
      xs = [r6.results[c]["xo"] for c in range(NCORES)]
    if upto < 4:
        l0ffn()
    if upto < 5:
      zb = np.zeros(D, dtype=f32)
      wq1 = np.ascontiguousarray(
          inp["L1_attn_a_wq"].transpose(1, 2, 0).reshape(D, D))
      wk1 = np.ascontiguousarray(
          inp["L1_attn_a_wk"].transpose(1, 2, 0).reshape(D, D))
      wv1 = np.ascontiguousarray(
          inp["L1_attn_a_wv"].transpose(1, 2, 0).reshape(D, D))
      wo1 = np.ascontiguousarray(inp["L1_attn_a_wo"].reshape(D, D).T)
      mha(wq1, wk1, wv1, zb, zb, zb, wo1, zb, cos1, sin1, p0t1, 1.0 / 128.0)
    if upto < 6:
      nc10 = _prog("tern", build_ternarize)
      wfc_p = pack_lhsT(inp["L1_ffn_w_fc"])
      wpj_p = pack_lhsT(inp["L1_ffn_w_proj"])
      r10 = run(nc10, _percore(lambda c: dict(
          wfc=wfc_p, wpj=wpj_p, wfc_sh=wfc_p[5 * c:5 * c + 5],
          wpj_sh=wpj_p[2 * c:2 * c + 2], **consts)))
      qfc = np.concatenate([r10.results[c]["qfc"] for c in range(NCORES)], axis=0)
      qpj = np.concatenate([r10.results[c]["qpj"] for c in range(NCORES)], axis=0)
      am = r10.results[0]["am"]
    if upto < 6:
        l1ffn(qfc, qpj, am)
    if upto < 7:
      nc12 = _prog("l1bqkv", build_l1b_qkv)
      r12 = run(nc12, _percore(lambda c: dict(
          x=xs[c], wq=pack_lhsT(inp["L1_attn_b_Wq"]),
          wk=pack_lhsT(inp["L1_attn_b_Wk"]), wv=pack_lhsT(inp["L1_attn_b_Wv"]),
          **consts)))
      kfull1 = [np.zeros((512, S), dtype=BF) for _ in range(B)]
      vtfull1 = [np.zeros((4, S, 128), dtype=BF) for _ in range(B)]
      for c in range(NCORES):
          kfull1[cb[c]][:, cc[c]] = un_pmaj(r12.results[c]["k_out"])
          vtfull1[cb[c]][:, cc[c], :] = r12.results[c]["vt_out"]
      kfull1 = [pmaj(k) for k in kfull1]
      nc13 = _prog("l1battn", build_l1b_attn)
      r13 = run(nc13, _percore(lambda c: dict(
          q=r12.results[c]["q_out"], kfull=kfull1[cb[c]], vtfull=vtfull1[cb[c]],
          x=xs[c], mem=pack_lhsT(np.ascontiguousarray(inp["L1_attn_b_memory"].T)),
          mnorm=np.ascontiguousarray(inp["L1_attn_b_memory_norm"].reshape(4, 128).T),
          mweight=inp["L1_attn_b_memory_weight"].reshape(1, 1).astype(f32),
          ident_f32=IDENT, **consts)))
      xs = [r13.results[c]["xo"] for c in range(NCORES)]
    if upto < 8:
        l1ffn(qfc, qpj, am)

    out = np.zeros((B, S, D), dtype=f32)
    for c in range(NCORES):
        out[cb[c]][cc[c], :] = un_pmaj(xs[c]).T
    return out
